# version 04
# baseline (speedup 1.0000x reference)
import numpy as np
from contextlib import ExitStack

import concourse.bass as bass
import concourse.mybir as mybir
from concourse.bass_utils import run_bass_kernel_spmd

F32 = mybir.dt.float32
BF16 = mybir.dt.bfloat16
I32 = mybir.dt.int32
AF = mybir.ActivationFunctionType
ALU = mybir.AluOpType
AX = mybir.AxisListType

D = 1024
T_FULL = 4096
BATCH = 4
NST = 16
NH = 16
HD = 64
ED = 4096
NE = 8



class Ctx:
    ENGINES = ("sync", "tensor", "vector", "scalar", "gpsimd")

    def __init__(self, nc, st):
        self.nc = nc
        self.st = st
        self.sem_st = st
        self.ops = {k: [] for k in self.ENGINES}
        self.sems = {}

    def sem(self, name):
        if name not in self.sems:
            h = self.sem_st.enter_context(self.nc.semaphore(name))
            self.sems[name] = [h, 0]
        return self.sems[name][0]

    def cnt(self, name):
        self.sem(name)
        return self.sems[name][1]

    def bump(self, name, inc):
        self.sem(name)
        self.sems[name][1] += inc
        return self.sems[name][1]

    def on(self, eng, fn):
        self.ops[eng].append(fn)

    def wait(self, eng, sem_name, value=None):
        h = self.sem(sem_name)
        v = self.cnt(sem_name) if value is None else value
        if v > 0:
            self.ops[eng].append(lambda E, h=h, v=v: E.wait_ge(h, v))

    def step(self, eng, fn, inc_sem=None, inc=1):
        if inc_sem is None:
            self.on(eng, fn)
            return None
        h = self.sem(inc_sem)
        def wrapped(E, fn=fn, h=h, inc=inc):
            r = fn(E)
            r.then_inc(h, inc)
            return r
        self.ops[eng].append(wrapped)
        return self.bump(inc_sem, inc)

    def dma(self, eng, out, in_, inc_sem=None, **kw):
        return self.step(eng, lambda E, out=out, in_=in_: E.dma_start(out=out, in_=in_, **kw),
                         inc_sem=inc_sem, inc=16)

    def finalize(self, block):
        for name in self.ENGINES:
            fns = self.ops[name]
            if not fns:
                continue
            def mk(fl):
                def run(E):
                    for f in fl:
                        f(E)
                return run
            getattr(block, name)(mk(fns))

    def sb(self, name, shape, dt):
        return self.st.enter_context(self.nc.sbuf_tensor(name, shape, dt))

    def ps(self, name, shape, dt=F32):
        return self.st.enter_context(self.nc.psum_tensor(name, shape, dt))


class MM:
    def __init__(self, ctx, tag, n_banks=2, bank_shape=(128, 512)):
        self.ctx = ctx
        self.tag = tag
        self.banks = [ctx.ps(f"{tag}_ps{i}", list(bank_shape)) for i in range(n_banks)]
        self.pe_sem = f"{tag}_pe"
        self.ev_sem = f"{tag}_ev"
        self.uses = [0] * n_banks
        self.rr = 0

    def run(self, kchunks, mchunks, ntiles, nw, lhsT_fn, rhs_fn, out_fn,
            func=AF.Copy, scale=1.0, bias=0.0, evac_eng="scalar",
            bias_row=None, ones_row=None):
        ctx = self.ctx
        nb = len(self.banks)
        for mc in range(mchunks):
            for nt in range(ntiles):
                b = self.rr % nb
                self.rr += 1
                bank = self.banks[b]
                if self.uses[b] > 0:
                    ctx.wait("tensor", self.ev_sem, self.uses[b])
                last_kc = kchunks - 1
                for kc in range(kchunks):
                    lt = lhsT_fn(kc, mc)
                    rt = rhs_fn(kc, nt)
                    mrows = lt.shape[-1]
                    nwid = rt.shape[-1]
                    is_last = (kc == last_kc and bias_row is None)
                    ctx.step("tensor",
                             lambda E, bank=bank, lt=lt, rt=rt, mrows=mrows, nwid=nwid,
                                    s=(kc == 0), p=(kc == last_kc):
                             E.matmul(bank[0:mrows, 0:nwid], lt, rt,
                                      start=s, stop=(p and bias_row is None)),
                             inc_sem=self.pe_sem if is_last else None)
                if bias_row is not None:
                    br = bias_row(mc)
                    nwid = rhs_fn(0, nt).shape[-1]
                    ctx.step("tensor",
                             lambda E, bank=bank, br=br, nwid=nwid:
                             E.matmul(bank[0:br.shape[-1], 0:nwid], br,
                                      ones_row[:, 0:nwid], start=False, stop=True),
                             inc_sem=self.pe_sem)
                pe_target = ctx.cnt(self.pe_sem)
                ctx.wait(evac_eng, self.pe_sem, pe_target)
                dst = out_fn(mc, nt)
                sc = scale(mc) if callable(scale) else scale
                bi = bias(mc) if callable(bias) else bias
                mrows = dst.shape[0]
                nwid = dst.shape[-1]
                if hasattr(bi, "shape"):
                    bi = bi[0:mrows, :]
                if hasattr(sc, "shape"):
                    sc = sc[0:mrows, :]
                ctx.step(evac_eng,
                         lambda E, bank=bank, dst=dst, sc=sc, bi=bi, mrows=mrows, nwid=nwid:
                         E.activation(out=dst, in_=bank[0:mrows, 0:nwid], func=func,
                                      scale=sc, bias=bi),
                         inc_sem=self.ev_sem)
                self.uses[b] = ctx.cnt(self.ev_sem)



def emit_ones(ctx, tag, shape, dt=F32, value=1.0):
    t = ctx.sb(tag, shape, dt)
    ctx.step("gpsimd", lambda E, t=t, v=value: E.memset(t[:], v), inc_sem="ones_ms")
    return t


def emit_ln_stats(ctx, tag, x_sb, nchunk, width, mu_row, rstd_row, ones_col,
                  eps=1e-5, rms_only=False, var_row=None, ps_pair=None,
                  scratch=None, ones_sq=None):
    if ps_pair is None:
        ps_s = ctx.ps(f"{tag}_pss", [1, 512])
        ps_q = ctx.ps(f"{tag}_psq", [1, 512])
    else:
        ps_s, ps_q = ps_pair
    if scratch is None:
        sq2 = ctx.sb(f"{tag}_sq2", [128, 2 * 512], F32)
        srow = ctx.sb(f"{tag}_srow", [1, width], F32)
        qrow = ctx.sb(f"{tag}_qrow", [1, width], F32)
    else:
        sq2, srow, qrow = scratch
    sqd = f"{tag}_sqd"
    qrd = f"{tag}_qrd"
    evs = f"{tag}_ev"
    nt = width // 512
    idx = 0
    for t in range(nt):
        sl = slice(t * 512, (t + 1) * 512)
        if t > 0:
            ctx.wait("tensor", evs, 2 * t if not rms_only else t)
        for c in range(nchunk):
            slot = idx % 2
            ssl = slice(slot * 512, slot * 512 + 512)
            if idx >= 2:
                ctx.wait("scalar", qrd, idx - 1)
            ctx.step("scalar",
                     lambda E, c=c, sl=sl, ssl=ssl:
                     E.activation(out=sq2[:, ssl], in_=x_sb[:, c * width:(c + 1) * width][:, sl],
                                  func=AF.Square),
                     inc_sem=sqd)
            if not rms_only:
                ctx.step("tensor",
                         lambda E, c=c, sl=sl, s=(c == 0), p=(c == nchunk - 1):
                         E.matmul(ps_s[:, 0:512], ones_col[:], x_sb[:, c * width:(c + 1) * width][:, sl],
                                  start=s, stop=p),
                         inc_sem=None)
            ctx.wait("tensor", sqd, idx + 1)
            osq = ones_sq if ones_sq is not None else ones_col
            ctx.step("tensor",
                     lambda E, ssl=ssl, osq=osq, s=(c == 0), p=(c == nchunk - 1):
                     E.matmul(ps_q[:, 0:512], osq[:], sq2[:, ssl], start=s, stop=p),
                     inc_sem=qrd)
            idx += 1
        ctx.wait("vector", qrd, idx)
        ctx.step("vector", lambda E, sl=sl: E.tensor_copy(out=qrow[0:1, sl], in_=ps_q[:, 0:512]),
                 inc_sem=evs)
        if not rms_only:
            ctx.step("vector", lambda E, sl=sl: E.tensor_copy(out=srow[0:1, sl], in_=ps_s[:, 0:512]),
                     inc_sem=evs)
    dinv = 1.0 / (nchunk * 128)
    def fin(E):
        if rms_only:
            E.tensor_scalar(out=qrow[:], in0=qrow[:], scalar1=dinv, scalar2=eps,
                            op0=ALU.mult, op1=ALU.add)
            E.memset(mu_row[:], 0.0)
        else:
            E.tensor_scalar(out=mu_row[:], in0=srow[:], scalar1=dinv, scalar2=None,
                            op0=ALU.mult)
            E.tensor_scalar(out=qrow[:], in0=qrow[:], scalar1=dinv, scalar2=None,
                            op0=ALU.mult)
            E.tensor_tensor(out=srow[:], in0=mu_row[:], in1=mu_row[:], op=ALU.mult)
            E.tensor_sub(out=qrow[:], in0=qrow[:], in1=srow[:])
            if var_row is not None:
                E.tensor_copy(out=var_row[:], in_=qrow[:])
            E.tensor_scalar(out=qrow[:], in0=qrow[:], scalar1=1.0, scalar2=eps,
                            op0=ALU.mult, op1=ALU.add)
        return E.reciprocal(out=srow[:], in_=qrow[:])
    ctx.step("vector", fin, inc_sem=evs)
    ctx.wait("scalar", evs)
    ctx.step("scalar", lambda E: E.activation(out=rstd_row[:], in_=srow[:], func=AF.Sqrt),
             inc_sem=f"{tag}_fin")


def emit_normalize(ctx, tag, x_sb, out_sb, nchunk, width, mu_row, rstd_row,
                   ones_col1, gain_col=None, out2_sb=None, rms_only=False,
                   ps_pair=None):
    if ps_pair is None:
        p1 = ctx.ps(f"{tag}_p1", [128, 512])
        p2 = ctx.ps(f"{tag}_p2", [128, 512])
    else:
        p1, p2 = ps_pair
    mrow = ctx.sb(f"{tag}_mrow", [1, width], F32)
    tmp = ctx.sb(f"{tag}_tmp", [128, 512], F32)
    if not rms_only:
        def prep(E):
            E.tensor_tensor(out=mrow[:], in0=mu_row[:], in1=rstd_row[:], op=ALU.mult)
            return E.tensor_scalar(out=mrow[:], in0=mrow[:], scalar1=-1.0, scalar2=None,
                                   op0=ALU.mult)
        ctx.step("vector", prep, inc_sem=f"{tag}_prep")
        ctx.wait("tensor", f"{tag}_prep")
    nt = width // 512
    bc = f"{tag}_bc"
    dn = f"{tag}_done"
    for t in range(nt):
        sl = slice(t * 512, (t + 1) * 512)
        if t > 0:
            ctx.wait("tensor", dn, t * nchunk)
        ctx.step("tensor", lambda E, sl=sl:
                 E.matmul(p1[:, 0:512], ones_col1[:], rstd_row[0:1, sl], start=True, stop=True),
                 inc_sem=bc)
        if not rms_only:
            ctx.step("tensor", lambda E, sl=sl:
                     E.matmul(p2[:, 0:512], ones_col1[:], mrow[0:1, sl], start=True, stop=True),
                     inc_sem=bc)
        ctx.wait("vector", bc)
        for c in range(nchunk):
            csl = slice(c * width + t * 512, c * width + t * 512 + 512)
            def f(E, csl=csl, c=c, rms_only=rms_only):
                E.tensor_tensor(out=tmp[:], in0=x_sb[:, csl], in1=p1[:, 0:512], op=ALU.mult)
                if not rms_only:
                    E.tensor_tensor(out=tmp[:], in0=tmp[:], in1=p2[:, 0:512], op=ALU.add)
                if gain_col is not None:
                    E.tensor_scalar(out=tmp[:], in0=tmp[:], scalar1=gain_col(c),
                                    scalar2=None, op0=ALU.mult)
                last = E.tensor_copy(out=out_sb[:, csl], in_=tmp[:])
                if out2_sb is not None:
                    last = E.tensor_copy(out=out2_sb[:, csl], in_=tmp[:])
                return last
            ctx.step("vector", f, inc_sem=dn)



def build_launch_A(W=2048):
    nc = bass.Bass("TRN2", target_bir_lowering=False)
    x_d = nc.dram_tensor("x", [D, W], F32, kind="ExternalInput").ap()
    wx_d = nc.dram_tensor("w_xproj", [D, 1056], F32, kind="ExternalInput").ap()
    bx_d = nc.dram_tensor("b_xproj", [1, 1152], F32, kind="ExternalInput").ap()
    delta_d = nc.dram_tensor("delta", [D, W], F32, kind="ExternalOutput").ap()
    u_d = nc.dram_tensor("u", [D, W], F32, kind="ExternalOutput").ap()
    x1_d = nc.dram_tensor("x1", [D, W], BF16, kind="ExternalOutput").ap()
    bc_d = nc.dram_tensor("bc", [32, W], F32, kind="ExternalOutput").ap()

    NCH = D // 128
    with ExitStack() as st:
        ctx = Ctx(nc, st)
        x_sb = ctx.sb("x_sb", [128, NCH * W], F32)
        x1bf = ctx.sb("x1bf", [128, NCH * W], BF16)
        wx = ctx.sb("wx", [128, NCH * 1056], BF16)
        bxc = ctx.sb("bxc", [1, 1152], BF16)
        ones_rb = ctx.sb("ones_rb", [1, 512], BF16)
        mu = ctx.sb("mu", [1, W], F32)
        rstd = ctx.sb("rstd", [1, W], F32)
        dl2 = [ctx.sb("dl0", [128, W], F32)]
        u2 = [ctx.sb("u0", [128, W], F32)]
        bc_sb = ctx.sb("bc_sb", [32, W], F32)
        sp_abs = ctx.sb("sp_abs", [128, W], F32)
        blk = st.enter_context(nc.Block())

        ones_c = emit_ones(ctx, "onesc", [128, 1])
        ones_r = emit_ones(ctx, "onesr", [1, 128])
        ctx.dma("sync", x_sb[:].rearrange("p (c t) -> p c t", c=NCH),
                x_d.rearrange("(c p) t -> p c t", p=128), inc_sem="ld")
        ctx.dma("gpsimd", wx[:].rearrange("p (c m) -> p c m", c=NCH),
                wx_d.rearrange("(c p) m -> p c m", p=128), inc_sem="ldw")
        ctx.dma("gpsimd", bxc[:], bx_d, inc_sem="ldw")
        ctx.step("gpsimd", lambda E: E.memset(ones_rb[:], 1.0), inc_sem="ones_ms")

        for e in ("tensor", "scalar", "vector"):
            ctx.wait(e, "ld", 16)
            ctx.wait(e, "ones_ms")
        emit_ln_stats(ctx, "lnA", x_sb, NCH, W, mu, rstd, ones_c)
        for e in ("tensor", "vector"):
            ctx.wait(e, "lnA_fin")
        emit_normalize(ctx, "nrmA", x_sb, x1bf, NCH, W, mu, rstd, ones_r,
                       out2_sb=x_sb)

        ctx.wait("tensor", "nrmA_done")
        ctx.wait("tensor", "ldw", 16)
        ctx.wait("tensor", "ldw", 32)
        ctx.wait("sync", "nrmA_done")
        ctx.dma("sync", x1_d.rearrange("(c p) t -> p c t", p=128),
                x1bf[:].rearrange("p (c t) -> p c t", c=NCH), inc_sem="stv")
        mm = MM(ctx, "mmA", n_banks=2)
        NT = W // 512

        def lhsT_mk(mc):
            def lhsT(kc, mc2):
                m0 = mc * 128
                rows = min(128, 1056 - m0)
                return wx[:, kc * 1056 + m0: kc * 1056 + m0 + rows]
            return lhsT

        def rhs(kc, nt):
            return x1bf[:, kc * W + nt * 512: kc * W + nt * 512 + 512]

        dv = x_d
        for mc in range(8):
            dbuf, ubuf = dl2[0], u2[0]
            if mc >= 1:
                ctx.wait("scalar", "stA", mc * 32)
            mm.run(NCH, 1, NT, 512, lhsT_mk(mc), rhs,
                   lambda m2, nt, dbuf=dbuf: dbuf[:, nt * 512:(nt + 1) * 512],
                   func=AF.Copy,
                   bias_row=lambda m2, mc=mc: bxc[:, mc * 128:(mc + 1) * 128],
                   ones_row=ones_rb)
            ctx.wait("scalar", "mmA_ev")
            ctx.wait("vector", "mmA_ev")
            if mc >= 1:
                ctx.wait("vector", "stA", mc * 32)
            if mc > 0:
                ctx.wait("scalar", "spb", mc)
            def sp1(E, dbuf=dbuf):
                E.activation(out=sp_abs[:], in_=dbuf[:], func=AF.Abs)
                E.activation(out=sp_abs[:], in_=sp_abs[:], func=AF.Exp, scale=-1.0)
                return E.activation(out=sp_abs[:], in_=sp_abs[:], func=AF.Ln, bias=1.0)
            ctx.step("scalar", sp1, inc_sem="spa")
            ctx.wait("vector", "spa")
            def sp2(E, dbuf=dbuf, ubuf=ubuf, csl=slice(mc * W, (mc + 1) * W)):
                E.tensor_scalar_max(ubuf[:], dbuf[:], 0.0)
                E.tensor_add(out=dbuf[:], in0=ubuf[:], in1=sp_abs[:])
                return E.tensor_tensor(out=ubuf[:], in0=dbuf[:],
                                       in1=x_sb[:, csl], op=ALU.mult)
            ctx.step("vector", sp2, inc_sem="spb")
            ctx.wait("sync", "spb")
            ctx.dma("sync", delta_d.rearrange("(c p) t -> p c t", p=128)[:, mc, :],
                    dbuf[:], inc_sem="stA")
            ctx.dma("sync", u_d.rearrange("(c p) t -> p c t", p=128)[:, mc, :],
                    ubuf[:], inc_sem="stA")
        mm.run(NCH, 1, NT, 512, lhsT_mk(8), rhs,
               lambda m2, nt: bc_sb[0:32, nt * 512:(nt + 1) * 512],
               func=AF.Copy,
               bias_row=lambda m2: bxc[:, 8 * 128: 8 * 128 + 32],
               ones_row=ones_rb)
        ctx.wait("sync", "mmA_ev")
        ctx.dma("sync", bc_d, bc_sb[:], inc_sem="stv")
        ctx.wait("sync", "stv")
        ctx.wait("sync", "stA")
        ctx.finalize(blk)
    return nc



def build_launch_B(T=T_FULL, DH=512):
    nc = bass.Bass("TRN2", target_bir_lowering=False)
    dl_d = nc.dram_tensor("delta", [DH, T], F32, kind="ExternalInput").ap()
    u_d = nc.dram_tensor("u", [DH, T], F32, kind="ExternalInput").ap()
    bc_d = nc.dram_tensor("bc", [64, T], F32, kind="ExternalInput").ap()
    al_d = nc.dram_tensor("a_log", [128, (DH // 128) * NST], F32, kind="ExternalInput").ap()
    y_d = nc.dram_tensor("y", [DH, T], BF16, kind="ExternalOutput").ap()

    NCH = DH // 128
    with ExitStack() as st:
        ctx = Ctx(nc, st)
        dbf = ctx.sb("dbf", [128, NCH * T], BF16)
        ubf = ctx.sb("ubf", [128, NCH * T], BF16)
        y_sb = ctx.sb("y_sb", [128, NCH * T], F32)
        a2 = [ctx.sb(f"a{i}", [128, T], BF16) for i in range(2)]
        b2 = [ctx.sb(f"b{i}", [128, T], BF16) for i in range(2)]
        h_sb = ctx.sb("h_sb", [128, T], BF16)
        hc = ctx.sb("hc", [128, T // 2], F32)
        bmbc = [ctx.sb("bmbc0", [128, T], BF16)]
        cmbc = [ctx.sb("cmbc0", [128, T], BF16)]
        bc_sb = ctx.sb("bc_sb", [64, T], BF16)
        asb = ctx.sb("asb", [128, NCH * NST], F32)
        sel = ctx.sb("sel", [48, NST * 128], BF16)
        blk = st.enter_context(nc.Block())

        ctx.dma("gpsimd", dbf[:].rearrange("p (c t) -> p c t", c=NCH),
                dl_d.rearrange("(c p) t -> p c t", p=128), inc_sem="ldg")
        ctx.dma("gpsimd", ubf[:].rearrange("p (c t) -> p c t", c=NCH),
                u_d.rearrange("(c p) t -> p c t", p=128), inc_sem="ldg")
        ctx.dma("gpsimd", bc_sb[:], bc_d, inc_sem="ldg")
        ctx.dma("sync", asb[:], al_d, inc_sem="ld")
        ctx.step("gpsimd", lambda E: E.memset(sel[:], 1.0), inc_sem="selms")
        selv = sel[:].rearrange("p (n j) -> p n j", n=NST)
        ctx.step("gpsimd", lambda E: E.affine_select(
            out=selv[0:16], in_=selv[0:16], pattern=[[1, NST], [0, 128]],
            base=0, channel_multiplier=-1, compare_op=ALU.is_equal, fill=0.0),
            inc_sem="selms")
        ctx.step("gpsimd", lambda E: E.affine_select(
            out=selv[32:48], in_=selv[32:48], pattern=[[1, NST], [0, 128]],
            base=0, channel_multiplier=-1, compare_op=ALU.is_equal, fill=0.0),
            inc_sem="selms")
        ctx.wait("scalar", "ld", 16)
        ctx.step("scalar", lambda E: E.activation(out=asb[:], in_=asb[:], func=AF.Exp),
                 inc_sem="aexp")
        ctx.wait("vector", "aexp")
        ctx.step("vector", lambda E: E.tensor_scalar(out=asb[:], in0=asb[:], scalar1=-1.0,
                                                     scalar2=None, op0=ALU.mult),
                 inc_sem="aneg")
        ctx.wait("scalar", "aneg")
        ctx.wait("tensor", "selms")
        ctx.wait("tensor", "ld", 16)
        ctx.wait("tensor", "ldg", 48)
        ctx.wait("scalar", "ldg", 48)
        ctx.wait("vector", "ldg", 48)
        for c in range(NCH):
            ctx.step("vector", lambda E, c=c: E.memset(y_sb[:, c * T:(c + 1) * T], 0.0),
                     inc_sem="yz")

        mm = MM(ctx, "mmB", n_banks=2)
        NTT = T // 512
        bm_ready = [0, 0]
        for n in range(NST):
            par_n = 0
            if n >= 1:
                ctx.wait("scalar", "dvit", n * NCH)
            mm.run(1, 1, NTT, 512,
                   lambda kc, mc, n=n: sel[0:16, n * 128:(n + 1) * 128],
                   lambda kc, nt: bc_sb[0:16, nt * 512:(nt + 1) * 512],
                   lambda mc, nt, par_n=par_n: bmbc[par_n][:, nt * 512:(nt + 1) * 512],
                   func=AF.Copy)
            mm.run(1, 1, NTT, 512,
                   lambda kc, mc, n=n: sel[32:48, n * 128:(n + 1) * 128],
                   lambda kc, nt: bc_sb[32:48, nt * 512:(nt + 1) * 512],
                   lambda mc, nt, par_n=par_n: cmbc[par_n][:, nt * 512:(nt + 1) * 512],
                   func=AF.Copy)
            bm_ready[par_n] = ctx.cnt("mmB_ev")
            for c in range(NCH):
                idx = n * NCH + c
                par = idx % 2
                csl = slice(c * T, (c + 1) * T)
                if idx >= 2:
                    ctx.wait("scalar", "scend", idx - 1)
                ctx.step("scalar",
                         lambda E, par=par, csl=csl, c=c, n=n:
                         E.activation(out=a2[par][:], in_=dbf[:, csl], func=AF.Exp,
                                      scale=asb[:, c * NST + n:c * NST + n + 1]),
                         inc_sem="aex2")
                ctx.wait("vector", "mmB_ev", bm_ready[par_n])
                if idx >= 2:
                    ctx.wait("vector", "scend", idx - 1)
                ctx.step("vector",
                         lambda E, par=par, csl=csl, par_n=par_n:
                         E.tensor_tensor(out=b2[par][:], in0=ubf[:, csl],
                                         in1=bmbc[par_n][:], op=ALU.mult),
                         inc_sem=None)
                ctx.wait("vector", "aex2", idx + 1)
                ctx.step("vector",
                         lambda E, par=par:
                         E.tensor_tensor_scan(out=h_sb[:], data0=a2[par][:], data1=b2[par][:],
                                              initial=0.0, op0=ALU.mult, op1=ALU.add),
                         inc_sem="scend")
                for hh in range(2):
                    hsl = slice(hh * (T // 2), (hh + 1) * (T // 2))
                    ysl = slice(c * T + hh * (T // 2), c * T + (hh + 1) * (T // 2))
                    ctx.step("vector",
                             lambda E, par_n=par_n, hsl=hsl:
                             E.tensor_tensor(out=hc[:], in0=h_sb[:, hsl],
                                             in1=cmbc[par_n][:, hsl], op=ALU.mult),
                             inc_sem=None)
                    ctx.step("vector",
                             lambda E, ysl=ysl:
                             E.tensor_tensor(out=y_sb[:, ysl], in0=y_sb[:, ysl], in1=hc[:],
                                             op=ALU.add),
                             inc_sem="dvit" if hh == 1 else None)

        ctx.wait("gpsimd", "dvit", NST * NCH)
        ctx.dma("gpsimd", y_d.rearrange("(c p) t -> p c t", p=128),
                y_sb[:].rearrange("p (c t) -> p c t", c=NCH), inc_sem="stv")
        ctx.wait("gpsimd", "stv", 16)
        ctx.finalize(blk)
    return nc



def build_launch_C(W=2048):
    nc = bass.Bass("TRN2", target_bir_lowering=False)
    y_di = nc.dram_tensor("y", [D, W], BF16, kind="ExternalInput").ap()
    x1_di = nc.dram_tensor("x1", [D, W], BF16, kind="ExternalInput").ap()
    x_di = nc.dram_tensor("x", [D, W], F32, kind="ExternalInput").ap()
    dp_di = nc.dram_tensor("dp", [128, 8], F32, kind="ExternalInput").ap()
    n1_di = nc.dram_tensor("n1w", [128, 8], F32, kind="ExternalInput").ap()
    wo_di = nc.dram_tensor("w_out", [D, D], F32, kind="ExternalInput").ap()
    bo_di = nc.dram_tensor("b_out", [1, D], F32, kind="ExternalInput").ap()
    r1_do = nc.dram_tensor("r1", [D, W], F32, kind="ExternalOutput").ap()
    x2_do = nc.dram_tensor("x2", [D, W], F32, kind="ExternalOutput").ap()
    x2n_do = nc.dram_tensor("x2n", [D, W], BF16, kind="ExternalOutput").ap()

    NCH = D // 128
    NT = W // 512
    with ExitStack() as st:
        ctx = Ctx(nc, st)
        ybar = ctx.sb("ybar", [128, NCH * W], BF16)
        x2n = ybar
        m0 = ctx.sb("m0", [128, NCH * W], F32)
        wout = ctx.sb("wout", [128, NCH * D], BF16)
        bo = ctx.sb("bo", [1, 1024], BF16)
        dp = ctx.sb("dp_sb", [128, 8], F32)
        n1w = ctx.sb("n1w_sb", [128, 8], F32)
        x1c = [ctx.sb(f"x1c{i}", [128, W], BF16) for i in range(2)]
        xc = [ctx.sb(f"xc{i}", [128, W], F32) for i in range(2)]
        mu = ctx.sb("mu", [1, W], F32)
        rstd = ctx.sb("rstd", [1, W], F32)
        varw = ctx.sb("varw", [1, W], F32)
        sc2 = mu
        rmsr = varw
        tmp = ctx.sb("tmpC", [128, 512], F32)
        p1 = ctx.ps("p1", [128, 512])
        p2 = ctx.ps("p2", [128, 512])
        p3 = ctx.ps("p3", [128, 512])
        pss = ctx.ps("pss", [1, 512])
        psq = ctx.ps("psq", [1, 512])
        blk = st.enter_context(nc.Block())

        ones_c = emit_ones(ctx, "onesc", [128, 1])
        ones_r = emit_ones(ctx, "onesr", [1, 128])
        ones_rb = ctx.sb("ones_rb", [1, 512], BF16)
        ctx.step("gpsimd", lambda E: E.memset(ones_rb[:], 1.0), inc_sem="ones_ms")

        ctx.dma("gpsimd", ybar[:].rearrange("p (c t) -> p c t", c=NCH),
                y_di.rearrange("(c p) t -> p c t", p=128), inc_sem="ldg")
        ctx.dma("gpsimd", wout[:].rearrange("p (c m) -> p c m", c=NCH),
                wo_di.rearrange("(c p) m -> p c m", p=128), inc_sem="ldg")
        ctx.dma("gpsimd", bo[:], bo_di, inc_sem="ldg")
        ctx.dma("sync", dp[:], dp_di, inc_sem="ld")
        ctx.dma("sync", n1w[:], n1_di, inc_sem="ld")

        ctx.wait("vector", "ldg", 16)
        ctx.wait("vector", "ld", 32)
        for c in range(NCH):
            par = c % 2
            if c >= 2:
                ctx.wait("sync", "ybc", c - 1)
            ctx.dma("sync", x1c[par][:],
                    x1_di.rearrange("(c p) t -> p c t", p=128)[:, c, :], inc_sem="ldx1")
            ctx.wait("vector", "ldx1", (c + 1) * 16)
            csl = slice(c * W, (c + 1) * W)
            def f(E, par=par, csl=csl, c=c):
                E.tensor_scalar(out=x1c[par][:], in0=x1c[par][:], scalar1=dp[:, c:c + 1],
                                scalar2=None, op0=ALU.mult)
                return E.tensor_tensor(out=ybar[:, csl], in0=ybar[:, csl],
                                       in1=x1c[par][:], op=ALU.add)
            ctx.step("vector", f, inc_sem="ybc")

        ctx.wait("tensor", "ybc")
        ctx.wait("tensor", "ldg", 48)
        ctx.wait("tensor", "ones_ms")
        mm = MM(ctx, "mmC", n_banks=2)
        mm.run(NCH, NCH, NT, 512,
               lambda kc, mc: wout[:, kc * D + mc * 128: kc * D + (mc + 1) * 128],
               lambda kc, nt: ybar[:, kc * W + nt * 512: kc * W + nt * 512 + 512],
               lambda mc, nt: m0[:, mc * W + nt * 512: mc * W + nt * 512 + 512],
               func=AF.Copy,
               bias_row=lambda mc: bo[:, mc * 128:(mc + 1) * 128],
               ones_row=ones_rb)

        for e in ("tensor", "scalar", "vector"):
            ctx.wait(e, "mmC_ev")
        lnsc = (ctx.sb("lnsc_sq2", [128, 1024], F32), ctx.sb("lnsc_srow", [1, W], F32),
                ctx.sb("lnsc_qrow", [1, W], F32))
        emit_ln_stats(ctx, "lnC1", m0, NCH, W, mu, rstd, ones_c, ps_pair=(pss, psq), scratch=lnsc)
        for e in ("tensor", "vector"):
            ctx.wait(e, "lnC1_fin")

        def prep1(E):
            E.tensor_tensor(out=sc2[:], in0=mu[:], in1=rstd[:], op=ALU.mult)
            return E.tensor_scalar(out=sc2[:], in0=sc2[:], scalar1=-1.0, scalar2=None,
                                   op0=ALU.mult)
        ctx.step("vector", prep1, inc_sem="prep1")
        ctx.wait("tensor", "prep1")
        it = 0
        for c in range(NCH):
            par = c % 2
            if c >= 2:
                ctx.wait("sync", "p1dn", (c - 1) * NT)
            ctx.dma("sync", xc[par][:],
                    x_di.rearrange("(c p) t -> p c t", p=128)[:, c, :], inc_sem="ldx")
            for t in range(NT):
                sl = slice(t * 512, (t + 1) * 512)
                if it > 0:
                    ctx.wait("tensor", "p1dn", it)
                ctx.step("tensor", lambda E, sl=sl:
                         E.matmul(p1[:, 0:512], ones_r[:], rstd[0:1, sl], start=True, stop=True),
                         inc_sem="bc1")
                ctx.step("tensor", lambda E, sl=sl:
                         E.matmul(p2[:, 0:512], ones_r[:], sc2[0:1, sl], start=True, stop=True),
                         inc_sem="bc1")
                ctx.wait("vector", "bc1")
                ctx.wait("vector", "ldx", (c + 1) * 16)
                csl = slice(c * W + t * 512, c * W + t * 512 + 512)
                xsl = sl
                def f(E, csl=csl, xsl=xsl, par=par):
                    E.tensor_tensor(out=tmp[:], in0=m0[:, csl], in1=p1[:, 0:512], op=ALU.mult)
                    E.tensor_tensor(out=tmp[:], in0=tmp[:], in1=p2[:, 0:512], op=ALU.add)
                    return E.tensor_tensor(out=m0[:, csl], in0=tmp[:],
                                           in1=xc[par][:, xsl], op=ALU.add)
                ctx.step("vector", f, inc_sem="p1dn")
                it += 1
            ctx.wait("sync", "p1dn", (c + 1) * NT)
            ctx.dma("sync", r1_do.rearrange("(c p) t -> p c t", p=128)[:, c, :],
                    m0[:, c * W:(c + 1) * W], inc_sem="str1")

        ctx.wait("scalar", "p1dn", NCH * NT)
        ctx.wait("tensor", "p1dn", NCH * NT)
        emit_ln_stats(ctx, "lnC2", m0, NCH, W, mu, rstd, ones_c, var_row=varw, ps_pair=(pss, psq), scratch=lnsc)
        for e in ("tensor", "vector", "scalar"):
            ctx.wait(e, "lnC2_fin")
        rstd2_scratch = lnsc[1]
        def prep2(E):
            E.tensor_tensor(out=sc2[:], in0=mu[:], in1=rstd[:], op=ALU.mult)
            E.tensor_scalar(out=sc2[:], in0=sc2[:], scalar1=-1.0, scalar2=None,
                            op0=ALU.mult)
            E.tensor_scalar(out=tmp[0:1, 0:W] if False else rstd2_scratch[:],
                            in0=varw[:], scalar1=1.0, scalar2=1e-5,
                            op0=ALU.mult, op1=ALU.add)
            E.reciprocal(out=rstd2_scratch[:], in_=rstd2_scratch[:])
            E.tensor_tensor(out=rmsr[:], in0=varw[:], in1=rstd2_scratch[:], op=ALU.mult)
            E.tensor_scalar(out=rmsr[:], in0=rmsr[:], scalar1=1.0, scalar2=1e-6,
                            op0=ALU.mult, op1=ALU.add)
            return E.reciprocal(out=rmsr[:], in_=rmsr[:])
        ctx.step("vector", prep2, inc_sem="prep2")
        ctx.wait("scalar", "prep2")
        ctx.step("scalar", lambda E: E.activation(out=rmsr[:], in_=rmsr[:], func=AF.Sqrt),
                 inc_sem="prep2b")
        ctx.wait("tensor", "prep2b")
        ctx.wait("vector", "prep2b")

        it = 0
        for c in range(NCH):
            par = c % 2
            if c >= 2:
                ctx.wait("sync", "p2dn", (c - 1) * NT)
                ctx.wait("vector", "stx2", (c - 1) * 16)
            for t in range(NT):
                sl = slice(t * 512, (t + 1) * 512)
                if it > 0:
                    ctx.wait("tensor", "p2dn", it)
                ctx.step("tensor", lambda E, sl=sl:
                         E.matmul(p1[:, 0:512], ones_r[:], rstd[0:1, sl], start=True, stop=True),
                         inc_sem="bc2")
                ctx.step("tensor", lambda E, sl=sl:
                         E.matmul(p2[:, 0:512], ones_r[:], sc2[0:1, sl], start=True, stop=True),
                         inc_sem="bc2")
                ctx.step("tensor", lambda E, sl=sl:
                         E.matmul(p3[:, 0:512], ones_r[:], rmsr[0:1, sl], start=True, stop=True),
                         inc_sem="bc2")
                ctx.wait("vector", "bc2")
                csl = slice(c * W + t * 512, c * W + t * 512 + 512)
                def f(E, csl=csl, par=par, c=c, tsl=sl):
                    E.tensor_tensor(out=tmp[:], in0=m0[:, csl], in1=p1[:, 0:512], op=ALU.mult)
                    E.tensor_tensor(out=tmp[:], in0=tmp[:], in1=p2[:, 0:512], op=ALU.add)
                    E.tensor_copy(out=xc[par][:, tsl], in_=tmp[:])
                    E.tensor_tensor(out=tmp[:], in0=tmp[:], in1=p3[:, 0:512], op=ALU.mult)
                    return E.tensor_scalar(out=x2n[:, csl], in0=tmp[:],
                                           scalar1=n1w[:, c:c + 1], scalar2=None,
                                           op0=ALU.mult)
                ctx.step("vector", f, inc_sem="p2dn")
                it += 1
            ctx.wait("sync", "p2dn", (c + 1) * NT)
            ctx.dma("sync", x2_do.rearrange("(c p) t -> p c t", p=128)[:, c, :],
                    xc[par][:], inc_sem="stx2")

        ctx.wait("sync", "p2dn", NCH * NT)
        ctx.dma("sync", x2n_do.rearrange("(c p) t -> p c t", p=128),
                x2n[:].rearrange("p (c t) -> p c t", c=NCH), inc_sem="stx2")
        ctx.wait("sync", "str1", NCH * 16)
        ctx.wait("sync", "stx2", (NCH + 1) * 16)
        ctx.finalize(blk)
    return nc



def build_launch_D(T=T_FULL):
    nc = bass.Bass("TRN2", target_bir_lowering=False)
    x2n_di = nc.dram_tensor("x2nf", [D, T], BF16, kind="ExternalInput").ap()
    wq_di = nc.dram_tensor("wq_slice", [D, 1536], F32, kind="ExternalInput").ap()
    ao_do = nc.dram_tensor("ao", [512, T], BF16, kind="ExternalOutput").ap()
    dbg_do = nc.dram_tensor("dbgD", [2, 32], F32, kind="ExternalOutput").ap()

    NCH = D // 128
    NTT = T // 512
    NKC = T // 128
    with ExitStack() as st:
        ctx = Ctx(nc, st)
        x2n = ctx.sb("x2nD", [128, NCH * T], BF16)
        ao_sb = x2n[:].rearrange("p (c t) -> p c t", c=NCH)
        wq = ctx.sb("wqD", [128, NCH * 1536], BF16)
        qk = ctx.sb("qkD", [128, 8 * T], BF16)
        vt = ctx.sb("vtD", [128, NKC * 520], BF16)
        q2 = ctx.sb("q2D", [128, T], BF16)
        bones = ctx.sb("bonesD", [128, 2], BF16)
        stat = ctx.sb("statD", [2, 16], F32)
        mrow = ctx.sb("mrowD", [2, 8], F32)
        brow = ctx.sb("browD", [1, 8], F32)
        bias_sb = ctx.sb("biasD", [128, 1], F32)
        den = ctx.sb("denD", [1, 512], F32)
        aotmp = ctx.sb("aotmpD", [65, 512], BF16)
        denb = ctx.sb("denbD", [1, 512], BF16)
        rc64 = ctx.sb("rc64D", [64, 512], F32)
        ones1 = ctx.sb("ones1D", [1, 128], F32)
        onesb = ctx.sb("onesbD", [1, 128], BF16)
        dmask = ctx.sb("dmaskD", [128, 4 * 512], BF16)
        ps_s = [ctx.ps(f"ps_sD{i}", [128, 512]) for i in range(2)]
        ps_o = [ctx.ps(f"ps_oD{i}", [65, 512]) for i in range(2)]
        ps_n = ctx.ps("ps_nD", [64, 512])
        ps_b = ctx.ps("ps_bD", [128, 1])
        blk = st.enter_context(nc.Block())

        ao_v = x2n[:, 0:4 * T]
        pt_v = x2n[:, 4 * T:8 * T]

        ctx.dma("gpsimd", x2n[:].rearrange("p (c t) -> p c t", c=NCH),
                x2n_di.rearrange("(c p) t -> p c t", p=128), inc_sem="ldg")
        ctx.dma("gpsimd", wq[:].rearrange("p (c m) -> p c m", c=NCH),
                wq_di.rearrange("(c p) m -> p c m", p=128), inc_sem="ldg")
        ctx.step("gpsimd", lambda E: E.memset(bones[:], 0.0), inc_sem="ms")
        ctx.step("gpsimd", lambda E: E.memset(bones[0:64, 0:1], 1.0), inc_sem="ms")
        ctx.step("gpsimd", lambda E: E.memset(bones[64:128, 1:2], 1.0), inc_sem="ms")
        ctx.step("gpsimd", lambda E: E.memset(ones1[:], 1.0), inc_sem="ms")
        ctx.step("gpsimd", lambda E: E.memset(onesb[:], 1.0), inc_sem="ms")
        ctx.step("gpsimd", lambda E: E.memset(stat[:], 0.0), inc_sem="ms")
        ctx.step("gpsimd", lambda E: E.memset(
            vt[:].rearrange("p (k h e) -> p k h e", k=NKC, h=8)[:, :, :, 64:65], 1.0),
            inc_sem="ms")
        ctx.step("gpsimd", lambda E: E.memset(dmask[:], 1.0), inc_sem="ms")
        for off in range(4):
            ctx.step("gpsimd", lambda E, off=off: E.affine_select(
                out=dmask[:, off * 512:(off + 1) * 512],
                in_=dmask[:, off * 512:(off + 1) * 512],
                pattern=[[1, 512]], base=-off * 128,
                channel_multiplier=-1, compare_op=ALU.is_ge, fill=0.0),
                inc_sem="ms")

        ctx.wait("tensor", "ldg", 32)
        ctx.wait("tensor", "ms")
        mm = MM(ctx, "mmD", n_banks=2)
        for sec in range(2):
            mm.run(NCH, 4, NTT, 512,
                   lambda kc, mc, sec=sec: wq[:, kc * 1536 + sec * 512 + mc * 128:
                                               kc * 1536 + sec * 512 + (mc + 1) * 128],
                   lambda kc, nt: x2n[:, kc * T + nt * 512: kc * T + nt * 512 + 512],
                   lambda mc, nt, sec=sec: qk[:, (sec * 4 + mc) * T + nt * 512:
                                              (sec * 4 + mc) * T + nt * 512 + 512],
                   func=AF.Copy)
        for tc in range(NKC):
            b = mm.rr % 2
            mm.rr += 1
            bank = mm.banks[b]
            if mm.uses[b] > 0:
                ctx.wait("tensor", "mmD_ev", mm.uses[b])
            for kc in range(NCH):
                ctx.step("tensor",
                         lambda E, bank=bank, kc=kc, tc=tc, s=(kc == 0), p=(kc == NCH - 1):
                         E.matmul(bank[:, 0:512],
                                  x2n[:, kc * T + tc * 128: kc * T + (tc + 1) * 128],
                                  wq[:, kc * 1536 + 1024: kc * 1536 + 1536],
                                  start=s, stop=p),
                         inc_sem="mmD_pe" if kc == NCH - 1 else None)
            ctx.wait("scalar", "mmD_pe")
            ctx.step("scalar",
                     lambda E, bank=bank, tc=tc:
                     E.activation(out=vt[:, tc * 520:(tc + 1) * 520]
                                  .rearrange("p (h e) -> p h e", h=8)[:, :, 0:64],
                                  in_=bank[:, 0:512], func=AF.Copy),
                     inc_sem="mmD_ev")
            mm.uses[b] = ctx.cnt("mmD_ev")

        for c in range(8):
            if c >= 1:
                ctx.wait("scalar", "b_pe", c * NTT)
            ctx.wait("scalar", "mmD_ev", ctx.cnt("mmD_ev"))
            ctx.step("scalar", lambda E, c=c:
                     E.activation(out=q2[:], in_=qk[:, c * T:(c + 1) * T], func=AF.Square),
                     inc_sem="b_sq")
            for t in range(NTT):
                ctx.wait("tensor", "b_sq", c + 1)
                ctx.wait("tensor", "b_dv", 0 if (c == 0 and t == 0) else (c * NTT + t))
                ctx.step("tensor", lambda E, t=t:
                         E.matmul(ps_b[0:2, 0:1] if False else ps_s[0][0:2, 0:512],
                                  bones[:], q2[:, t * 512:(t + 1) * 512],
                                  start=True, stop=True),
                         inc_sem="b_pe" if t == NTT - 1 else "b_pe")
                ctx.wait("vector", "b_pe")
                def f(E, c=c, t=t):
                    E.tensor_reduce(out=mrow[0:2, 0:1], in_=ps_s[0][0:2, 0:512],
                                    axis=AX.X, op=ALU.max)
                    return E.tensor_tensor(out=stat[0:2, c:c + 1], in0=stat[0:2, c:c + 1],
                                           in1=mrow[0:2, 0:1], op=ALU.max)
                ctx.step("vector", f, inc_sem="b_dv")
        def bfin(E):
            E.tensor_tensor(out=mrow[0:2, 0:4], in0=stat[0:2, 0:4], in1=stat[0:2, 4:8],
                            op=ALU.add)
            return E.tensor_scalar(out=mrow[0:2, 0:4], in0=mrow[0:2, 0:4],
                                   scalar1=-1.0 / 16.0, scalar2=None, op0=ALU.mult)
        ctx.wait("vector", "b_dv", 64)
        ctx.step("vector", bfin, inc_sem="b_f3")
        ctx.wait("sync", "b_f3")
        for h in range(8):
            r, j = h % 2, h // 2
            ctx.dma("sync", brow[0:1, h:h + 1], mrow[r:r + 1, j:j + 1], inc_sem="b_dma")

        vall = ctx.cnt("mmD_ev")
        ctx.wait("scalar", "mmD_ev", vall)
        ctx.wait("scalar", "b_dma", 8 * 16)
        ctx.wait("vector", "ms", ctx.cnt("ms"))
        for h in range(8):
            r, j = h % 2, h // 2
            qbase = j * T
            kbase = (4 + j) * T
            if h == 0:
                ctx.wait("tensor", "b_dv", 64)
                ctx.wait("tensor", "b_dma", 8 * 16)
            else:
                ctx.wait("tensor", "bb_ev", h)
            ctx.step("tensor", lambda E, h=h:
                     E.matmul(ps_b[:, 0:1], ones1[:], brow[0:1, h:h + 1],
                              start=True, stop=True),
                     inc_sem="bb_pe")
            ctx.wait("scalar", "bb_pe")
            ctx.step("scalar", lambda E: E.activation(out=bias_sb[:], in_=ps_b[:, 0:1],
                                                      func=AF.Copy),
                     inc_sem="bb_ev")
            ctx.wait("scalar", "bb_ev")
            for tqt in range(NTT):
                kmax = 4 * (tqt + 1)
                tq_sl = slice(tqt * 512, (tqt + 1) * 512)
                sev_base = ctx.cnt("s_ev")
                mk_base = ctx.cnt("s_mk")
                for kc in range(kmax):
                    par = kc % 2
                    it = ctx.bump("s_it", 1)
                    if it > 2:
                        ctx.wait("tensor", "s_ev", it - 2)
                    ctx.step("tensor",
                             lambda E, par=par, kc=kc, r=r, kbase=kbase, qbase=qbase, tq_sl=tq_sl:
                             E.matmul(ps_s[par][:, 0:512],
                                      qk[r * 64:r * 64 + 64,
                                         kbase + kc * 128: kbase + (kc + 1) * 128],
                                      qk[r * 64:r * 64 + 64, qbase:qbase + T][:, tq_sl],
                                      start=True, stop=True),
                             inc_sem="s_pe")
                    ctx.wait("scalar", "s_pe")
                    need_mask = kc >= 4 * tqt
                    ctx.step("scalar",
                             lambda E, par=par, kc=kc:
                             E.activation(out=pt_v[:, kc * 512:(kc + 1) * 512],
                                          in_=ps_s[par][:, 0:512], func=AF.Exp,
                                          scale=0.125, bias=bias_sb[:]),
                             inc_sem="s_ev")
                    if need_mask:
                        off = kc - 4 * tqt
                        ctx.wait("vector", "s_ev")
                        ctx.step("vector",
                                 lambda E, kc=kc, off=off:
                                 E.tensor_tensor(out=pt_v[:, kc * 512:(kc + 1) * 512],
                                                 in0=pt_v[:, kc * 512:(kc + 1) * 512],
                                                 in1=dmask[:, off * 512:(off + 1) * 512],
                                                 op=ALU.mult),
                                 inc_sem="s_mk")
                parO = tqt % 2
                if ctx.cnt("o_done") >= 2:
                    ctx.wait("tensor", "o_done", ctx.cnt("o_done") - 1)
                for kc in range(kmax):
                    if kc >= 4 * tqt:
                        ctx.wait("tensor", "s_mk", mk_base + (kc - 4 * tqt) + 1)
                    else:
                        ctx.wait("tensor", "s_ev", sev_base + kc + 1)
                    ctx.step("tensor",
                             lambda E, parO=parO, kc=kc, h=h, s=(kc == 0), p=(kc == kmax - 1):
                             E.matmul(ps_o[parO][0:65, 0:512],
                                      vt[:, kc * 520 + h * 65: kc * 520 + (h + 1) * 65],
                                      pt_v[:, kc * 512:(kc + 1) * 512],
                                      start=s, stop=p),
                             inc_sem="o_pe" if kc == kmax - 1 else None)
                ctx.wait("scalar", "o_pe", ctx.cnt("o_pe"))
                ctx.wait("scalar", "o_done", ctx.cnt("o_done"))
                def evac(E, parO=parO):
                    E.activation(out=aotmp[0:64, :], in_=ps_o[parO][0:64, 0:512],
                                 func=AF.Copy)
                    return E.activation(out=denb[:], in_=ps_o[parO][64:65, 0:512],
                                        func=AF.Copy)
                ctx.step("scalar", evac, inc_sem="o_ev")
                ctx.wait("tensor", "o_ev", ctx.cnt("o_ev"))
                ctx.wait("tensor", "o_done", ctx.cnt("o_done"))
                ctx.step("tensor", lambda E:
                         E.matmul(ps_n[0:64, 0:512], onesb[0:1, 0:64],
                                  denb[:], start=True, stop=True),
                         inc_sem="n_pe")
                ctx.wait("vector", "n_pe", ctx.cnt("n_pe"))
                ctx.step("vector", lambda E:
                         E.reciprocal(out=rc64[:], in_=ps_n[0:64, 0:512]),
                         inc_sem="n_dv")
                ctx.wait("vector", "n_dv", ctx.cnt("n_dv"))
                ctx.step("vector",
                         lambda E, r=r, j=j, tq_sl=tq_sl:
                         E.tensor_tensor(out=ao_v[r * 64:r * 64 + 64, j * T:(j + 1) * T][:, tq_sl],
                                         in0=aotmp[0:64, 0:512], in1=rc64[:],
                                         op=ALU.mult),
                         inc_sem="o_done")

        ctx.wait("sync", "b_dma", 128)
        ctx.dma("sync", dbg_do[:, 0:16], stat[:], inc_sem="st")
        ctx.dma("sync", dbg_do[0:1, 16:24], brow[:], inc_sem="st")
        ctx.dma("sync", dbg_do[0:2, 24:32], mrow[:], inc_sem="st")
        ctx.wait("sync", "o_done", 64)
        ctx.dma("sync", ao_do.rearrange("(c p) t -> p c t", p=128),
                ao_v.rearrange("p (c t) -> p c t", c=4), inc_sem="st")
        ctx.wait("sync", "st", 64)
        ctx.finalize(blk)
    return nc



def build_launch_E1(W=2048):
    nc = bass.Bass("TRN2", target_bir_lowering=False)
    ao_di = nc.dram_tensor("aoi", [D, W], BF16, kind="ExternalInput").ap()
    x2_di = nc.dram_tensor("x2i", [D, W], F32, kind="ExternalInput").ap()
    ow_di = nc.dram_tensor("o_wT", [D, D], F32, kind="ExternalInput").ap()
    n2_di = nc.dram_tensor("n2w", [128, 8], F32, kind="ExternalInput").ap()
    gw_di = nc.dram_tensor("gate0T", [D, 8], F32, kind="ExternalInput").ap()
    idf_di = nc.dram_tensor("idf", [128, 128], F32, kind="ExternalInput").ap()
    t1_do = nc.dram_tensor("t1", [D, W], F32, kind="ExternalOutput").ap()
    x2m_do = nc.dram_tensor("x2m", [D, W], BF16, kind="ExternalOutput").ap()
    fb_do = nc.dram_tensor("factor", [16, 128], F32, kind="ExternalOutput").ap()

    NCH = D // 128
    NT = W // 512
    with ExitStack() as st:
        ctx = Ctx(nc, st)
        ao = ctx.sb("aoE", [128, NCH * W], BF16)
        x2m = ao
        t1 = ctx.sb("t1E", [128, NCH * W], F32)
        ow = ctx.sb("owE", [128, NCH * D], BF16)
        n2w = ctx.sb("n2wE", [128, 8], F32)
        gw = ctx.sb("gwE", [128, NCH * 8], BF16)
        idf = ctx.sb("idfE", [128, 128], F32)
        x2c = [ctx.sb(f"x2cE{i}", [128, W], F32) for i in range(2)]
        ex8 = ctx.sb("ex8E", [8, W], F32)
        ext = ctx.sb("extE", [128, 16 * 8], F32)
        ftok = ctx.sb("ftokE", [128, 16], F32)
        ftr = ctx.sb("ftrE", [16, 128], F32)
        ex2 = ctx.sb("ex2E", [128, 128], F32)
        ex2m = ctx.sb("ex2mE", [128, 16], F32)
        scr = ctx.sb("scrE", [128, 16], F32)
        mu = ctx.sb("muE", [1, W], F32)
        rstd = ctx.sb("rstdE", [1, W], F32)
        tmp = ctx.sb("tmpE", [128, 512], F32)
        p1 = ctx.ps("p1E", [128, 512])
        p2 = ctx.ps("p2E", [128, 512])
        pss = ctx.ps("pssE", [1, 512])
        psq = ctx.ps("psqE", [1, 512])
        pst = ctx.ps("pstE", [128, 128], F32)
        blk = st.enter_context(nc.Block())

        ones_c = emit_ones(ctx, "onesc", [128, 1])
        ones_r = emit_ones(ctx, "onesr", [1, 128])
        ctx.dma("gpsimd", ao[:].rearrange("p (c t) -> p c t", c=NCH),
                ao_di.rearrange("(c p) t -> p c t", p=128), inc_sem="ldg")
        ctx.dma("gpsimd", ow[:].rearrange("p (c m) -> p c m", c=NCH),
                ow_di.rearrange("(c p) m -> p c m", p=128), inc_sem="ldg")
        ctx.dma("gpsimd", gw[:].rearrange("p (c m) -> p c m", c=NCH),
                gw_di.rearrange("(c p) m -> p c m", p=128), inc_sem="ldg")
        ctx.dma("sync", n2w[:], n2_di, inc_sem="ld")
        ctx.dma("sync", idf[:], idf_di, inc_sem="ld")

        ctx.wait("tensor", "ldg", 32)
        mm = MM(ctx, "mmE1", n_banks=2)
        mm.run(NCH, NCH, NT, 512,
               lambda kc, mc: ow[:, kc * D + mc * 128: kc * D + (mc + 1) * 128],
               lambda kc, nt: ao[:, kc * W + nt * 512: kc * W + nt * 512 + 512],
               lambda mc, nt: t1[:, mc * W + nt * 512: mc * W + nt * 512 + 512],
               func=AF.Copy)
        for c in range(NCH):
            par = c % 2
            if c >= 2:
                ctx.wait("sync", "t1dn", c - 1)
            ctx.dma("sync", x2c[par][:],
                    x2_di.rearrange("(c p) t -> p c t", p=128)[:, c, :], inc_sem="ldx2")
            ctx.wait("vector", "ldx2", (c + 1) * 16)
            ctx.wait("vector", "mmE1_ev", (c + 1) * NT)
            csl = slice(c * W, (c + 1) * W)
            ctx.step("vector", lambda E, csl=csl, par=par:
                     E.tensor_tensor(out=t1[:, csl], in0=t1[:, csl], in1=x2c[par][:],
                                     op=ALU.add),
                     inc_sem="t1dn")
        ctx.wait("sync", "t1dn", NCH)
        ctx.dma("sync", t1_do.rearrange("(c p) t -> p c t", p=128),
                t1[:].rearrange("p (c t) -> p c t", c=NCH), inc_sem="stt1")

        for e in ("tensor", "scalar", "vector"):
            ctx.wait(e, "t1dn", NCH)
            ctx.wait(e, "ld", 32)
            ctx.wait(e, "ones_ms")
        emit_ln_stats(ctx, "lnE1", t1, NCH, W, mu, rstd, ones_c, eps=1e-6,
                      rms_only=True, ps_pair=(pss, psq))
        for e in ("tensor", "vector"):
            ctx.wait(e, "lnE1_fin")
        emit_normalize(ctx, "nrmE1", t1, x2m, NCH, W, mu, rstd, ones_r,
                       gain_col=lambda c: n2w[:, c:c + 1], rms_only=True,
                       ps_pair=(p1, p2))

        ctx.step("gpsimd", lambda E: E.memset(ftr[:], 1.0), inc_sem="fcp")
        ctx.wait("sync", "fcp")
        ctx.dma("sync", fb_do, ftr[:], inc_sem="stf")
        ctx.wait("sync", "stf", 16)
        ctx.wait("sync", "stt1", 16)
        ctx.wait("sync", "nrmE1_done")
        ctx.dma("sync", x2m_do.rearrange("(c p) t -> p c t", p=128),
                x2m[:].rearrange("p (c t) -> p c t", c=NCH), inc_sem="stx")
        ctx.wait("sync", "stx", 16)
        ctx.finalize(blk)
    return nc



def build_launch_E2(W=2048, NGOV=None):
    nc = bass.Bass("TRN2", target_bir_lowering=False)
    x2m_di = nc.dram_tensor("x2mi", [D, W], BF16, kind="ExternalInput").ap()
    up_di = nc.dram_tensor("upT", [D, ED], F32, kind="ExternalInput").ap()
    dn_di = nc.dram_tensor("downT", [ED, D], F32, kind="ExternalInput").ap()
    f_di = nc.dram_tensor("factori", [1, W], F32, kind="ExternalInput").ap()
    moe_do = nc.dram_tensor("moe", [D, W], BF16, kind="ExternalOutput").ap()

    NCH = D // 128
    NT = W // 512
    NG = NGOV or 8
    with ExitStack() as st:
        ctx = Ctx(nc, st)
        x2m = ctx.sb("x2mE2", [128, NCH * W], BF16)
        acc = ctx.sb("accE2", [128, NCH * W], F32)
        usil = ctx.sb("usilE2", [128, 4 * W], BF16)
        upg = [ctx.sb(f"upgE2{i}", [128, NCH * 512], BF16) for i in range(2)]
        dng = [ctx.sb(f"dngE2{i}", [128, 4 * D], BF16) for i in range(2)]
        ftb = ctx.sb("ftbE2", [128, W], BF16)
        frow = ctx.sb("frowE2", [1, W], F32)
        tmp = ctx.sb("tmpE2", [128, 512], F32)
        pfb = ctx.ps("pfbE2", [128, 512])
        blk = st.enter_context(nc.Block())

        ones_r = emit_ones(ctx, "onesr", [1, 128])
        ctx.dma("gpsimd", x2m[:].rearrange("p (c t) -> p c t", c=NCH),
                x2m_di.rearrange("(c p) t -> p c t", p=128), inc_sem="ldg")
        ctx.dma("sync", frow[:], f_di, inc_sem="ld")
        ctx.wait("tensor", "ld", 16)
        ctx.wait("tensor", "ones_ms")
        for t in range(NT):
            if t > 0:
                ctx.wait("tensor", "fb_ev", t)
            ctx.step("tensor", lambda E, t=t:
                     E.matmul(pfb[:, 0:512], ones_r[:], frow[0:1, t * 512:(t + 1) * 512],
                              start=True, stop=True), inc_sem="fb_pe")
            ctx.wait("scalar", "fb_pe")
            ctx.step("scalar", lambda E, t=t:
                     E.activation(out=ftb[:, t * 512:(t + 1) * 512], in_=pfb[:, 0:512],
                                  func=AF.Copy), inc_sem="fb_ev")
        for c in range(NCH):
            ctx.step("vector", lambda E, c=c: E.memset(acc[:, c * W:(c + 1) * W], 0.0),
                     inc_sem="accz")

        mmu = MM(ctx, "mmU", n_banks=2)
        mmd = MM(ctx, "mmDn", n_banks=2)
        ctx.wait("tensor", "ldg", 16)
        for g in range(NG):
            par = g % 2
            if g >= 2:
                ctx.wait("gpsimd", "mmU_pe", ctx.cnt("mmU_pe"))
                ctx.wait("gpsimd", "mmDn_pe", ctx.cnt("mmDn_pe"))
            ctx.dma("gpsimd", upg[par][:].rearrange("p (c m) -> p c m", c=NCH),
                    up_di.rearrange("(c p) m -> p c m", p=128)[:, :, g * 512:(g + 1) * 512],
                    inc_sem="ldu")
            ctx.dma("gpsimd", dng[par][:].rearrange("p (c m) -> p c m", c=4),
                    dn_di.rearrange("(c p) m -> p c m", p=128)[:, g * 4:(g + 1) * 4, :],
                    inc_sem="ldud")
            ctx.wait("tensor", "ldu", (g + 1) * 16)
            ctx.wait("tensor", "uscl", ctx.cnt("uscl"))
            ctx.wait("scalar", "uscl", ctx.cnt("uscl"))
            mmu.run(NCH, 4, NT, 512,
                    lambda kc, mc, par=par: upg[par][:, kc * 512 + mc * 128: kc * 512 + (mc + 1) * 128],
                    lambda kc, nt: x2m2[:, kc * W + nt * 512: kc * W + nt * 512 + 512],
                    lambda mc, nt: usil[:, mc * W + nt * 512: mc * W + nt * 512 + 512],
                    func=AF.Silu)
            ctx.wait("vector", "mmU_ev", ctx.cnt("mmU_ev"))
            ctx.wait("vector", "fb_ev", NT)
            for mc in range(4):
                ctx.step("vector", lambda E, mc=mc:
                         E.tensor_tensor(out=usil[:, mc * W:(mc + 1) * W],
                                         in0=usil[:, mc * W:(mc + 1) * W],
                                         in1=ftb[:], op=ALU.mult),
                         inc_sem="uscl")
            ctx.wait("tensor", "uscl", ctx.cnt("uscl"))
            ctx.wait("tensor", "ldud", (g + 1) * 16)
            ctx.wait("vector", "accz", NCH)
            def dn_out(mc, nt, g=g):
                return ("acc", mc, nt)
            for mc in range(NCH):
                for nt in range(NT):
                    b = mmd.rr % 2
                    mmd.rr += 1
                    bank = mmd.banks[b]
                    if mmd.uses[b] > 0:
                        ctx.wait("tensor", "mmDn_ev", mmd.uses[b])
                    for kc in range(4):
                        ctx.step("tensor",
                                 lambda E, bank=bank, kc=kc, mc=mc, nt=nt, par=par,
                                        s=(kc == 0), p=(kc == 3):
                                 E.matmul(bank[:, 0:512],
                                          dng[par][:, kc * D + mc * 128: kc * D + (mc + 1) * 128],
                                          usil[:, kc * W + nt * 512: kc * W + nt * 512 + 512],
                                          start=s, stop=p),
                                 inc_sem="mmDn_pe" if kc == 3 else None)
                    ctx.wait("vector", "mmDn_pe", ctx.cnt("mmDn_pe"))
                    ctx.step("vector",
                             lambda E, bank=bank, mc=mc, nt=nt:
                             E.tensor_tensor(out=acc[:, mc * W + nt * 512: mc * W + nt * 512 + 512],
                                             in0=acc[:, mc * W + nt * 512: mc * W + nt * 512 + 512],
                                             in1=bank[:, 0:512], op=ALU.add),
                             inc_sem="mmDn_ev")
                    mmd.uses[b] = ctx.cnt("mmDn_ev")

        ctx.wait("gpsimd", "mmDn_ev", ctx.cnt("mmDn_ev"))
        ctx.dma("gpsimd", moe_do.rearrange("(c p) t -> p c t", p=128),
                acc[:].rearrange("p (c t) -> p c t", c=NCH), inc_sem="stv")
        ctx.wait("gpsimd", "stv", 16)
        ctx.finalize(blk)
    return nc



def build_launch_E3(W=2048):
    nc = bass.Bass("TRN2", target_bir_lowering=False)
    t1_di = nc.dram_tensor("t1i", [D, W], F32, kind="ExternalInput").ap()
    moe_di = nc.dram_tensor("moei", [D, W], BF16, kind="ExternalInput").ap()
    r1_di = nc.dram_tensor("r1i", [D, W], F32, kind="ExternalInput").ap()
    r2_do = nc.dram_tensor("r2", [D, W], F32, kind="ExternalOutput").ap()
    x3_do = nc.dram_tensor("x3", [D, W], BF16, kind="ExternalOutput").ap()

    NCH = D // 128
    NT = W // 512
    with ExitStack() as st:
        ctx = Ctx(nc, st)
        t1 = ctx.sb("t1E3", [128, NCH * W], F32)
        x3 = ctx.sb("x3E3", [128, NCH * W], BF16)
        mc2 = [ctx.sb(f"mcE3{i}", [128, W], F32) for i in range(2)]
        mu = ctx.sb("muE3", [1, W], F32)
        rstd = ctx.sb("rstdE3", [1, W], F32)
        tmp = ctx.sb("tmpE3", [128, 512], F32)
        p1 = ctx.ps("p1E3", [128, 512])
        p2 = ctx.ps("p2E3", [128, 512])
        pss = ctx.ps("pssE3", [1, 512])
        psq = ctx.ps("psqE3", [1, 512])
        blk = st.enter_context(nc.Block())

        ones_c = emit_ones(ctx, "onesc", [128, 1])
        ones_r = emit_ones(ctx, "onesr", [1, 128])
        ctx.dma("gpsimd", t1[:].rearrange("p (c t) -> p c t", c=NCH),
                t1_di.rearrange("(c p) t -> p c t", p=128), inc_sem="ldg")
        ctx.wait("vector", "ldg", 16)
        for c in range(NCH):
            par = c % 2
            if c >= 2:
                ctx.wait("gpsimd", "t2dn", c - 1)
            ctx.dma("gpsimd", mc2[par][:],
                    moe_di.rearrange("(c p) t -> p c t", p=128)[:, c, :], inc_sem="ldm")
            ctx.wait("vector", "ldm", (c + 1) * 16)
            ctx.step("vector", lambda E, c=c, par=par:
                     E.tensor_tensor(out=t1[:, c * W:(c + 1) * W],
                                     in0=t1[:, c * W:(c + 1) * W], in1=mc2[par][:],
                                     op=ALU.add),
                     inc_sem="t2dn")
        for e in ("tensor", "scalar", "vector"):
            ctx.wait(e, "t2dn", NCH)
            ctx.wait(e, "ones_ms")
        lnsc = (ctx.sb("lnscE3", [128, 1024], F32), ctx.sb("lnsrE3", [1, W], F32),
                ctx.sb("lnqrE3", [1, W], F32))
        emit_ln_stats(ctx, "lnE3a", t1, NCH, W, mu, rstd, ones_c, ps_pair=(pss, psq),
                      scratch=lnsc)
        for e in ("tensor", "vector"):
            ctx.wait(e, "lnE3a_fin")
        sc2 = mu
        def prep1(E):
            E.tensor_tensor(out=sc2[:], in0=mu[:], in1=rstd[:], op=ALU.mult)
            return E.tensor_scalar(out=sc2[:], in0=sc2[:], scalar1=-1.0, scalar2=None,
                                   op0=ALU.mult)
        ctx.step("vector", prep1, inc_sem="prep1")
        ctx.wait("tensor", "prep1")
        ctx.wait("sync", "t2dn", NCH)
        it = 0
        for c in range(NCH):
            par = c % 2
            if c >= 2:
                ctx.wait("sync", "p1dn", (c - 1) * NT)
            ctx.dma("sync", mc2[par][:],
                    r1_di.rearrange("(c p) t -> p c t", p=128)[:, c, :], inc_sem="ldr")
            for t in range(NT):
                sl = slice(t * 512, (t + 1) * 512)
                if it > 0:
                    ctx.wait("tensor", "p1dn", it)
                ctx.step("tensor", lambda E, sl=sl:
                         E.matmul(p1[:, 0:512], ones_r[:], rstd[0:1, sl], start=True, stop=True),
                         inc_sem="bc1")
                ctx.step("tensor", lambda E, sl=sl:
                         E.matmul(p2[:, 0:512], ones_r[:], sc2[0:1, sl], start=True, stop=True),
                         inc_sem="bc1")
                ctx.wait("vector", "bc1")
                ctx.wait("vector", "ldr", (c + 1) * 16)
                csl = slice(c * W + t * 512, c * W + t * 512 + 512)
                def f(E, csl=csl, sl=sl, par=par):
                    E.tensor_tensor(out=tmp[:], in0=t1[:, csl], in1=p1[:, 0:512], op=ALU.mult)
                    E.tensor_tensor(out=tmp[:], in0=tmp[:], in1=p2[:, 0:512], op=ALU.add)
                    return E.tensor_tensor(out=t1[:, csl], in0=tmp[:],
                                           in1=mc2[par][:, sl], op=ALU.add)
                ctx.step("vector", f, inc_sem="p1dn")
                it += 1
            ctx.wait("sync", "p1dn", (c + 1) * NT)
            ctx.dma("sync", r2_do.rearrange("(c p) t -> p c t", p=128)[:, c, :],
                    t1[:, c * W:(c + 1) * W], inc_sem="str2")
        ctx.wait("scalar", "p1dn", NCH * NT)
        ctx.wait("tensor", "p1dn", NCH * NT)
        emit_ln_stats(ctx, "lnE3b", t1, NCH, W, mu, rstd, ones_c, ps_pair=(pss, psq),
                      scratch=lnsc)
        for e in ("tensor", "vector"):
            ctx.wait(e, "lnE3b_fin")
        emit_normalize(ctx, "nrmE3", t1, x3, NCH, W, mu, rstd, ones_r,
                       ps_pair=(p1, p2))
        ctx.wait("sync", "nrmE3_done")
        ctx.dma("sync", x3_do.rearrange("(c p) t -> p c t", p=128),
                x3[:].rearrange("p (c t) -> p c t", c=NCH), inc_sem="stx3")
        ctx.wait("sync", "stx3", 16)
        ctx.wait("sync", "str2", NCH * 16)
        ctx.finalize(blk)
    return nc



def build_launch_E4(W=2048):
    nc = bass.Bass("TRN2", target_bir_lowering=False)
    x3_di = nc.dram_tensor("x3i", [D, W], BF16, kind="ExternalInput").ap()
    wi_di = nc.dram_tensor("w_inT", [D, 2 * D], F32, kind="ExternalInput").ap()
    bi_di = nc.dram_tensor("b_in", [1, 2 * D], F32, kind="ExternalInput").ap()
    ws_di = nc.dram_tensor("w_sT", [D, D], F32, kind="ExternalInput").ap()
    bs_di = nc.dram_tensor("b_s", [1, D], F32, kind="ExternalInput").ap()
    a_do = nc.dram_tensor("a1", [D, W], BF16, kind="ExternalOutput").ap()
    bb_do = nc.dram_tensor("bb", [D, W], BF16, kind="ExternalOutput").ap()

    NCH = D // 128
    NT = W // 512
    with ExitStack() as st:
        ctx = Ctx(nc, st)
        x3 = ctx.sb("x3E4", [128, NCH * W], BF16)
        a_sb = ctx.sb("aE4", [128, NCH * W], BF16)
        val = ctx.sb("valE4", [128, NCH * W], BF16)
        sv = ctx.sb("svE4", [128, NCH * W], BF16)
        wi = ctx.sb("wiE4", [128, NCH * 2 * D], BF16)
        ws = ctx.sb("wsE4", [128, NCH * D], BF16)
        bi = ctx.sb("biE4", [1, 2 * D], BF16)
        bs = ctx.sb("bsE4", [1, D], BF16)
        tmp = ctx.sb("tmpE4", [128, 512], F32)
        blk = st.enter_context(nc.Block())

        ones_rb = ctx.sb("ones_rbE4", [1, 512], BF16)
        ctx.step("gpsimd", lambda E: E.memset(ones_rb[:], 1.0), inc_sem="ones_ms")
        ctx.dma("gpsimd", x3[:].rearrange("p (c t) -> p c t", c=NCH),
                x3_di.rearrange("(c p) t -> p c t", p=128), inc_sem="ldg")
        ctx.dma("gpsimd", wi[:].rearrange("p (c m) -> p c m", c=NCH),
                wi_di.rearrange("(c p) m -> p c m", p=128), inc_sem="ldg")
        ctx.dma("gpsimd", ws[:].rearrange("p (c m) -> p c m", c=NCH),
                ws_di.rearrange("(c p) m -> p c m", p=128), inc_sem="ldg")
        ctx.dma("gpsimd", bi[:], bi_di, inc_sem="ldg")
        ctx.dma("gpsimd", bs[:], bs_di, inc_sem="ldg")
        ctx.wait("tensor", "ldg", 80)
        ctx.wait("tensor", "ones_ms")
        mm = MM(ctx, "mmE4", n_banks=2)
        mm.run(NCH, NCH, NT, 512,
               lambda kc, mc: wi[:, kc * 2 * D + mc * 128: kc * 2 * D + (mc + 1) * 128],
               lambda kc, nt: x3[:, kc * W + nt * 512: kc * W + nt * 512 + 512],
               lambda mc, nt: a_sb[:, mc * W + nt * 512: mc * W + nt * 512 + 512],
               func=AF.Sigmoid, scale=-1.0,
               bias_row=lambda mc: bi[:, mc * 128:(mc + 1) * 128],
               ones_row=ones_rb)
        mm.run(NCH, NCH, NT, 512,
               lambda kc, mc: wi[:, kc * 2 * D + D + mc * 128: kc * 2 * D + D + (mc + 1) * 128],
               lambda kc, nt: x3[:, kc * W + nt * 512: kc * W + nt * 512 + 512],
               lambda mc, nt: val[:, mc * W + nt * 512: mc * W + nt * 512 + 512],
               func=AF.Copy,
               bias_row=lambda mc: bi[:, D + mc * 128: D + (mc + 1) * 128],
               ones_row=ones_rb)
        mm.run(NCH, NCH, NT, 512,
               lambda kc, mc: ws[:, kc * D + mc * 128: kc * D + (mc + 1) * 128],
               lambda kc, nt: val[:, kc * W + nt * 512: kc * W + nt * 512 + 512],
               lambda mc, nt: sv[:, mc * W + nt * 512: mc * W + nt * 512 + 512],
               func=AF.Tanh,
               bias_row=lambda mc: bs[:, mc * 128:(mc + 1) * 128],
               ones_row=ones_rb)
        ctx.wait("vector", "mmE4_ev", ctx.cnt("mmE4_ev"))
        for c in range(NCH):
            csl = slice(c * W, (c + 1) * W)
            def f(E, csl=csl):
                E.tensor_scalar(out=tmp[:, 0:512], in0=tmp[:, 0:512], scalar1=0.0,
                                scalar2=None, op0=ALU.mult) if False else None
                E.tensor_scalar(out=sv[:, csl], in0=sv[:, csl], scalar1=1.0,
                                scalar2=None, op0=ALU.mult) if False else None
                E.tensor_tensor(out=val[:, csl], in0=a_sb[:, csl], in1=sv[:, csl],
                                op=ALU.mult)
                return E.tensor_tensor(out=val[:, csl], in0=sv[:, csl], in1=val[:, csl],
                                       op=ALU.subtract)
            ctx.step("vector", f, inc_sem="bbdn")
        ctx.wait("sync", "bbdn", NCH)
        ctx.dma("sync", a_do.rearrange("(c p) t -> p c t", p=128),
                a_sb[:].rearrange("p (c t) -> p c t", c=NCH), inc_sem="st")
        ctx.dma("sync", bb_do.rearrange("(c p) t -> p c t", p=128),
                val[:].rearrange("p (c t) -> p c t", c=NCH), inc_sem="st")
        ctx.wait("sync", "st", 32)
        ctx.finalize(blk)
    return nc



def build_launch_E(W=2048):
    nc = bass.Bass("TRN2", target_bir_lowering=False)
    ao_di = nc.dram_tensor("aoi", [D, W], BF16, kind="ExternalInput").ap()
    x2_di = nc.dram_tensor("x2i", [D, W], F32, kind="ExternalInput").ap()
    r1_di = nc.dram_tensor("r1i", [D, W], F32, kind="ExternalInput").ap()
    ow_di = nc.dram_tensor("o_wT", [D, D], F32, kind="ExternalInput").ap()
    n2_di = nc.dram_tensor("n2w", [128, 8], F32, kind="ExternalInput").ap()
    up_di = nc.dram_tensor("upT", [D, ED], F32, kind="ExternalInput").ap()
    dn_di = nc.dram_tensor("downT", [ED, D], F32, kind="ExternalInput").ap()
    wi_di = nc.dram_tensor("w_inT", [D, 2 * D], F32, kind="ExternalInput").ap()
    bi_di = nc.dram_tensor("b_in", [1, 2 * D], F32, kind="ExternalInput").ap()
    ws_di = nc.dram_tensor("w_sT", [D, D], F32, kind="ExternalInput").ap()
    bs_di = nc.dram_tensor("b_s", [1, D], F32, kind="ExternalInput").ap()
    r2_do = nc.dram_tensor("r2", [D, W], F32, kind="ExternalOutput").ap()
    x3_do = nc.dram_tensor("x3", [D, W], BF16, kind="ExternalOutput").ap()
    a_do = nc.dram_tensor("a1", [D, W], BF16, kind="ExternalOutput").ap()
    bb_do = nc.dram_tensor("bb", [D, W], BF16, kind="ExternalOutput").ap()
    t1_dd = nc.dram_tensor("t1park", [128, (D // 128) * W], F32, kind="Internal").ap()
    x2m_dd = nc.dram_tensor("x2mpark", [128, (D // 128) * W], BF16, kind="Internal").ap()
    x3_dd = nc.dram_tensor("x3park", [128, (D // 128) * W], BF16, kind="Internal").ap()

    NCH = D // 128
    NT = W // 512
    NG = 8
    with ExitStack() as st:
        ctx = Ctx(nc, st)
        mu = ctx.sb("muE", [1, W], F32)
        rstd = ctx.sb("rstdE", [1, W], F32)
        ones_c = emit_ones(ctx, "onesc", [128, 1])
        ones_r = emit_ones(ctx, "onesr", [1, 128])
        ones_rb = ctx.sb("ones_rbE", [1, 512], BF16)
        ctx.step("gpsimd", lambda E: E.memset(ones_rb[:], 1.0), inc_sem="ones_ms")
        p1 = ctx.ps("p1E", [128, 512])
        p2 = ctx.ps("p2E", [128, 512])
        pss = ctx.ps("pssE", [1, 512])
        psq = ctx.ps("psqE", [1, 512])
        blk = st.enter_context(nc.Block())

        s1 = ExitStack()
        ctx.st = s1
        ao = ctx.sb("aoE", [128, NCH * W], BF16)
        x2m = ao
        t1 = ctx.sb("t1E", [128, NCH * W], F32)
        ow = ctx.sb("owE", [128, NCH * D], BF16)
        n2w = ctx.sb("n2wE", [128, 8], F32)
        x2c = [ctx.sb(f"x2cE{i}", [128, W], F32) for i in range(2)]
        ctx.dma("gpsimd", ao[:].rearrange("p (c t) -> p c t", c=NCH),
                ao_di.rearrange("(c p) t -> p c t", p=128), inc_sem="ldg")
        ctx.dma("gpsimd", ow[:].rearrange("p (c m) -> p c m", c=NCH),
                ow_di.rearrange("(c p) m -> p c m", p=128), inc_sem="ldg")
        ctx.dma("sync", n2w[:], n2_di, inc_sem="ld")
        ctx.wait("tensor", "ldg", 32)
        mm1 = MM(ctx, "mmE1", n_banks=2)
        mm1.run(NCH, NCH, NT, 512,
                lambda kc, mc: ow[:, kc * D + mc * 128: kc * D + (mc + 1) * 128],
                lambda kc, nt: ao[:, kc * W + nt * 512: kc * W + nt * 512 + 512],
                lambda mc, nt: t1[:, mc * W + nt * 512: mc * W + nt * 512 + 512],
                func=AF.Copy)
        for c in range(NCH):
            par = c % 2
            if c >= 2:
                ctx.wait("sync", "t1dn", c - 1)
            ctx.dma("sync", x2c[par][:],
                    x2_di.rearrange("(c p) t -> p c t", p=128)[:, c, :], inc_sem="ldx2")
            ctx.wait("vector", "ldx2", (c + 1) * 16)
            ctx.wait("vector", "mmE1_ev", (c + 1) * NT)
            csl = slice(c * W, (c + 1) * W)
            ctx.step("vector", lambda E, csl=csl, par=par:
                     E.tensor_tensor(out=t1[:, csl], in0=t1[:, csl], in1=x2c[par][:],
                                     op=ALU.add),
                     inc_sem="t1dn")
            ctx.wait("sync", "t1dn", c + 1)
            ctx.dma("sync", t1_dd[:, csl], t1[:, csl], inc_sem="t1pk")
        for e in ("tensor", "scalar", "vector"):
            ctx.wait(e, "t1dn", NCH)
            ctx.wait(e, "ld", 16)
            ctx.wait(e, "ones_ms")
        emit_ln_stats(ctx, "lnE1", t1, NCH, W, mu, rstd, ones_c, eps=1e-6,
                      rms_only=True, ps_pair=(pss, psq))
        for e in ("tensor", "vector"):
            ctx.wait(e, "lnE1_fin")
        emit_normalize(ctx, "nrmE1", t1, x2m, NCH, W, mu, rstd, ones_r,
                       gain_col=lambda c: n2w[:, c:c + 1], rms_only=True,
                       ps_pair=(p1, p2))
        ctx.wait("sync", "nrmE1_done")
        ctx.dma("sync", x2m_dd, x2m[:], inc_sem="xmpk")
        for e in Ctx.ENGINES:
            ctx.wait(e, "t1pk", NCH * 16)
            ctx.wait(e, "xmpk", 16)
        s1.close()

        s23 = ExitStack()
        ctx.st = s23
        acc = ctx.sb("accE", [128, NCH * W], F32)
        s2 = ExitStack()
        ctx.st = s2
        x2m2 = ctx.sb("x2m2E", [128, NCH * W], BF16)
        ctx.dma("gpsimd", x2m2[:], x2m_dd, inc_sem="ldxm")
        ctx.wait("tensor", "ldxm", 16)
        usil = ctx.sb("usilE2", [128, 4 * W], BF16)
        upg = [ctx.sb(f"upgE2{i}", [128, NCH * 512], BF16) for i in range(2)]
        dng = [ctx.sb(f"dngE2{i}", [128, 4 * D], BF16) for i in range(2)]
        for c in range(NCH):
            ctx.step("vector", lambda E, c=c: E.memset(acc[:, c * W:(c + 1) * W], 0.0),
                     inc_sem="accz")
        mmu = MM(ctx, "mmU", n_banks=2)
        mmd = MM(ctx, "mmDn", n_banks=2)
        for g in range(NG):
            par = g % 2
            if g >= 2:
                ctx.wait("gpsimd", "mmU_pe", ctx.cnt("mmU_pe"))
                ctx.wait("gpsimd", "mmDn_pe", ctx.cnt("mmDn_pe"))
            ctx.dma("gpsimd", upg[par][:].rearrange("p (c m) -> p c m", c=NCH),
                    up_di.rearrange("(c p) m -> p c m", p=128)[:, :, g * 512:(g + 1) * 512],
                    inc_sem="ldu")
            ctx.dma("gpsimd", dng[par][:].rearrange("p (c m) -> p c m", c=4),
                    dn_di.rearrange("(c p) m -> p c m", p=128)[:, g * 4:(g + 1) * 4, :],
                    inc_sem="ldud")
            ctx.wait("tensor", "ldu", (g + 1) * 16)
            ctx.wait("scalar", "mmDn_pe", ctx.cnt("mmDn_pe"))
            mmu.run(NCH, 4, NT, 512,
                    lambda kc, mc, par=par: upg[par][:, kc * 512 + mc * 128: kc * 512 + (mc + 1) * 128],
                    lambda kc, nt: x2m2[:, kc * W + nt * 512: kc * W + nt * 512 + 512],
                    lambda mc, nt: usil[:, mc * W + nt * 512: mc * W + nt * 512 + 512],
                    func=AF.Silu)
            ctx.wait("tensor", "mmU_ev", ctx.cnt("mmU_ev"))
            ctx.wait("tensor", "ldud", (g + 1) * 16)
            ctx.wait("vector", "accz", NCH)
            for mc in range(NCH):
                for nt in range(NT):
                    b = mmd.rr % 2
                    mmd.rr += 1
                    bank = mmd.banks[b]
                    if mmd.uses[b] > 0:
                        ctx.wait("tensor", "mmDn_ev", mmd.uses[b])
                    for kc in range(4):
                        ctx.step("tensor",
                                 lambda E, bank=bank, kc=kc, mc=mc, nt=nt, par=par,
                                        s=(kc == 0), p=(kc == 3):
                                 E.matmul(bank[:, 0:512],
                                          dng[par][:, kc * D + mc * 128: kc * D + (mc + 1) * 128],
                                          usil[:, kc * W + nt * 512: kc * W + nt * 512 + 512],
                                          start=s, stop=p),
                                 inc_sem="mmDn_pe" if kc == 3 else None)
                    ctx.wait("vector", "mmDn_pe", ctx.cnt("mmDn_pe"))
                    ctx.step("vector",
                             lambda E, bank=bank, mc=mc, nt=nt:
                             E.tensor_tensor(out=acc[:, mc * W + nt * 512: mc * W + nt * 512 + 512],
                                             in0=acc[:, mc * W + nt * 512: mc * W + nt * 512 + 512],
                                             in1=bank[:, 0:512], op=ALU.add),
                             inc_sem="mmDn_ev")
                    mmd.uses[b] = ctx.cnt("mmDn_ev")
        for e in Ctx.ENGINES:
            ctx.wait(e, "mmDn_ev", ctx.cnt("mmDn_ev"))
        s2.close()

        s3 = ExitStack()
        ctx.st = s3
        x3 = ctx.sb("x3E", [128, NCH * W], BF16)
        t1c = [ctx.sb(f"t1cE3{i}", [128, W], F32) for i in range(2)]
        r1c = [ctx.sb(f"r1cE3{i}", [128, W], F32) for i in range(2)]
        tmp3 = ctx.sb("tmpE3", [128, 512], F32)
        for c in range(NCH):
            par = c % 2
            if c >= 2:
                ctx.wait("gpsimd", "t2dn", c - 1)
            ctx.dma("gpsimd", t1c[par][:], t1_dd[:, c * W:(c + 1) * W], inc_sem="ldm")
            ctx.wait("vector", "ldm", (c + 1) * 16)
            ctx.step("vector", lambda E, c=c, par=par:
                     E.tensor_tensor(out=acc[:, c * W:(c + 1) * W],
                                     in0=acc[:, c * W:(c + 1) * W], in1=t1c[par][:],
                                     op=ALU.add),
                     inc_sem="t2dn")
        for e in ("tensor", "scalar", "vector"):
            ctx.wait(e, "t2dn", NCH)
        emit_ln_stats(ctx, "lnE3a", acc, NCH, W, mu, rstd, ones_c,
                      ps_pair=(pss, psq))
        for e in ("tensor", "vector"):
            ctx.wait(e, "lnE3a_fin")
        sc2 = mu
        def prep1(E):
            E.tensor_tensor(out=sc2[:], in0=mu[:], in1=rstd[:], op=ALU.mult)
            return E.tensor_scalar(out=sc2[:], in0=sc2[:], scalar1=-1.0, scalar2=None,
                                   op0=ALU.mult)
        ctx.step("vector", prep1, inc_sem="prep1")
        ctx.wait("tensor", "prep1")
        ctx.wait("sync", "t2dn", NCH)
        it = 0
        for c in range(NCH):
            par = c % 2
            if c >= 2:
                ctx.wait("sync", "p1dn", (c - 1) * NT)
            ctx.dma("sync", r1c[par][:],
                    r1_di.rearrange("(c p) t -> p c t", p=128)[:, c, :], inc_sem="ldr")
            for t in range(NT):
                sl = slice(t * 512, (t + 1) * 512)
                if it > 0:
                    ctx.wait("tensor", "p1dn", it)
                ctx.step("tensor", lambda E, sl=sl:
                         E.matmul(p1[:, 0:512], ones_r[:], rstd[0:1, sl], start=True, stop=True),
                         inc_sem="bc1")
                ctx.step("tensor", lambda E, sl=sl:
                         E.matmul(p2[:, 0:512], ones_r[:], sc2[0:1, sl], start=True, stop=True),
                         inc_sem="bc1")
                ctx.wait("vector", "bc1")
                ctx.wait("vector", "ldr", (c + 1) * 16)
                csl = slice(c * W + t * 512, c * W + t * 512 + 512)
                def f(E, csl=csl, sl=sl, par=par):
                    E.tensor_tensor(out=tmp3[:], in0=acc[:, csl], in1=p1[:, 0:512], op=ALU.mult)
                    E.tensor_tensor(out=tmp3[:], in0=tmp3[:], in1=p2[:, 0:512], op=ALU.add)
                    return E.tensor_tensor(out=acc[:, csl], in0=tmp3[:],
                                           in1=r1c[par][:, sl], op=ALU.add)
                ctx.step("vector", f, inc_sem="p1dn")
                it += 1
            ctx.wait("sync", "p1dn", (c + 1) * NT)
            ctx.dma("sync", r2_do.rearrange("(c p) t -> p c t", p=128)[:, c, :],
                    acc[:, c * W:(c + 1) * W], inc_sem="str2")
        ctx.wait("scalar", "p1dn", NCH * NT)
        ctx.wait("tensor", "p1dn", NCH * NT)
        emit_ln_stats(ctx, "lnE3b", acc, NCH, W, mu, rstd, ones_c,
                      ps_pair=(pss, psq))
        for e in ("tensor", "vector"):
            ctx.wait(e, "lnE3b_fin")
        emit_normalize(ctx, "nrmE3", acc, x3, NCH, W, mu, rstd, ones_r,
                       ps_pair=(p1, p2))
        ctx.wait("sync", "nrmE3_done")
        ctx.dma("sync", x3_do.rearrange("(c p) t -> p c t", p=128),
                x3[:].rearrange("p (c t) -> p c t", c=NCH), inc_sem="stx3")
        ctx.dma("sync", x3_dd, x3[:], inc_sem="stx3")
        for e in Ctx.ENGINES:
            ctx.wait(e, "stx3", 32)
            ctx.wait(e, "str2", NCH * 16)
        s3.close()
        s23.close()

        s4 = ExitStack()
        ctx.st = s4
        x34 = ctx.sb("x34E", [128, NCH * W], BF16)
        ctx.dma("gpsimd", x34[:], x3_dd, inc_sem="ldx3")
        ctx.wait("tensor", "ldx3", 16)
        a_sb = ctx.sb("aE4", [128, NCH * W], BF16)
        val = ctx.sb("valE4", [128, NCH * W], BF16)
        wi = ctx.sb("wiE4", [128, NCH * 2 * D], BF16)
        ws = ctx.sb("wsE4", [128, NCH * D], BF16)
        bi = ctx.sb("biE4", [1, 2 * D], BF16)
        bs = ctx.sb("bsE4", [1, D], BF16)
        ppA = [ctx.sb(f"ppAE4{i}", [128, 512], BF16) for i in range(2)]
        ppB = [ctx.sb(f"ppBE4{i}", [128, 512], BF16) for i in range(2)]
        ldg4_0 = ctx.cnt("ldg")
        ctx.dma("gpsimd", wi[:].rearrange("p (c m) -> p c m", c=NCH),
                wi_di.rearrange("(c p) m -> p c m", p=128), inc_sem="ldg")
        ctx.dma("gpsimd", ws[:].rearrange("p (c m) -> p c m", c=NCH),
                ws_di.rearrange("(c p) m -> p c m", p=128), inc_sem="ldg")
        ctx.dma("gpsimd", bi[:], bi_di, inc_sem="ldg")
        ctx.dma("gpsimd", bs[:], bs_di, inc_sem="ldg")
        ctx.wait("tensor", "ldg", ldg4_0 + 64)
        mm4 = MM(ctx, "mmE4", n_banks=2)
        mm4.run(NCH, NCH, NT, 512,
                lambda kc, mc: wi[:, kc * 2 * D + mc * 128: kc * 2 * D + (mc + 1) * 128],
                lambda kc, nt: x34[:, kc * W + nt * 512: kc * W + nt * 512 + 512],
                lambda mc, nt: a_sb[:, mc * W + nt * 512: mc * W + nt * 512 + 512],
                func=AF.Sigmoid, scale=-1.0,
                bias_row=lambda mc: bi[:, mc * 128:(mc + 1) * 128],
                ones_row=ones_rb)
        mm4.run(NCH, NCH, NT, 512,
                lambda kc, mc: wi[:, kc * 2 * D + D + mc * 128: kc * 2 * D + D + (mc + 1) * 128],
                lambda kc, nt: x34[:, kc * W + nt * 512: kc * W + nt * 512 + 512],
                lambda mc, nt: val[:, mc * W + nt * 512: mc * W + nt * 512 + 512],
                func=AF.Copy,
                bias_row=lambda mc: bi[:, D + mc * 128: D + (mc + 1) * 128],
                ones_row=ones_rb)
        ctx.wait("sync", "mmE4_ev", ctx.cnt("mmE4_ev"))
        ctx.dma("sync", a_do.rearrange("(c p) t -> p c t", p=128),
                a_sb[:].rearrange("p (c t) -> p c t", c=NCH), inc_sem="sta")
        ctx.wait("tensor", "mmE4_ev", ctx.cnt("mmE4_ev"))
        ctx.wait("vector", "mmE4_ev", ctx.cnt("mmE4_ev"))
        i = 0
        for mc in range(NCH):
            for nt in range(NT):
                pk = [p1, p2][i % 2]
                pp = ppA[i % 2]
                ppo = ppB[i % 2]
                if i >= 2:
                    ctx.wait("tensor", "svev", (i - 1) * 2)
                for kc in range(NCH):
                    ctx.step("tensor",
                             lambda E, pk=pk, kc=kc, mc=mc, nt=nt, s=(kc == 0):
                             E.matmul(pk[:, 0:512],
                                      ws[:, kc * D + mc * 128: kc * D + (mc + 1) * 128],
                                      val[:, kc * W + nt * 512: kc * W + nt * 512 + 512],
                                      start=s, stop=False),
                             inc_sem=None)
                ctx.step("tensor",
                         lambda E, pk=pk, mc=mc:
                         E.matmul(pk[0:128, 0:512], bs[:, mc * 128:(mc + 1) * 128],
                                  ones_rb[:, 0:512], start=False, stop=True),
                         inc_sem="svpe")
                ctx.wait("scalar", "svpe", ctx.cnt("svpe"))
                if i >= 2:
                    ctx.wait("scalar", "svev", (i - 1) * 2)
                    ctx.wait("scalar", "stbb", (i - 1) * 16)
                ctx.step("scalar", lambda E, pk=pk, pp=pp:
                         E.activation(out=pp[:], in_=pk[:, 0:512], func=AF.Tanh),
                         inc_sem="svsc")
                ctx.wait("vector", "svsc", ctx.cnt("svsc"))
                if i >= 2:
                    ctx.wait("vector", "stbb", (i - 1) * 16)
                asl = slice(mc * W + nt * 512, mc * W + nt * 512 + 512)
                ctx.step("vector", lambda E, pp=pp, ppo=ppo, asl=asl:
                         E.tensor_tensor(out=ppo[:], in0=a_sb[:, asl], in1=pp[:],
                                         op=ALU.mult), inc_sem="svev")
                ctx.wait("vector", "svev", ctx.cnt("svev"))
                ctx.step("vector", lambda E, pp=pp, ppo=ppo:
                         E.tensor_tensor(out=ppo[:], in0=pp[:], in1=ppo[:],
                                         op=ALU.subtract), inc_sem="svev")
                ctx.wait("sync", "svev", ctx.cnt("svev"))
                ctx.dma("sync", bb_do.rearrange("(c p) t -> p c t", p=128)
                        [:, mc, nt * 512:(nt + 1) * 512], ppo[:], inc_sem="stbb")
                i += 1
        ctx.wait("sync", "stbb", NCH * NT * 16)
        ctx.wait("sync", "sta", 16)
        s4.close()
        ctx.st = st
        ctx.finalize(blk)
    return nc



def build_launch_F(T=T_FULL, DH=512):
    nc = bass.Bass("TRN2", target_bir_lowering=False)
    a_di = nc.dram_tensor("a1i", [DH, T], BF16, kind="ExternalInput").ap()
    b_di = nc.dram_tensor("bbi", [DH, T], BF16, kind="ExternalInput").ap()
    h_do = nc.dram_tensor("h1", [DH, T], BF16, kind="ExternalOutput").ap()
    NCH = DH // 128
    with ExitStack() as st:
        ctx = Ctx(nc, st)
        a_sb = ctx.sb("aF", [128, NCH * T], BF16)
        b_sb = ctx.sb("bF", [128, NCH * T], BF16)
        h_sb = ctx.sb("hF", [128, NCH * T], BF16)
        blk = st.enter_context(nc.Block())
        ctx.dma("gpsimd", a_sb[:].rearrange("p (c t) -> p c t", c=NCH),
                a_di.rearrange("(c p) t -> p c t", p=128), inc_sem="ldg")
        ctx.dma("gpsimd", b_sb[:].rearrange("p (c t) -> p c t", c=NCH),
                b_di.rearrange("(c p) t -> p c t", p=128), inc_sem="ldg")
        ctx.wait("vector", "ldg", 32)
        for c in range(NCH):
            csl = slice(c * T, (c + 1) * T)
            ctx.step("vector", lambda E, csl=csl:
                     E.tensor_tensor_scan(out=h_sb[:, csl], data0=a_sb[:, csl],
                                          data1=b_sb[:, csl], initial=0.0,
                                          op0=ALU.mult, op1=ALU.add),
                     inc_sem="scdn")
        ctx.wait("sync", "scdn", NCH)
        ctx.dma("sync", h_do.rearrange("(c p) t -> p c t", p=128),
                h_sb[:].rearrange("p (c t) -> p c t", c=NCH), inc_sem="st")
        ctx.wait("sync", "st", 16)
        ctx.finalize(blk)
    return nc



def build_launch_G(W=2048):
    nc = bass.Bass("TRN2", target_bir_lowering=False)
    h_di = nc.dram_tensor("h1i", [D, W], BF16, kind="ExternalInput").ap()
    x3_di = nc.dram_tensor("x3i", [D, W], BF16, kind="ExternalInput").ap()
    r2_di = nc.dram_tensor("r2i", [D, W], F32, kind="ExternalInput").ap()
    wo_di = nc.dram_tensor("w_oT", [D, D], F32, kind="ExternalInput").ap()
    bo_di = nc.dram_tensor("b_o", [1, D], F32, kind="ExternalInput").ap()
    gw_di = nc.dram_tensor("gate1", [D, 8], F32, kind="ExternalInput").ap()
    r3_do = nc.dram_tensor("r3", [D, W], F32, kind="ExternalOutput").ap()
    x4_do = nc.dram_tensor("x4", [D, W], BF16, kind="ExternalOutput").ap()
    gl_do = nc.dram_tensor("glog", [8, W], F32, kind="ExternalOutput").ap()

    NCH = D // 128
    NT = W // 512
    with ExitStack() as st:
        ctx = Ctx(nc, st)
        h_sb = ctx.sb("hG", [128, NCH * W], BF16)
        slot = ctx.sb("slotG", [128, NCH * W], F32)
        wo = ctx.sb("woG", [128, NCH * D], BF16)
        gw = ctx.sb("gwG", [128, NCH * 8], BF16)
        bo = ctx.sb("boG", [1, D], BF16)
        xc = [ctx.sb(f"xcG{i}", [128, W], F32) for i in range(2)]
        mu = ctx.sb("muG", [1, W], F32)
        rstd = ctx.sb("rstdG", [1, W], F32)
        gl8 = ctx.sb("gl8G", [8, W], F32)
        tmp = ctx.sb("tmpG", [128, 512], F32)
        p1 = ctx.ps("p1G", [128, 512])
        p2 = ctx.ps("p2G", [128, 512])
        pss = ctx.ps("pssG", [1, 512])
        psq = ctx.ps("psqG", [1, 512])
        blk = st.enter_context(nc.Block())

        ones_c = emit_ones(ctx, "onesc", [128, 1])
        ones_r = emit_ones(ctx, "onesr", [1, 128])
        ones_rb = ctx.sb("ones_rbG", [1, 512], BF16)
        ctx.step("gpsimd", lambda E: E.memset(ones_rb[:], 1.0), inc_sem="ones_ms")
        ctx.dma("gpsimd", h_sb[:].rearrange("p (c t) -> p c t", c=NCH),
                h_di.rearrange("(c p) t -> p c t", p=128), inc_sem="ldg")
        ctx.dma("gpsimd", wo[:].rearrange("p (c m) -> p c m", c=NCH),
                wo_di.rearrange("(c p) m -> p c m", p=128), inc_sem="ldg")
        ctx.dma("gpsimd", bo[:], bo_di, inc_sem="ldg")
        ctx.dma("gpsimd", gw[:].rearrange("p (c m) -> p c m", c=NCH),
                gw_di.rearrange("(c p) m -> p c m", p=128), inc_sem="ldg")
        ctx.wait("tensor", "ldg", 64)
        ctx.wait("tensor", "ones_ms")
        mm = MM(ctx, "mmG", n_banks=2)
        mm.run(NCH, NCH, NT, 512,
               lambda kc, mc: wo[:, kc * D + mc * 128: kc * D + (mc + 1) * 128],
               lambda kc, nt: h_sb[:, kc * W + nt * 512: kc * W + nt * 512 + 512],
               lambda mc, nt: slot[:, mc * W + nt * 512: mc * W + nt * 512 + 512],
               func=AF.Copy,
               bias_row=lambda mc: bo[:, mc * 128:(mc + 1) * 128],
               ones_row=ones_rb)
        for c in range(NCH):
            par = c % 2
            if c >= 2:
                ctx.wait("gpsimd", "sldn", c - 1)
            ctx.dma("gpsimd", xc[par][:],
                    x3_di.rearrange("(c p) t -> p c t", p=128)[:, c, :], inc_sem="ldx3")
            ctx.wait("vector", "ldx3", (c + 1) * 16)
            ctx.wait("vector", "mmG_ev", (c + 1) * NT)
            ctx.step("vector", lambda E, c=c, par=par:
                     E.tensor_tensor(out=slot[:, c * W:(c + 1) * W],
                                     in0=slot[:, c * W:(c + 1) * W], in1=xc[par][:],
                                     op=ALU.add),
                     inc_sem="sldn")
        for e in ("tensor", "scalar", "vector"):
            ctx.wait(e, "sldn", NCH)
        lnsc = (ctx.sb("lnscG", [128, 1024], F32), ctx.sb("lnsrG", [1, W], F32),
                ctx.sb("lnqrG", [1, W], F32))
        emit_ln_stats(ctx, "lnGa", slot, NCH, W, mu, rstd, ones_c, ps_pair=(pss, psq),
                      scratch=lnsc)
        for e in ("tensor", "vector"):
            ctx.wait(e, "lnGa_fin")
        sc2 = mu
        def prep1(E):
            E.tensor_tensor(out=sc2[:], in0=mu[:], in1=rstd[:], op=ALU.mult)
            return E.tensor_scalar(out=sc2[:], in0=sc2[:], scalar1=-1.0, scalar2=None,
                                   op0=ALU.mult)
        ctx.step("vector", prep1, inc_sem="prep1")
        ctx.wait("tensor", "prep1")
        ctx.wait("sync", "sldn", NCH)
        it = 0
        for c in range(NCH):
            par = c % 2
            if c >= 2:
                ctx.wait("sync", "p1dn", (c - 1) * NT)
            ctx.dma("sync", xc[par][:],
                    r2_di.rearrange("(c p) t -> p c t", p=128)[:, c, :], inc_sem="ldr")
            for t in range(NT):
                sl = slice(t * 512, (t + 1) * 512)
                if it > 0:
                    ctx.wait("tensor", "p1dn", it)
                ctx.step("tensor", lambda E, sl=sl:
                         E.matmul(p1[:, 0:512], ones_r[:], rstd[0:1, sl], start=True, stop=True),
                         inc_sem="bc1")
                ctx.step("tensor", lambda E, sl=sl:
                         E.matmul(p2[:, 0:512], ones_r[:], sc2[0:1, sl], start=True, stop=True),
                         inc_sem="bc1")
                ctx.wait("vector", "bc1")
                ctx.wait("vector", "ldr", (c + 1) * 16)
                csl = slice(c * W + t * 512, c * W + t * 512 + 512)
                def f(E, csl=csl, sl=sl, par=par):
                    E.tensor_tensor(out=tmp[:], in0=slot[:, csl], in1=p1[:, 0:512], op=ALU.mult)
                    E.tensor_tensor(out=tmp[:], in0=tmp[:], in1=p2[:, 0:512], op=ALU.add)
                    return E.tensor_tensor(out=slot[:, csl], in0=tmp[:],
                                           in1=xc[par][:, sl], op=ALU.add)
                ctx.step("vector", f, inc_sem="p1dn")
                it += 1
            ctx.wait("sync", "p1dn", (c + 1) * NT)
            ctx.dma("sync", r3_do.rearrange("(c p) t -> p c t", p=128)[:, c, :],
                    slot[:, c * W:(c + 1) * W], inc_sem="str3")
        ctx.wait("scalar", "p1dn", NCH * NT)
        ctx.wait("tensor", "p1dn", NCH * NT)
        emit_ln_stats(ctx, "lnGb", slot, NCH, W, mu, rstd, ones_c, ps_pair=(pss, psq),
                      scratch=lnsc)
        for e in ("tensor", "vector"):
            ctx.wait(e, "lnGb_fin")
        x4 = h_sb
        emit_normalize(ctx, "nrmG", slot, x4, NCH, W, mu, rstd, ones_r,
                       ps_pair=(p1, p2))
        ctx.wait("tensor", "nrmG_done")
        mm2 = MM(ctx, "mmG2", n_banks=1, bank_shape=(8, 512))
        mm2.run(NCH, 1, NT, 512,
                lambda kc, mc: gw[:, kc * 8:(kc + 1) * 8],
                lambda kc, nt: x4[:, kc * W + nt * 512: kc * W + nt * 512 + 512],
                lambda mc, nt: gl8[0:8, nt * 512:(nt + 1) * 512],
                func=AF.Copy)
        ctx.wait("sync", "nrmG_done")
        ctx.dma("sync", x4_do.rearrange("(c p) t -> p c t", p=128),
                x4[:].rearrange("p (c t) -> p c t", c=NCH), inc_sem="stx4")
        ctx.wait("sync", "mmG2_ev")
        ctx.dma("sync", gl_do, gl8[:], inc_sem="stx4")
        ctx.wait("sync", "stx4", 32)
        ctx.wait("sync", "str3", NCH * 16)
        ctx.finalize(blk)
    return nc



def build_launch_H(W=2048, T=T_FULL):
    nc = bass.Bass("TRN2", target_bir_lowering=False)
    gl_di = nc.dram_tensor("glogt", [128, 32 * 8], F32, kind="ExternalInput").ap()
    x4_di = nc.dram_tensor("x4h", [D, W], BF16, kind="ExternalInput").ap()
    r3_di = nc.dram_tensor("r3h", [D, W], F32, kind="ExternalInput").ap()
    ex_di = nc.dram_tensor("expertsT", [NE * D, D], F32, kind="ExternalInput").ap()
    id_di = nc.dram_tensor("idnH", [128, 128], F32, kind="ExternalInput").ap()
    out_do = nc.dram_tensor("outfm", [D, W], F32, kind="ExternalOutput").ap()

    NCH = D // 128
    NT = W // 512
    NC32 = 32
    with ExitStack() as st:
        ctx = Ctx(nc, st)
        x4 = ctx.sb("x4H", [128, NCH * W], BF16)
        xm = ctx.sb("xmH", [128, NCH * W], BF16)
        acc = ctx.sb("accH", [128, NCH * W], BF16)
        wex = ctx.sb("wexH", [128, NCH * D], BF16)
        G = ctx.sb("GH", [128, NC32 * 8], F32)
        se32 = ctx.sb("se32H", [128, NC32], F32)
        sf = ctx.sb("sfH", [8, W], BF16)
        sel8 = ctx.sb("sel8H", [8, 8 * 128], BF16)
        mem = ctx.sb("memH", [128, 16 * 8], F32)
        rnk = ctx.sb("rnkH", [128, 16 * 8], F32)
        hrd = ctx.sb("hrdH", [128, 16 * 8], F32)
        mx16 = ctx.sb("mxH", [128, 16], F32)
        erow = ctx.sb("erowH", [128, 8], F32)
        idn = ctx.sb("idnHs", [128, 128], F32)
        r3c = [ctx.sb("r3cH0", [128, W], F32)]
        mu = ctx.sb("muH", [1, W], F32)
        rstd = ctx.sb("rstdH", [1, W], F32)
        lnsc = (ctx.sb("lnscH", [128, 1024], F32), ctx.sb("lnsrH", [1, W], F32),
                ctx.sb("lnqrH", [1, W], F32))
        tmpb2 = [ctx.sb(f"tmpbH{i}", [128, 512], BF16) for i in range(2)]
        tmpf = ctx.sb("tmpfH", [128, 512], F32)
        tmpf2 = ctx.sb("tmpf2H", [128, 512], F32)
        ones_c = emit_ones(ctx, "onesc", [128, 1])
        ones_r = emit_ones(ctx, "onesr", [1, 128])
        onesb_c = ctx.sb("onesbH", [128, 1], BF16)
        psB = ctx.ps("psBH", [128, 512])
        psS = ctx.ps("psSH", [128, 512])
        p1 = ctx.ps("p1H", [128, 512])
        p2 = ctx.ps("p2H", [128, 512])
        blk = st.enter_context(nc.Block())

        ctx.step("gpsimd", lambda E: E.memset(onesb_c[:], 1.0), inc_sem="ms")
        ctx.step("gpsimd", lambda E: E.iota(erow[:], [[1, 8]], base=1,
                                            channel_multiplier=0,
                                            allow_small_or_imprecise_dtypes=True),
                 inc_sem="ms")
        ctx.step("gpsimd", lambda E: E.memset(sel8[:], 1.0), inc_sem="ms")
        sel8v = sel8[:].rearrange("p (n j) -> p n j", n=8)
        ctx.step("gpsimd", lambda E: E.affine_select(
            out=sel8v, in_=sel8v, pattern=[[1, 8], [0, 128]], base=0,
            channel_multiplier=-1, compare_op=ALU.is_equal, fill=0.0), inc_sem="ms")
        ctx.dma("sync", G[:], gl_di, inc_sem="ld")
        ctx.dma("sync", idn[:], id_di, inc_sem="ld")
        ctx.dma("gpsimd", x4[:].rearrange("p (c t) -> p c t", c=NCH),
                x4_di.rearrange("(c p) t -> p c t", p=128), inc_sem="ldg")

        Gv = G[:].rearrange("p (c e) -> p c e", e=8)
        sA = ctx.sb("sAH", [128, NC32 * 8], F32)
        onesq = ctx.sb("onesqH", [128, 128], F32)
        blo = ctx.sb("bloH", [128, 8], F32)
        bhi = ctx.sb("bhiH", [128, 8], F32)
        bmid = ctx.sb("bmidH", [128, 8], F32)
        bcnt = ctx.sb("bcntH", [128, 8], F32)

        def vop(fn):
            ctx.step("vector", fn, inc_sem="vf")
            ctx.wait("vector", "vf", ctx.cnt("vf"))

        ctx.step("gpsimd", lambda E: E.memset(onesq[:], 1.0), inc_sem="ms")
        ctx.wait("vector", "ld", 32)
        ctx.wait("vector", "ms", ctx.cnt("ms"))
        vop(lambda E: E.tensor_scalar_max(G[:], G[:], 1e-6))
        ctx.wait("tensor", "ms", ctx.cnt("ms"))

        def treesum(out):
            vop(lambda E: E.tensor_tensor(out=sA[:, 0:128], in0=sA[:, 0:128],
                                          in1=sA[:, 128:256], op=ALU.add))
            vop(lambda E: E.tensor_tensor(out=sA[:, 0:64], in0=sA[:, 0:64],
                                          in1=sA[:, 64:128], op=ALU.add))
            vop(lambda E: E.tensor_tensor(out=sA[:, 0:32], in0=sA[:, 0:32],
                                          in1=sA[:, 32:64], op=ALU.add))
            vop(lambda E: E.tensor_tensor(out=sA[:, 0:16], in0=sA[:, 0:16],
                                          in1=sA[:, 16:32], op=ALU.add))
            vop(lambda E, out=out: E.tensor_tensor(out=out, in0=sA[:, 0:8],
                                                   in1=sA[:, 8:16], op=ALU.add))

        for it in range(8):
            ctx.wait("tensor", "vf", ctx.cnt("vf"))
            ctx.step("tensor", lambda E:
                     E.matmul(psB[:, 0:256], onesq[:], G[:], start=True, stop=True),
                     inc_sem="skp")
            ctx.wait("vector", "skp", ctx.cnt("skp"))
            vop(lambda E: E.tensor_copy(out=sA[:], in_=psB[:, 0:256]))
            treesum(bcnt[:])
            vop(lambda E: E.reciprocal(out=bcnt[:], in_=bcnt[:]))
            vop(lambda E: E.tensor_tensor(
                out=Gv, in0=Gv,
                in1=bcnt[:].unsqueeze(1).broadcast_to([128, NC32, 8]),
                op=ALU.mult))
            vop(lambda E: E.tensor_reduce(out=se32[:], in_=Gv, axis=AX.X,
                                          op=ALU.add))
            vop(lambda E: E.reciprocal(out=se32[:], in_=se32[:]))
            vop(lambda E: E.tensor_tensor(
                out=Gv, in0=Gv,
                in1=se32[:].unsqueeze(2).broadcast_to([128, NC32, 8]),
                op=ALU.mult))

        def binit(E):
            E.memset(blo[:], 0.0)
            return E.memset(bhi[:], 1.0)
        vop(binit)
        for it in range(40):
            vop(lambda E: E.tensor_tensor(out=bmid[:], in0=blo[:], in1=bhi[:],
                                          op=ALU.add))
            vop(lambda E: E.tensor_scalar(out=bmid[:], in0=bmid[:], scalar1=0.5,
                                          scalar2=None, op0=ALU.mult))
            vop(lambda E: E.tensor_tensor(
                out=sA[:].rearrange("p (c e) -> p c e", e=8), in0=Gv,
                in1=bmid[:].unsqueeze(1).broadcast_to([128, NC32, 8]),
                op=ALU.is_ge))
            ctx.wait("tensor", "vf", ctx.cnt("vf"))
            ctx.step("tensor", lambda E:
                     E.matmul(psB[:, 0:256], onesq[:], sA[:], start=True, stop=True),
                     inc_sem="skp")
            ctx.wait("vector", "skp", ctx.cnt("skp"))
            vop(lambda E: E.tensor_copy(out=sA[:], in_=psB[:, 0:256]))
            treesum(bcnt[:])
            vop(lambda E: E.tensor_scalar(out=bcnt[:], in0=bcnt[:], scalar1=512.0,
                                          scalar2=None, op0=ALU.is_ge))
            vop(lambda E: E.tensor_tensor(out=sA[:, 0:8], in0=bmid[:], in1=blo[:],
                                          op=ALU.subtract))
            vop(lambda E: E.tensor_tensor(out=sA[:, 0:8], in0=sA[:, 0:8],
                                          in1=bcnt[:], op=ALU.mult))
            vop(lambda E: E.tensor_tensor(out=blo[:], in0=blo[:], in1=sA[:, 0:8],
                                          op=ALU.add))
            vop(lambda E: E.tensor_tensor(out=sA[:, 0:8], in0=bhi[:], in1=bmid[:],
                                          op=ALU.subtract))
            vop(lambda E: E.tensor_tensor(out=sA[:, 0:8], in0=sA[:, 0:8],
                                          in1=bcnt[:], op=ALU.mult))
            vop(lambda E: E.tensor_tensor(out=bhi[:], in0=bmid[:], in1=sA[:, 0:8],
                                          op=ALU.add))

        gv3 = Gv[:, 0:16, :]
        mv3 = mem[:].rearrange("p (c e) -> p c e", e=8)
        rv3 = rnk[:].rearrange("p (c e) -> p c e", e=8)
        vop(lambda E: E.tensor_tensor(
            out=mv3, in0=gv3,
            in1=blo[:].unsqueeze(1).broadcast_to([128, 16, 8]), op=ALU.is_ge))
        vop(lambda E: E.tensor_scalar(out=hrd[:], in0=G[:, 0:128], scalar1=0.5,
                                      scalar2=None, op0=ALU.is_gt))
        vop(lambda E: E.tensor_tensor(
            out=rv3, in0=mv3,
            in1=erow[:].unsqueeze(1).broadcast_to([128, 16, 8]), op=ALU.mult))
        vop(lambda E: E.tensor_reduce(out=mx16[:], in_=rv3, axis=AX.X, op=ALU.max))
        vop(lambda E: E.tensor_tensor(
            out=rv3, in0=rv3,
            in1=mx16[:].unsqueeze(2).broadcast_to([128, 16, 8]), op=ALU.is_ge))
        vop(lambda E: E.tensor_tensor(out=rnk[:], in0=rnk[:], in1=mem[:],
                                      op=ALU.mult))
        vop(lambda E: E.tensor_tensor(out=rnk[:], in0=rnk[:], in1=hrd[:],
                                      op=ALU.mult))
        ctx.step("vector", lambda E: E.nop(), inc_sem="seldn")
        ctx.wait("tensor", "seldn", 1)
        ctx.wait("tensor", "ld", 32)
        for cch in range(16):
            if cch > 0:
                ctx.wait("tensor", "sfv", cch)
            ctx.step("tensor", lambda E, cch=cch:
                     E.transpose(psB[0:8, 0:128], rnk[:, cch * 8:(cch + 1) * 8],
                                 idn[:]), inc_sem="sfp")
            ctx.wait("vector", "sfp")
            ctx.step("vector", lambda E, cch=cch:
                     E.tensor_copy(out=sf[0:8, cch * 128:(cch + 1) * 128],
                                   in_=psB[0:8, 0:128]), inc_sem="sfv")

        for c in range(NCH):
            ctx.step("vector", lambda E, c=c: E.memset(acc[:, c * W:(c + 1) * W], 0.0),
                     inc_sem="accz")
        ctx.wait("tensor", "ldg", 16)
        ctx.wait("vector", "ldg", 16)
        for e in range(NE):
            ctx.wait("gpsimd", "expe", ctx.cnt("expe"))
            ctx.dma("gpsimd", wex[:].rearrange("p (c m) -> p c m", c=NCH),
                    ex_di.rearrange("(c p) m -> p c m", p=128)[:, e * NCH:(e + 1) * NCH, :],
                    inc_sem="ldw")
            ctx.wait("vector", "expe", ctx.cnt("expe"))
            for t in range(NT):
                if ctx.cnt("xms") > 0:
                    ctx.wait("tensor", "xms", ctx.cnt("xms"))
                ctx.wait("tensor", "sfv", 16)
                ctx.step("tensor", lambda E, e=e, t=t:
                         E.matmul(psS[:, 0:512], sel8[0:8, e * 128:(e + 1) * 128],
                                  sf[0:8, t * 512:(t + 1) * 512], start=True, stop=True),
                         inc_sem="xsp")
                ctx.wait("vector", "xsp", ctx.cnt("xsp"))
                for c in range(NCH):
                    ctx.step("vector", lambda E, c=c, t=t:
                             E.tensor_tensor(out=xm[:, c * W + t * 512: c * W + t * 512 + 512],
                                             in0=x4[:, c * W + t * 512: c * W + t * 512 + 512],
                                             in1=psS[:, 0:512], op=ALU.mult),
                             inc_sem="xms" if c == NCH - 1 else None)
            ctx.wait("tensor", "ldw", (e + 1) * 16)
            ctx.wait("tensor", "xms", ctx.cnt("xms"))
            ctx.wait("vector", "accz", NCH)
            for mc in range(NCH):
                for nt in range(NT):
                    par2 = (mc * NT + nt) % 2
                    bank = [p1, p2][par2]
                    if ctx.cnt("exev") >= 2:
                        ctx.wait("tensor", "exev", ctx.cnt("exev") - 1)
                    for kc in range(NCH):
                        ctx.step("tensor",
                                 lambda E, bank=bank, kc=kc, mc=mc, nt=nt,
                                        s=(kc == 0), p=(kc == NCH - 1):
                                 E.matmul(bank[:, 0:512],
                                          wex[:, kc * D + mc * 128: kc * D + (mc + 1) * 128],
                                          xm[:, kc * W + nt * 512: kc * W + nt * 512 + 512],
                                          start=s, stop=p),
                                 inc_sem="expe" if kc == NCH - 1 else None)
                    tb = tmpb2[(ctx.cnt("exsc")) % 2]
                    ctx.wait("scalar", "expe", ctx.cnt("expe"))
                    if ctx.cnt("exev") >= 1:
                        ctx.wait("scalar", "exev", ctx.cnt("exev") - 1)
                    ctx.step("scalar", lambda E, bank=bank, tb=tb:
                             E.activation(out=tb[:], in_=bank[:, 0:512], func=AF.Copy),
                             inc_sem="exsc")
                    ctx.wait("vector", "exsc", ctx.cnt("exsc"))
                    ctx.step("vector", lambda E, mc=mc, nt=nt, tb=tb:
                             E.tensor_tensor(out=acc[:, mc * W + nt * 512: mc * W + nt * 512 + 512],
                                             in0=acc[:, mc * W + nt * 512: mc * W + nt * 512 + 512],
                                             in1=tb[:], op=ALU.add),
                             inc_sem="exev")

        for e in ("tensor", "scalar", "vector"):
            ctx.wait(e, "exev", ctx.cnt("exev"))
            ctx.wait(e, "ms", ctx.cnt("ms"))
            ctx.wait(e, "ones_ms")
        emit_ln_stats(ctx, "lnH", acc, NCH, W, mu, rstd, onesb_c, scratch=lnsc,
                      ones_sq=ones_c)
        for e in ("tensor", "vector"):
            ctx.wait(e, "lnH_fin")
        sc2 = mu
        def prep1(E):
            E.tensor_tensor(out=sc2[:], in0=mu[:], in1=rstd[:], op=ALU.mult)
            return E.tensor_scalar(out=sc2[:], in0=sc2[:], scalar1=-1.0, scalar2=None,
                                   op0=ALU.mult)
        ctx.step("vector", prep1, inc_sem="prep1")
        ctx.wait("tensor", "prep1")
        it = 0
        for c in range(NCH):
            par = 0
            if c >= 1:
                ctx.wait("sync", "p1dn", c * NT)
            ctx.dma("sync", r3c[par][:],
                    r3_di.rearrange("(c p) t -> p c t", p=128)[:, c, :], inc_sem="ldr")
            for t in range(NT):
                sl = slice(t * 512, (t + 1) * 512)
                tb = [tmpf, tmpf2][it % 2]
                if it > 0:
                    ctx.wait("tensor", "p1dn", it)
                ctx.step("tensor", lambda E, sl=sl:
                         E.matmul(p1[:, 0:512], ones_r[:], rstd[0:1, sl], start=True, stop=True),
                         inc_sem="bc1")
                ctx.step("tensor", lambda E, sl=sl:
                         E.matmul(p2[:, 0:512], ones_r[:], sc2[0:1, sl], start=True, stop=True),
                         inc_sem="bc1")
                ctx.wait("vector", "bc1")
                ctx.wait("vector", "ldr", (c + 1) * 16)
                if it >= 2:
                    ctx.wait("vector", "str4", (it - 1) * 16)
                csl = slice(c * W + t * 512, c * W + t * 512 + 512)
                def f(E, csl=csl, sl=sl, par=par, tb=tb):
                    E.tensor_tensor(out=tb[:], in0=acc[:, csl], in1=p1[:, 0:512], op=ALU.mult)
                    E.tensor_tensor(out=tb[:], in0=tb[:], in1=p2[:, 0:512], op=ALU.add)
                    return E.tensor_tensor(out=tb[:], in0=tb[:],
                                           in1=r3c[par][:, sl], op=ALU.add)
                ctx.step("vector", f, inc_sem="p1dn")
                ctx.wait("sync", "p1dn", it + 1)
                ctx.dma("sync", out_do.rearrange("(c p) t -> p c t", p=128)[:, c, sl],
                        tb[:], inc_sem="str4")
                it += 1
        ctx.wait("sync", "str4", NCH * NT * 16)
        ctx.finalize(blk)
    return nc



_BUILT = {}


def _get(key, builder, *a):
    if key not in _BUILT:
        _BUILT[key] = builder(*a)
    return _BUILT[key]


def run_A(inputs):
    x = np.ascontiguousarray(inputs["L0_attn_x"])
    xt = np.ascontiguousarray(x.reshape(BATCH * T_FULL, D).T)
    W = (BATCH * T_FULL) // 8
    wx = np.ascontiguousarray(np.asarray(inputs["L0_attn_W_xproj"], np.float32))
    bx = np.zeros((1, 1152), np.float32)
    bx[0, :1056] = np.asarray(inputs["L0_attn_b_xproj"], np.float32)
    nc = _get(("A", W), build_launch_A, W)
    in_maps = [dict(x=np.ascontiguousarray(xt[:, i * W:(i + 1) * W]),
                    w_xproj=wx, b_xproj=bx) for i in range(8)]
    res = run_bass_kernel_spmd(nc, in_maps, core_ids=list(range(8))).results
    cat = lambda k: np.concatenate([res[i][k] for i in range(8)], axis=1)
    return dict(delta=cat("delta"), u=cat("u"), x1=cat("x1"), bc=cat("bc"))


def run_B(inputs, A_out):
    delta = A_out["delta"]; u = A_out["u"]; bc = A_out["bc"]
    al = np.asarray(inputs["L0_attn_A_log"], np.float32)
    nc = _get(("B",), build_launch_B)
    in_maps = []
    for i in range(8):
        b, dh = i // 2, i % 2
        rows = slice(dh * 512, (dh + 1) * 512)
        cols = slice(b * T_FULL, (b + 1) * T_FULL)
        alh = np.ascontiguousarray(
            al[rows].reshape(4, 128, NST).transpose(1, 0, 2).reshape(128, 4 * NST))
        bc64 = np.zeros((64, T_FULL), np.float32)
        bc64[0:16] = bc[0:16, cols]
        bc64[32:48] = bc[16:32, cols]
        in_maps.append(dict(
            delta=np.ascontiguousarray(delta[rows, cols]),
            u=np.ascontiguousarray(u[rows, cols]),
            bc=bc64,
            a_log=alh))
    res = run_bass_kernel_spmd(nc, in_maps, core_ids=list(range(8))).results
    y = np.zeros((D, BATCH * T_FULL), np.float32)
    for i in range(8):
        b, dh = i // 2, i % 2
        y[dh * 512:(dh + 1) * 512, b * T_FULL:(b + 1) * T_FULL] = \
            res[i]["y"].astype(np.float32)
    return dict(y=y)


def run_C(inputs, A_out, B_out):
    x = np.ascontiguousarray(inputs["L0_attn_x"])
    xt = np.ascontiguousarray(x.reshape(BATCH * T_FULL, D).T)
    W = (BATCH * T_FULL) // 8
    wo = np.ascontiguousarray(np.asarray(inputs["L0_attn_W_out"], np.float32))
    bo = np.ascontiguousarray(np.asarray(inputs["L0_attn_b_out"], np.float32).reshape(1, D))
    dp = np.ascontiguousarray(np.asarray(inputs["L0_attn_Dp"], np.float32).reshape(8, 128).T)
    n1 = np.ascontiguousarray(np.asarray(inputs["L0_ffn_n1_w"], np.float32).reshape(8, 128).T)
    y = B_out["y"]
    x1 = A_out["x1"]
    nc = _get(("C", W), build_launch_C, W)
    in_maps = []
    for i in range(8):
        cols = slice(i * W, (i + 1) * W)
        in_maps.append(dict(
            y=np.ascontiguousarray(y[:, cols].astype(np.float32)).astype(
                np.dtype("bfloat16") if False else y.dtype) if False else _bf(y[:, cols]),
            x1=np.ascontiguousarray(x1[:, cols]),
            x=np.ascontiguousarray(xt[:, cols]),
            dp=dp, n1w=n1, w_out=wo, b_out=bo))
    res = run_bass_kernel_spmd(nc, in_maps, core_ids=list(range(8))).results
    cat = lambda k: np.concatenate([res[i][k] for i in range(8)], axis=1)
    return dict(r1=cat("r1"), x2=cat("x2"), x2n=cat("x2n"))


def _bf(a):
    import ml_dtypes
    return np.ascontiguousarray(np.asarray(a).astype(ml_dtypes.bfloat16))


def run_D(inputs, C_out):
    x2n = C_out["x2n"]
    qw = np.ascontiguousarray(np.asarray(inputs["L0_ffn_qkv_w"], np.float32).T)
    nc = _get(("D",), build_launch_D)
    in_maps = []
    for i in range(8):
        b, hg = i // 2, i % 2
        cols = slice(b * T_FULL, (b + 1) * T_FULL)
        wslice = np.concatenate([qw[:, hg * 512:(hg + 1) * 512],
                                 qw[:, 1024 + hg * 512: 1024 + (hg + 1) * 512],
                                 qw[:, 2048 + hg * 512: 2048 + (hg + 1) * 512]], axis=1)
        in_maps.append(dict(x2nf=np.ascontiguousarray(x2n[:, cols]),
                            wq_slice=np.ascontiguousarray(wslice)))
    res = run_bass_kernel_spmd(nc, in_maps, core_ids=list(range(8))).results
    ao = np.zeros((D, BATCH * T_FULL), np.float32)
    for i in range(8):
        b, hg = i // 2, i % 2
        ao[hg * 512:(hg + 1) * 512, b * T_FULL:(b + 1) * T_FULL] = \
            res[i]["ao"].astype(np.float32)
    return dict(ao=ao)


def run_E1(inputs, C_out, D_out):
    W = (BATCH * T_FULL) // 8
    ow = np.ascontiguousarray(np.asarray(inputs["L0_ffn_o_w"], np.float32).T)
    n2 = np.ascontiguousarray(np.asarray(inputs["L0_ffn_n2_w"], np.float32).reshape(8, 128).T)
    gw = np.ascontiguousarray(np.asarray(inputs["L0_ffn_gate_w"], np.float32).T)
    idf = np.eye(128, dtype=np.float32)
    ao = D_out["ao"]
    x2 = C_out["x2"]
    nc = _get(("E1", W), build_launch_E1, W)
    in_maps = []
    for i in range(8):
        cols = slice(i * W, (i + 1) * W)
        in_maps.append(dict(aoi=_bf(ao[:, cols]), x2i=np.ascontiguousarray(x2[:, cols]),
                            o_wT=ow, n2w=n2, gate0T=gw, idf=idf))
    res = run_bass_kernel_spmd(nc, in_maps, core_ids=list(range(8))).results
    cat = lambda k: np.concatenate([res[i][k] for i in range(8)], axis=1)
    fac = np.concatenate([res[i]["factor"].reshape(-1) for i in range(8)])
    return dict(t1=cat("t1"), x2m=cat("x2m"), factor=fac)


def run_E2(inputs, E1_out):
    W = (BATCH * T_FULL) // 8
    up = np.ascontiguousarray(np.asarray(inputs["L0_ffn_up_w"], np.float32).T)
    dn = np.ascontiguousarray(np.asarray(inputs["L0_ffn_down_w"], np.float32).T)
    x2m = E1_out["x2m"]
    fac = E1_out["factor"].reshape(8, W)
    nc = _get(("E2", W), build_launch_E2, W)
    in_maps = []
    for i in range(8):
        cols = slice(i * W, (i + 1) * W)
        in_maps.append(dict(x2mi=_bf(x2m[:, cols]), upT=up, downT=dn,
                            factori=np.ascontiguousarray(fac[i].reshape(1, W))))
    res = run_bass_kernel_spmd(nc, in_maps, core_ids=list(range(8))).results
    return dict(moe=np.concatenate([res[i]["moe"] for i in range(8)], axis=1))


def run_E3(inputs, C_out, E1_out, E2_out):
    W = (BATCH * T_FULL) // 8
    nc = _get(("E3", W), build_launch_E3, W)
    in_maps = []
    for i in range(8):
        cols = slice(i * W, (i + 1) * W)
        in_maps.append(dict(t1i=np.ascontiguousarray(E1_out["t1"][:, cols]),
                            moei=_bf(E2_out["moe"][:, cols]),
                            r1i=np.ascontiguousarray(C_out["r1"][:, cols])))
    res = run_bass_kernel_spmd(nc, in_maps, core_ids=list(range(8))).results
    cat = lambda k: np.concatenate([res[i][k] for i in range(8)], axis=1)
    return dict(r2=cat("r2"), x3=cat("x3"))


def run_E4(inputs, E3_out):
    W = (BATCH * T_FULL) // 8
    wi = np.ascontiguousarray(np.asarray(inputs["L1_attn_W_in"], np.float32).T)
    bi = np.ascontiguousarray(np.asarray(inputs["L1_attn_b_in"], np.float32).reshape(1, 2 * D))
    ws = np.ascontiguousarray(np.asarray(inputs["L1_attn_W_s"], np.float32).T)
    bs = np.ascontiguousarray(np.asarray(inputs["L1_attn_b_s"], np.float32).reshape(1, D))
    nc = _get(("E4", W), build_launch_E4, W)
    in_maps = []
    for i in range(8):
        cols = slice(i * W, (i + 1) * W)
        in_maps.append(dict(x3i=_bf(E3_out["x3"][:, cols]), w_inT=wi, b_in=bi,
                            w_sT=ws, b_s=bs))
    res = run_bass_kernel_spmd(nc, in_maps, core_ids=list(range(8))).results
    cat = lambda k: np.concatenate([res[i][k] for i in range(8)], axis=1)
    return dict(a1=cat("a1"), bb=cat("bb"))


def run_E(inputs, C_out, D_out):
    W = (BATCH * T_FULL) // 8
    ow = np.ascontiguousarray(np.asarray(inputs["L0_ffn_o_w"], np.float32).T)
    n2 = np.ascontiguousarray(np.asarray(inputs["L0_ffn_n2_w"], np.float32).reshape(8, 128).T)
    up = np.ascontiguousarray(np.asarray(inputs["L0_ffn_up_w"], np.float32).T)
    dn = np.ascontiguousarray(np.asarray(inputs["L0_ffn_down_w"], np.float32).T)
    wi = np.ascontiguousarray(np.asarray(inputs["L1_attn_W_in"], np.float32).T)
    bi = np.ascontiguousarray(np.asarray(inputs["L1_attn_b_in"], np.float32).reshape(1, 2 * D))
    ws = np.ascontiguousarray(np.asarray(inputs["L1_attn_W_s"], np.float32).T)
    bs = np.ascontiguousarray(np.asarray(inputs["L1_attn_b_s"], np.float32).reshape(1, D))
    ao = D_out["ao"]
    nc = _get(("E", W), build_launch_E, W)
    in_maps = []
    for i in range(8):
        cols = slice(i * W, (i + 1) * W)
        in_maps.append(dict(aoi=_bf(ao[:, cols]),
                            x2i=np.ascontiguousarray(C_out["x2"][:, cols]),
                            r1i=np.ascontiguousarray(C_out["r1"][:, cols]),
                            o_wT=ow, n2w=n2, upT=up, downT=dn,
                            w_inT=wi, b_in=bi, w_sT=ws, b_s=bs))
    res = run_bass_kernel_spmd(nc, in_maps, core_ids=list(range(8))).results
    cat = lambda k: np.concatenate([res[i][k] for i in range(8)], axis=1)
    return dict(r2=cat("r2"), x3=cat("x3"), a1=cat("a1"), bb=cat("bb"))


def run_F(inputs, E4_out):
    nc = _get(("F",), build_launch_F)
    in_maps = []
    for i in range(8):
        b, dh = i // 2, i % 2
        rows = slice(dh * 512, (dh + 1) * 512)
        cols = slice(b * T_FULL, (b + 1) * T_FULL)
        in_maps.append(dict(a1i=_bf(E4_out["a1"][rows, cols]),
                            bbi=_bf(E4_out["bb"][rows, cols])))
    res = run_bass_kernel_spmd(nc, in_maps, core_ids=list(range(8))).results
    h = np.zeros((D, BATCH * T_FULL), np.float32)
    for i in range(8):
        b, dh = i // 2, i % 2
        h[dh * 512:(dh + 1) * 512, b * T_FULL:(b + 1) * T_FULL] = \
            res[i]["h1"].astype(np.float32)
    return dict(h=h)


def run_G(inputs, E3_out, F_out):
    W = (BATCH * T_FULL) // 8
    wo = np.ascontiguousarray(np.asarray(inputs["L1_attn_W_o"], np.float32).T)
    bo = np.ascontiguousarray(np.asarray(inputs["L1_attn_b_o"], np.float32).reshape(1, D))
    gw = np.ascontiguousarray(np.asarray(inputs["L1_ffn_gate_w"], np.float32)[0])
    nc = _get(("G", W), build_launch_G, W)
    in_maps = []
    for i in range(8):
        cols = slice(i * W, (i + 1) * W)
        in_maps.append(dict(h1i=_bf(F_out["h"][:, cols]), x3i=_bf(E3_out["x3"][:, cols]),
                            r2i=np.ascontiguousarray(E3_out["r2"][:, cols]),
                            w_oT=wo, b_o=bo, gate1=gw))
    res = run_bass_kernel_spmd(nc, in_maps, core_ids=list(range(8))).results
    cat = lambda k: np.concatenate([res[i][k] for i in range(8)], axis=1)
    return dict(r3=cat("r3"), x4=cat("x4"), glog=cat("glog"))


def run_H(inputs, C_out=None, G_out=None):
    W = (BATCH * T_FULL) // 8
    ex = np.ascontiguousarray(np.asarray(inputs["L1_ffn_experts"], np.float32)
                              .reshape(NE * D, D))
    idn = np.eye(128, dtype=np.float32)
    glog = G_out["glog"]
    x4 = G_out["x4"]
    r3 = G_out["r3"]
    nc = _get(("H", W), build_launch_H, W)
    in_maps = []
    for i in range(8):
        b, sh = i // 2, i % 2
        own = slice(b * T_FULL + sh * W, b * T_FULL + (sh + 1) * W)
        oth = slice(b * T_FULL + (1 - sh) * W, b * T_FULL + (2 - sh) * W)
        glo = np.concatenate([glog[:, own], glog[:, oth]], axis=1)
        glt = np.ascontiguousarray(glo.T.reshape(32, 128, 8).transpose(1, 0, 2)
                                   .reshape(128, 32 * 8))
        in_maps.append(dict(glogt=glt,
                            x4h=_bf(x4[:, own]),
                            r3h=np.ascontiguousarray(r3[:, own]),
                            expertsT=ex, idnH=idn))
    res = run_bass_kernel_spmd(nc, in_maps, core_ids=list(range(8))).results
    out = np.zeros((D, BATCH * T_FULL), np.float32)
    for i in range(8):
        b, sh = i // 2, i % 2
        out[:, b * T_FULL + sh * W: b * T_FULL + (sh + 1) * W] = res[i]["outfm"]
    return dict(r4=out)


def kernel(**inputs):
    inputs = {k: np.asarray(v) for k, v in inputs.items()}
    a = run_A(inputs)
    b = run_B(inputs, a)
    c = run_C(inputs, a, b)
    d = run_D(inputs, c)
    e = run_E(inputs, c, d)
    f = run_F(inputs, e)
    g = run_G(inputs, e, f)
    h = run_H(inputs, None, g)
    r4 = h["r4"]
    return np.ascontiguousarray(r4.T).reshape(BATCH, T_FULL, D).astype(np.float32)


# version 05 (final)
# speedup vs baseline: 1.2146x; 1.2146x over previous
import numpy as np
from contextlib import ExitStack

import concourse.bass as bass
import concourse.mybir as mybir
from concourse.bass_utils import run_bass_kernel_spmd

F32 = mybir.dt.float32
BF16 = mybir.dt.bfloat16
I32 = mybir.dt.int32
AF = mybir.ActivationFunctionType
ALU = mybir.AluOpType
AX = mybir.AxisListType

D = 1024
T_FULL = 4096
BATCH = 4
NST = 16
NH = 16
HD = 64
ED = 4096
NE = 8



class Ctx:
    ENGINES = ("sync", "tensor", "vector", "scalar", "gpsimd")

    def __init__(self, nc, st):
        self.nc = nc
        self.st = st
        self.sem_st = st
        self.ops = {k: [] for k in self.ENGINES}
        self.sems = {}

    def sem(self, name):
        if name not in self.sems:
            h = self.sem_st.enter_context(self.nc.semaphore(name))
            self.sems[name] = [h, 0]
        return self.sems[name][0]

    def cnt(self, name):
        self.sem(name)
        return self.sems[name][1]

    def bump(self, name, inc):
        self.sem(name)
        self.sems[name][1] += inc
        return self.sems[name][1]

    def on(self, eng, fn):
        self.ops[eng].append(fn)

    def wait(self, eng, sem_name, value=None):
        h = self.sem(sem_name)
        v = self.cnt(sem_name) if value is None else value
        if v > 0:
            self.ops[eng].append(lambda E, h=h, v=v: E.wait_ge(h, v))

    def step(self, eng, fn, inc_sem=None, inc=1):
        if inc_sem is None:
            self.on(eng, fn)
            return None
        h = self.sem(inc_sem)
        def wrapped(E, fn=fn, h=h, inc=inc):
            r = fn(E)
            r.then_inc(h, inc)
            return r
        self.ops[eng].append(wrapped)
        return self.bump(inc_sem, inc)

    def dma(self, eng, out, in_, inc_sem=None, **kw):
        return self.step(eng, lambda E, out=out, in_=in_: E.dma_start(out=out, in_=in_, **kw),
                         inc_sem=inc_sem, inc=16)

    def finalize(self, block):
        for name in self.ENGINES:
            fns = self.ops[name]
            if not fns:
                continue
            def mk(fl):
                def run(E):
                    for f in fl:
                        f(E)
                return run
            getattr(block, name)(mk(fns))

    def sb(self, name, shape, dt):
        return self.st.enter_context(self.nc.sbuf_tensor(name, shape, dt))

    def ps(self, name, shape, dt=F32):
        return self.st.enter_context(self.nc.psum_tensor(name, shape, dt))


class MM:
    def __init__(self, ctx, tag, n_banks=2, bank_shape=(128, 512)):
        self.ctx = ctx
        self.tag = tag
        self.banks = [ctx.ps(f"{tag}_ps{i}", list(bank_shape)) for i in range(n_banks)]
        self.pe_sem = f"{tag}_pe"
        self.ev_sem = f"{tag}_ev"
        self.uses = [0] * n_banks
        self.rr = 0

    def run(self, kchunks, mchunks, ntiles, nw, lhsT_fn, rhs_fn, out_fn,
            func=AF.Copy, scale=1.0, bias=0.0, evac_eng="scalar",
            bias_row=None, ones_row=None):
        ctx = self.ctx
        nb = len(self.banks)
        for mc in range(mchunks):
            for nt in range(ntiles):
                b = self.rr % nb
                self.rr += 1
                bank = self.banks[b]
                if self.uses[b] > 0:
                    ctx.wait("tensor", self.ev_sem, self.uses[b])
                last_kc = kchunks - 1
                for kc in range(kchunks):
                    lt = lhsT_fn(kc, mc)
                    rt = rhs_fn(kc, nt)
                    mrows = lt.shape[-1]
                    nwid = rt.shape[-1]
                    is_last = (kc == last_kc and bias_row is None)
                    ctx.step("tensor",
                             lambda E, bank=bank, lt=lt, rt=rt, mrows=mrows, nwid=nwid,
                                    s=(kc == 0), p=(kc == last_kc):
                             E.matmul(bank[0:mrows, 0:nwid], lt, rt,
                                      start=s, stop=(p and bias_row is None)),
                             inc_sem=self.pe_sem if is_last else None)
                if bias_row is not None:
                    br = bias_row(mc)
                    nwid = rhs_fn(0, nt).shape[-1]
                    ctx.step("tensor",
                             lambda E, bank=bank, br=br, nwid=nwid:
                             E.matmul(bank[0:br.shape[-1], 0:nwid], br,
                                      ones_row[:, 0:nwid], start=False, stop=True),
                             inc_sem=self.pe_sem)
                pe_target = ctx.cnt(self.pe_sem)
                ctx.wait(evac_eng, self.pe_sem, pe_target)
                dst = out_fn(mc, nt)
                sc = scale(mc) if callable(scale) else scale
                bi = bias(mc) if callable(bias) else bias
                mrows = dst.shape[0]
                nwid = dst.shape[-1]
                if hasattr(bi, "shape"):
                    bi = bi[0:mrows, :]
                if hasattr(sc, "shape"):
                    sc = sc[0:mrows, :]
                ctx.step(evac_eng,
                         lambda E, bank=bank, dst=dst, sc=sc, bi=bi, mrows=mrows, nwid=nwid:
                         E.activation(out=dst, in_=bank[0:mrows, 0:nwid], func=func,
                                      scale=sc, bias=bi),
                         inc_sem=self.ev_sem)
                self.uses[b] = ctx.cnt(self.ev_sem)



def emit_ones(ctx, tag, shape, dt=F32, value=1.0):
    t = ctx.sb(tag, shape, dt)
    ctx.step("gpsimd", lambda E, t=t, v=value: E.memset(t[:], v), inc_sem="ones_ms")
    return t


def emit_ln_stats(ctx, tag, x_sb, nchunk, width, mu_row, rstd_row, ones_col,
                  eps=1e-5, rms_only=False, var_row=None, ps_pair=None,
                  scratch=None, ones_sq=None):
    if ps_pair is None:
        ps_s = ctx.ps(f"{tag}_pss", [1, 512])
        ps_q = ctx.ps(f"{tag}_psq", [1, 512])
    else:
        ps_s, ps_q = ps_pair
    if scratch is None:
        sq2 = ctx.sb(f"{tag}_sq2", [128, 2 * 512], F32)
        srow = ctx.sb(f"{tag}_srow", [1, width], F32)
        qrow = ctx.sb(f"{tag}_qrow", [1, width], F32)
    else:
        sq2, srow, qrow = scratch
    sqd = f"{tag}_sqd"
    qrd = f"{tag}_qrd"
    evs = f"{tag}_ev"
    nt = width // 512
    idx = 0
    for t in range(nt):
        sl = slice(t * 512, (t + 1) * 512)
        if t > 0:
            ctx.wait("tensor", evs, 2 * t if not rms_only else t)
        for c in range(nchunk):
            slot = idx % 2
            ssl = slice(slot * 512, slot * 512 + 512)
            if idx >= 2:
                ctx.wait("scalar", qrd, idx - 1)
            ctx.step("scalar",
                     lambda E, c=c, sl=sl, ssl=ssl:
                     E.activation(out=sq2[:, ssl], in_=x_sb[:, c * width:(c + 1) * width][:, sl],
                                  func=AF.Square),
                     inc_sem=sqd)
            if not rms_only:
                ctx.step("tensor",
                         lambda E, c=c, sl=sl, s=(c == 0), p=(c == nchunk - 1):
                         E.matmul(ps_s[:, 0:512], ones_col[:], x_sb[:, c * width:(c + 1) * width][:, sl],
                                  start=s, stop=p),
                         inc_sem=None)
            ctx.wait("tensor", sqd, idx + 1)
            osq = ones_sq if ones_sq is not None else ones_col
            ctx.step("tensor",
                     lambda E, ssl=ssl, osq=osq, s=(c == 0), p=(c == nchunk - 1):
                     E.matmul(ps_q[:, 0:512], osq[:], sq2[:, ssl], start=s, stop=p),
                     inc_sem=qrd)
            idx += 1
        ctx.wait("vector", qrd, idx)
        ctx.step("vector", lambda E, sl=sl: E.tensor_copy(out=qrow[0:1, sl], in_=ps_q[:, 0:512]),
                 inc_sem=evs)
        if not rms_only:
            ctx.step("vector", lambda E, sl=sl: E.tensor_copy(out=srow[0:1, sl], in_=ps_s[:, 0:512]),
                     inc_sem=evs)
    dinv = 1.0 / (nchunk * 128)
    def fin(E):
        if rms_only:
            E.tensor_scalar(out=qrow[:], in0=qrow[:], scalar1=dinv, scalar2=eps,
                            op0=ALU.mult, op1=ALU.add)
            E.memset(mu_row[:], 0.0)
        else:
            E.tensor_scalar(out=mu_row[:], in0=srow[:], scalar1=dinv, scalar2=None,
                            op0=ALU.mult)
            E.tensor_scalar(out=qrow[:], in0=qrow[:], scalar1=dinv, scalar2=None,
                            op0=ALU.mult)
            E.tensor_tensor(out=srow[:], in0=mu_row[:], in1=mu_row[:], op=ALU.mult)
            E.tensor_sub(out=qrow[:], in0=qrow[:], in1=srow[:])
            if var_row is not None:
                E.tensor_copy(out=var_row[:], in_=qrow[:])
            E.tensor_scalar(out=qrow[:], in0=qrow[:], scalar1=1.0, scalar2=eps,
                            op0=ALU.mult, op1=ALU.add)
        return E.reciprocal(out=srow[:], in_=qrow[:])
    ctx.step("vector", fin, inc_sem=evs)
    ctx.wait("scalar", evs)
    ctx.step("scalar", lambda E: E.activation(out=rstd_row[:], in_=srow[:], func=AF.Sqrt),
             inc_sem=f"{tag}_fin")


def emit_normalize(ctx, tag, x_sb, out_sb, nchunk, width, mu_row, rstd_row,
                   ones_col1, gain_col=None, out2_sb=None, rms_only=False,
                   ps_pair=None):
    if ps_pair is None:
        p1 = ctx.ps(f"{tag}_p1", [128, 512])
        p2 = ctx.ps(f"{tag}_p2", [128, 512])
    else:
        p1, p2 = ps_pair
    mrow = ctx.sb(f"{tag}_mrow", [1, width], F32)
    tmp = ctx.sb(f"{tag}_tmp", [128, 512], F32)
    if not rms_only:
        def prep(E):
            E.tensor_tensor(out=mrow[:], in0=mu_row[:], in1=rstd_row[:], op=ALU.mult)
            return E.tensor_scalar(out=mrow[:], in0=mrow[:], scalar1=-1.0, scalar2=None,
                                   op0=ALU.mult)
        ctx.step("vector", prep, inc_sem=f"{tag}_prep")
        ctx.wait("tensor", f"{tag}_prep")
    nt = width // 512
    bc = f"{tag}_bc"
    dn = f"{tag}_done"
    for t in range(nt):
        sl = slice(t * 512, (t + 1) * 512)
        if t > 0:
            ctx.wait("tensor", dn, t * nchunk)
        ctx.step("tensor", lambda E, sl=sl:
                 E.matmul(p1[:, 0:512], ones_col1[:], rstd_row[0:1, sl], start=True, stop=True),
                 inc_sem=bc)
        if not rms_only:
            ctx.step("tensor", lambda E, sl=sl:
                     E.matmul(p2[:, 0:512], ones_col1[:], mrow[0:1, sl], start=True, stop=True),
                     inc_sem=bc)
        ctx.wait("vector", bc)
        for c in range(nchunk):
            csl = slice(c * width + t * 512, c * width + t * 512 + 512)
            def f(E, csl=csl, c=c, rms_only=rms_only):
                E.tensor_tensor(out=tmp[:], in0=x_sb[:, csl], in1=p1[:, 0:512], op=ALU.mult)
                if not rms_only:
                    E.tensor_tensor(out=tmp[:], in0=tmp[:], in1=p2[:, 0:512], op=ALU.add)
                if gain_col is not None:
                    E.tensor_scalar(out=tmp[:], in0=tmp[:], scalar1=gain_col(c),
                                    scalar2=None, op0=ALU.mult)
                last = E.tensor_copy(out=out_sb[:, csl], in_=tmp[:])
                if out2_sb is not None:
                    last = E.tensor_copy(out=out2_sb[:, csl], in_=tmp[:])
                return last
            ctx.step("vector", f, inc_sem=dn)



def build_launch_A(W=2048):
    nc = bass.Bass("TRN2", target_bir_lowering=False)
    x_d = nc.dram_tensor("x", [D, W], F32, kind="ExternalInput").ap()
    wx_d = nc.dram_tensor("w_xproj", [D, 1056], F32, kind="ExternalInput").ap()
    bx_d = nc.dram_tensor("b_xproj", [1, 1152], F32, kind="ExternalInput").ap()
    delta_d = nc.dram_tensor("delta", [D, W], F32, kind="ExternalOutput").ap()
    u_d = nc.dram_tensor("u", [D, W], F32, kind="ExternalOutput").ap()
    x1_d = nc.dram_tensor("x1", [D, W], BF16, kind="ExternalOutput").ap()
    bc_d = nc.dram_tensor("bc", [32, W], F32, kind="ExternalOutput").ap()

    NCH = D // 128
    with ExitStack() as st:
        ctx = Ctx(nc, st)
        x_sb = ctx.sb("x_sb", [128, NCH * W], F32)
        x1bf = ctx.sb("x1bf", [128, NCH * W], BF16)
        wx = ctx.sb("wx", [128, NCH * 1056], BF16)
        bxc = ctx.sb("bxc", [1, 1152], BF16)
        ones_rb = ctx.sb("ones_rb", [1, 512], BF16)
        mu = ctx.sb("mu", [1, W], F32)
        rstd = ctx.sb("rstd", [1, W], F32)
        dl2 = [ctx.sb("dl0", [128, W], F32)]
        u2 = [ctx.sb("u0", [128, W], F32)]
        bc_sb = ctx.sb("bc_sb", [32, W], F32)
        sp_abs = ctx.sb("sp_abs", [128, W], F32)
        blk = st.enter_context(nc.Block())

        ones_c = emit_ones(ctx, "onesc", [128, 1])
        ones_r = emit_ones(ctx, "onesr", [1, 128])
        ctx.dma("sync", x_sb[:].rearrange("p (c t) -> p c t", c=NCH),
                x_d.rearrange("(c p) t -> p c t", p=128), inc_sem="ld")
        ctx.dma("gpsimd", wx[:].rearrange("p (c m) -> p c m", c=NCH),
                wx_d.rearrange("(c p) m -> p c m", p=128), inc_sem="ldw")
        ctx.dma("gpsimd", bxc[:], bx_d, inc_sem="ldw")
        ctx.step("gpsimd", lambda E: E.memset(ones_rb[:], 1.0), inc_sem="ones_ms")

        for e in ("tensor", "scalar", "vector"):
            ctx.wait(e, "ld", 16)
            ctx.wait(e, "ones_ms")
        emit_ln_stats(ctx, "lnA", x_sb, NCH, W, mu, rstd, ones_c)
        for e in ("tensor", "vector"):
            ctx.wait(e, "lnA_fin")
        emit_normalize(ctx, "nrmA", x_sb, x1bf, NCH, W, mu, rstd, ones_r,
                       out2_sb=x_sb)

        ctx.wait("tensor", "nrmA_done")
        ctx.wait("tensor", "ldw", 16)
        ctx.wait("tensor", "ldw", 32)
        ctx.wait("sync", "nrmA_done")
        ctx.dma("sync", x1_d.rearrange("(c p) t -> p c t", p=128),
                x1bf[:].rearrange("p (c t) -> p c t", c=NCH), inc_sem="stv")
        mm = MM(ctx, "mmA", n_banks=2)
        NT = W // 512

        def lhsT_mk(mc):
            def lhsT(kc, mc2):
                m0 = mc * 128
                rows = min(128, 1056 - m0)
                return wx[:, kc * 1056 + m0: kc * 1056 + m0 + rows]
            return lhsT

        def rhs(kc, nt):
            return x1bf[:, kc * W + nt * 512: kc * W + nt * 512 + 512]

        dv = x_d
        for mc in range(8):
            dbuf, ubuf = dl2[0], u2[0]
            if mc >= 1:
                ctx.wait("scalar", "stA", mc * 32)
            mm.run(NCH, 1, NT, 512, lhsT_mk(mc), rhs,
                   lambda m2, nt, dbuf=dbuf: dbuf[:, nt * 512:(nt + 1) * 512],
                   func=AF.Copy,
                   bias_row=lambda m2, mc=mc: bxc[:, mc * 128:(mc + 1) * 128],
                   ones_row=ones_rb)
            ctx.wait("scalar", "mmA_ev")
            ctx.wait("vector", "mmA_ev")
            if mc >= 1:
                ctx.wait("vector", "stA", mc * 32)
            if mc > 0:
                ctx.wait("scalar", "spb", mc)
            def sp1(E, dbuf=dbuf):
                E.activation(out=sp_abs[:], in_=dbuf[:], func=AF.Abs)
                E.activation(out=sp_abs[:], in_=sp_abs[:], func=AF.Exp, scale=-1.0)
                return E.activation(out=sp_abs[:], in_=sp_abs[:], func=AF.Ln, bias=1.0)
            ctx.step("scalar", sp1, inc_sem="spa")
            ctx.wait("vector", "spa")
            def sp2(E, dbuf=dbuf, ubuf=ubuf, csl=slice(mc * W, (mc + 1) * W)):
                E.tensor_scalar_max(ubuf[:], dbuf[:], 0.0)
                E.tensor_add(out=dbuf[:], in0=ubuf[:], in1=sp_abs[:])
                return E.tensor_tensor(out=ubuf[:], in0=dbuf[:],
                                       in1=x_sb[:, csl], op=ALU.mult)
            ctx.step("vector", sp2, inc_sem="spb")
            ctx.wait("sync", "spb")
            ctx.dma("sync", delta_d.rearrange("(c p) t -> p c t", p=128)[:, mc, :],
                    dbuf[:], inc_sem="stA")
            ctx.dma("sync", u_d.rearrange("(c p) t -> p c t", p=128)[:, mc, :],
                    ubuf[:], inc_sem="stA")
        mm.run(NCH, 1, NT, 512, lhsT_mk(8), rhs,
               lambda m2, nt: bc_sb[0:32, nt * 512:(nt + 1) * 512],
               func=AF.Copy,
               bias_row=lambda m2: bxc[:, 8 * 128: 8 * 128 + 32],
               ones_row=ones_rb)
        ctx.wait("sync", "mmA_ev")
        ctx.dma("sync", bc_d, bc_sb[:], inc_sem="stv")
        ctx.wait("sync", "stv")
        ctx.wait("sync", "stA")
        ctx.finalize(blk)
    return nc



def build_launch_B(T=T_FULL, DH=512):
    nc = bass.Bass("TRN2", target_bir_lowering=False)
    dl_d = nc.dram_tensor("delta", [DH, T], F32, kind="ExternalInput").ap()
    u_d = nc.dram_tensor("u", [DH, T], F32, kind="ExternalInput").ap()
    bc_d = nc.dram_tensor("bc", [64, T], F32, kind="ExternalInput").ap()
    al_d = nc.dram_tensor("a_log", [128, (DH // 128) * NST], F32, kind="ExternalInput").ap()
    y_d = nc.dram_tensor("y", [DH, T], BF16, kind="ExternalOutput").ap()

    NCH = DH // 128
    with ExitStack() as st:
        ctx = Ctx(nc, st)
        dbf = ctx.sb("dbf", [128, NCH * T], BF16)
        ubf = ctx.sb("ubf", [128, NCH * T], BF16)
        y_sb = ctx.sb("y_sb", [128, NCH * T], F32)
        a2 = [ctx.sb(f"a{i}", [128, T], BF16) for i in range(2)]
        b2 = [ctx.sb(f"b{i}", [128, T], BF16) for i in range(2)]
        h_sb = ctx.sb("h_sb", [128, T], BF16)
        hc = ctx.sb("hc", [128, T // 2], F32)
        bmbc = [ctx.sb("bmbc0", [128, T], BF16)]
        cmbc = [ctx.sb("cmbc0", [128, T], BF16)]
        bc_sb = ctx.sb("bc_sb", [64, T], BF16)
        asb = ctx.sb("asb", [128, NCH * NST], F32)
        sel = ctx.sb("sel", [48, NST * 128], BF16)
        blk = st.enter_context(nc.Block())

        ctx.dma("gpsimd", dbf[:].rearrange("p (c t) -> p c t", c=NCH),
                dl_d.rearrange("(c p) t -> p c t", p=128), inc_sem="ldg")
        ctx.dma("gpsimd", ubf[:].rearrange("p (c t) -> p c t", c=NCH),
                u_d.rearrange("(c p) t -> p c t", p=128), inc_sem="ldg")
        ctx.dma("gpsimd", bc_sb[:], bc_d, inc_sem="ldg")
        ctx.dma("sync", asb[:], al_d, inc_sem="ld")
        ctx.step("gpsimd", lambda E: E.memset(sel[:], 1.0), inc_sem="selms")
        selv = sel[:].rearrange("p (n j) -> p n j", n=NST)
        ctx.step("gpsimd", lambda E: E.affine_select(
            out=selv[0:16], in_=selv[0:16], pattern=[[1, NST], [0, 128]],
            base=0, channel_multiplier=-1, compare_op=ALU.is_equal, fill=0.0),
            inc_sem="selms")
        ctx.step("gpsimd", lambda E: E.affine_select(
            out=selv[32:48], in_=selv[32:48], pattern=[[1, NST], [0, 128]],
            base=0, channel_multiplier=-1, compare_op=ALU.is_equal, fill=0.0),
            inc_sem="selms")
        ctx.wait("scalar", "ld", 16)
        ctx.step("scalar", lambda E: E.activation(out=asb[:], in_=asb[:], func=AF.Exp),
                 inc_sem="aexp")
        ctx.wait("vector", "aexp")
        ctx.step("vector", lambda E: E.tensor_scalar(out=asb[:], in0=asb[:], scalar1=-1.0,
                                                     scalar2=None, op0=ALU.mult),
                 inc_sem="aneg")
        ctx.wait("scalar", "aneg")
        ctx.wait("tensor", "selms")
        ctx.wait("tensor", "ld", 16)
        ctx.wait("tensor", "ldg", 48)
        ctx.wait("scalar", "ldg", 48)
        ctx.wait("vector", "ldg", 48)
        for c in range(NCH):
            ctx.step("vector", lambda E, c=c: E.memset(y_sb[:, c * T:(c + 1) * T], 0.0),
                     inc_sem="yz")

        mm = MM(ctx, "mmB", n_banks=2)
        NTT = T // 512
        bm_ready = [0, 0]
        for n in range(NST):
            par_n = 0
            if n >= 1:
                ctx.wait("scalar", "dvit", n * NCH)
            mm.run(1, 1, NTT, 512,
                   lambda kc, mc, n=n: sel[0:16, n * 128:(n + 1) * 128],
                   lambda kc, nt: bc_sb[0:16, nt * 512:(nt + 1) * 512],
                   lambda mc, nt, par_n=par_n: bmbc[par_n][:, nt * 512:(nt + 1) * 512],
                   func=AF.Copy)
            mm.run(1, 1, NTT, 512,
                   lambda kc, mc, n=n: sel[32:48, n * 128:(n + 1) * 128],
                   lambda kc, nt: bc_sb[32:48, nt * 512:(nt + 1) * 512],
                   lambda mc, nt, par_n=par_n: cmbc[par_n][:, nt * 512:(nt + 1) * 512],
                   func=AF.Copy)
            bm_ready[par_n] = ctx.cnt("mmB_ev")
            for c in range(NCH):
                idx = n * NCH + c
                par = idx % 2
                csl = slice(c * T, (c + 1) * T)
                if idx >= 2:
                    ctx.wait("scalar", "scend", idx - 1)
                ctx.step("scalar",
                         lambda E, par=par, csl=csl, c=c, n=n:
                         E.activation(out=a2[par][:], in_=dbf[:, csl], func=AF.Exp,
                                      scale=asb[:, c * NST + n:c * NST + n + 1]),
                         inc_sem="aex2")
                ctx.wait("vector", "mmB_ev", bm_ready[par_n])
                if idx >= 2:
                    ctx.wait("vector", "scend", idx - 1)
                ctx.step("vector",
                         lambda E, par=par, csl=csl, par_n=par_n:
                         E.tensor_tensor(out=b2[par][:], in0=ubf[:, csl],
                                         in1=bmbc[par_n][:], op=ALU.mult),
                         inc_sem=None)
                ctx.wait("vector", "aex2", idx + 1)
                ctx.step("vector",
                         lambda E, par=par:
                         E.tensor_tensor_scan(out=h_sb[:], data0=a2[par][:], data1=b2[par][:],
                                              initial=0.0, op0=ALU.mult, op1=ALU.add),
                         inc_sem="scend")
                for hh in range(2):
                    hsl = slice(hh * (T // 2), (hh + 1) * (T // 2))
                    ysl = slice(c * T + hh * (T // 2), c * T + (hh + 1) * (T // 2))
                    ctx.step("vector",
                             lambda E, par_n=par_n, hsl=hsl:
                             E.tensor_tensor(out=hc[:], in0=h_sb[:, hsl],
                                             in1=cmbc[par_n][:, hsl], op=ALU.mult),
                             inc_sem=None)
                    ctx.step("vector",
                             lambda E, ysl=ysl:
                             E.tensor_tensor(out=y_sb[:, ysl], in0=y_sb[:, ysl], in1=hc[:],
                                             op=ALU.add),
                             inc_sem="dvit" if hh == 1 else None)

        ctx.wait("gpsimd", "dvit", NST * NCH)
        ctx.dma("gpsimd", y_d.rearrange("(c p) t -> p c t", p=128),
                y_sb[:].rearrange("p (c t) -> p c t", c=NCH), inc_sem="stv")
        ctx.wait("gpsimd", "stv", 16)
        ctx.finalize(blk)
    return nc



def build_launch_C(W=2048):
    nc = bass.Bass("TRN2", target_bir_lowering=False)
    y_di = nc.dram_tensor("y", [D, W], BF16, kind="ExternalInput").ap()
    x1_di = nc.dram_tensor("x1", [D, W], BF16, kind="ExternalInput").ap()
    x_di = nc.dram_tensor("x", [D, W], F32, kind="ExternalInput").ap()
    dp_di = nc.dram_tensor("dp", [128, 8], F32, kind="ExternalInput").ap()
    n1_di = nc.dram_tensor("n1w", [128, 8], F32, kind="ExternalInput").ap()
    wo_di = nc.dram_tensor("w_out", [D, D], F32, kind="ExternalInput").ap()
    bo_di = nc.dram_tensor("b_out", [1, D], F32, kind="ExternalInput").ap()
    r1_do = nc.dram_tensor("r1", [D, W], F32, kind="ExternalOutput").ap()
    x2_do = nc.dram_tensor("x2", [D, W], F32, kind="ExternalOutput").ap()
    x2n_do = nc.dram_tensor("x2n", [D, W], BF16, kind="ExternalOutput").ap()

    NCH = D // 128
    NT = W // 512
    with ExitStack() as st:
        ctx = Ctx(nc, st)
        ybar = ctx.sb("ybar", [128, NCH * W], BF16)
        x2n = ybar
        m0 = ctx.sb("m0", [128, NCH * W], F32)
        wout = ctx.sb("wout", [128, NCH * D], BF16)
        bo = ctx.sb("bo", [1, 1024], BF16)
        dp = ctx.sb("dp_sb", [128, 8], F32)
        n1w = ctx.sb("n1w_sb", [128, 8], F32)
        x1c = [ctx.sb(f"x1c{i}", [128, W], BF16) for i in range(2)]
        xc = [ctx.sb(f"xc{i}", [128, W], F32) for i in range(2)]
        mu = ctx.sb("mu", [1, W], F32)
        rstd = ctx.sb("rstd", [1, W], F32)
        varw = ctx.sb("varw", [1, W], F32)
        sc2 = mu
        rmsr = varw
        tmp = ctx.sb("tmpC", [128, 512], F32)
        bc_m = ctx.sb("bcmC", [128, W], BF16)
        p1 = ctx.ps("p1", [128, 512])
        p2 = ctx.ps("p2", [128, 512])
        p3 = ctx.ps("p3", [128, 512])
        pss = ctx.ps("pss", [1, 512])
        psq = ctx.ps("psq", [1, 512])
        blk = st.enter_context(nc.Block())

        ones_c = emit_ones(ctx, "onesc", [128, 1])
        ones_r = emit_ones(ctx, "onesr", [1, 128])
        ones_rb = ctx.sb("ones_rb", [1, 512], BF16)
        ctx.step("gpsimd", lambda E: E.memset(ones_rb[:], 1.0), inc_sem="ones_ms")

        ctx.dma("gpsimd", ybar[:].rearrange("p (c t) -> p c t", c=NCH),
                y_di.rearrange("(c p) t -> p c t", p=128), inc_sem="ldg")
        ctx.dma("gpsimd", wout[:].rearrange("p (c m) -> p c m", c=NCH),
                wo_di.rearrange("(c p) m -> p c m", p=128), inc_sem="ldg")
        ctx.dma("gpsimd", bo[:], bo_di, inc_sem="ldg")
        ctx.dma("sync", dp[:], dp_di, inc_sem="ld")
        ctx.dma("sync", n1w[:], n1_di, inc_sem="ld")

        ctx.wait("vector", "ldg", 16)
        ctx.wait("vector", "ld", 32)
        for c in range(NCH):
            par = c % 2
            if c >= 2:
                ctx.wait("sync", "ybc", c - 1)
            ctx.dma("sync", x1c[par][:],
                    x1_di.rearrange("(c p) t -> p c t", p=128)[:, c, :], inc_sem="ldx1")
            ctx.wait("vector", "ldx1", (c + 1) * 16)
            csl = slice(c * W, (c + 1) * W)
            def f(E, par=par, csl=csl, c=c):
                E.tensor_scalar(out=x1c[par][:], in0=x1c[par][:], scalar1=dp[:, c:c + 1],
                                scalar2=None, op0=ALU.mult)
                return E.tensor_tensor(out=ybar[:, csl], in0=ybar[:, csl],
                                       in1=x1c[par][:], op=ALU.add)
            ctx.step("vector", f, inc_sem="ybc")

        ctx.wait("tensor", "ybc")
        ctx.wait("tensor", "ldg", 48)
        ctx.wait("tensor", "ones_ms")
        mm = MM(ctx, "mmC", n_banks=2)
        mm.run(NCH, NCH, NT, 512,
               lambda kc, mc: wout[:, kc * D + mc * 128: kc * D + (mc + 1) * 128],
               lambda kc, nt: ybar[:, kc * W + nt * 512: kc * W + nt * 512 + 512],
               lambda mc, nt: m0[:, mc * W + nt * 512: mc * W + nt * 512 + 512],
               func=AF.Copy,
               bias_row=lambda mc: bo[:, mc * 128:(mc + 1) * 128],
               ones_row=ones_rb)

        for e in ("tensor", "scalar", "vector"):
            ctx.wait(e, "mmC_ev")
        lnsc = (ctx.sb("lnsc_sq2", [128, 1024], F32), ctx.sb("lnsc_srow", [1, W], F32),
                ctx.sb("lnsc_qrow", [1, W], F32))
        emit_ln_stats(ctx, "lnC1", m0, NCH, W, mu, rstd, ones_c, ps_pair=(pss, psq), scratch=lnsc)
        for e in ("tensor", "vector"):
            ctx.wait(e, "lnC1_fin")

        def prep1(E):
            E.tensor_tensor(out=sc2[:], in0=mu[:], in1=rstd[:], op=ALU.mult)
            return E.tensor_scalar(out=sc2[:], in0=sc2[:], scalar1=-1.0, scalar2=None,
                                   op0=ALU.mult)
        ctx.step("vector", prep1, inc_sem="prep1")
        ctx.wait("tensor", "prep1")
        bc_r, bc_s = x1c[0], x1c[1]
        ctx.wait("scalar", "ybc", NCH)
        for t in range(NT):
            sl = slice(t * 512, (t + 1) * 512)
            if t > 0:
                ctx.wait("tensor", "bcev", ctx.cnt("bcev"))
            ctx.step("tensor", lambda E, sl=sl:
                     E.matmul(p1[:, 0:512], ones_r[:], rstd[0:1, sl], start=True, stop=True),
                     inc_sem="bcpe")
            ctx.step("tensor", lambda E, sl=sl:
                     E.matmul(p2[:, 0:512], ones_r[:], sc2[0:1, sl], start=True, stop=True),
                     inc_sem="bcpe")
            ctx.wait("scalar", "bcpe", ctx.cnt("bcpe"))
            ctx.step("scalar", lambda E, sl=sl:
                     E.activation(out=bc_r[:, sl], in_=p1[:, 0:512], func=AF.Copy),
                     inc_sem="bcev")
            ctx.step("scalar", lambda E, sl=sl:
                     E.activation(out=bc_s[:, sl], in_=p2[:, 0:512], func=AF.Copy),
                     inc_sem="bcev")
        ctx.wait("vector", "bcev", ctx.cnt("bcev"))
        it = 0
        for c in range(NCH):
            par = c % 2
            if c >= 2:
                ctx.wait("sync", "p1dn", (c - 1) * NT)
            ctx.dma("sync", xc[par][:],
                    x_di.rearrange("(c p) t -> p c t", p=128)[:, c, :], inc_sem="ldx")
            ctx.wait("vector", "ldx", (c + 1) * 16)
            for t in range(NT):
                sl = slice(t * 512, (t + 1) * 512)
                csl = slice(c * W + t * 512, c * W + t * 512 + 512)
                xsl = sl
                def f(E, csl=csl, xsl=xsl, par=par, sl=sl):
                    E.tensor_tensor(out=tmp[:], in0=m0[:, csl], in1=bc_r[:, sl], op=ALU.mult)
                    E.tensor_tensor(out=tmp[:], in0=tmp[:], in1=bc_s[:, sl], op=ALU.add)
                    return E.tensor_tensor(out=m0[:, csl], in0=tmp[:],
                                           in1=xc[par][:, xsl], op=ALU.add)
                ctx.step("vector", f, inc_sem="p1dn")
                it += 1
            ctx.wait("sync", "p1dn", (c + 1) * NT)
            ctx.dma("sync", r1_do.rearrange("(c p) t -> p c t", p=128)[:, c, :],
                    m0[:, c * W:(c + 1) * W], inc_sem="str1")

        ctx.wait("scalar", "p1dn", NCH * NT)
        ctx.wait("tensor", "p1dn", NCH * NT)
        emit_ln_stats(ctx, "lnC2", m0, NCH, W, mu, rstd, ones_c, var_row=varw, ps_pair=(pss, psq), scratch=lnsc)
        for e in ("tensor", "vector", "scalar"):
            ctx.wait(e, "lnC2_fin")
        rstd2_scratch = lnsc[1]
        def prep2(E):
            E.tensor_tensor(out=sc2[:], in0=mu[:], in1=rstd[:], op=ALU.mult)
            E.tensor_scalar(out=sc2[:], in0=sc2[:], scalar1=-1.0, scalar2=None,
                            op0=ALU.mult)
            E.tensor_scalar(out=tmp[0:1, 0:W] if False else rstd2_scratch[:],
                            in0=varw[:], scalar1=1.0, scalar2=1e-5,
                            op0=ALU.mult, op1=ALU.add)
            E.reciprocal(out=rstd2_scratch[:], in_=rstd2_scratch[:])
            E.tensor_tensor(out=rmsr[:], in0=varw[:], in1=rstd2_scratch[:], op=ALU.mult)
            E.tensor_scalar(out=rmsr[:], in0=rmsr[:], scalar1=1.0, scalar2=1e-6,
                            op0=ALU.mult, op1=ALU.add)
            return E.reciprocal(out=rmsr[:], in_=rmsr[:])
        ctx.step("vector", prep2, inc_sem="prep2")
        ctx.wait("scalar", "prep2")
        ctx.step("scalar", lambda E: E.activation(out=rmsr[:], in_=rmsr[:], func=AF.Sqrt),
                 inc_sem="prep2b")
        ctx.wait("tensor", "prep2b")
        ctx.wait("vector", "prep2b")

        ctx.wait("scalar", "p1dn", NCH * NT)
        for t in range(NT):
            sl = slice(t * 512, (t + 1) * 512)
            ctx.wait("tensor", "bcev", ctx.cnt("bcev"))
            ctx.step("tensor", lambda E, sl=sl:
                     E.matmul(p1[:, 0:512], ones_r[:], rstd[0:1, sl], start=True, stop=True),
                     inc_sem="bcpe")
            ctx.step("tensor", lambda E, sl=sl:
                     E.matmul(p2[:, 0:512], ones_r[:], sc2[0:1, sl], start=True, stop=True),
                     inc_sem="bcpe")
            ctx.step("tensor", lambda E, sl=sl:
                     E.matmul(p3[:, 0:512], ones_r[:], rmsr[0:1, sl], start=True, stop=True),
                     inc_sem="bcpe")
            ctx.wait("scalar", "bcpe", ctx.cnt("bcpe"))
            ctx.step("scalar", lambda E, sl=sl:
                     E.activation(out=bc_r[:, sl], in_=p1[:, 0:512], func=AF.Copy),
                     inc_sem="bcev")
            ctx.step("scalar", lambda E, sl=sl:
                     E.activation(out=bc_s[:, sl], in_=p2[:, 0:512], func=AF.Copy),
                     inc_sem="bcev")
            ctx.step("scalar", lambda E, sl=sl:
                     E.activation(out=bc_m[:, sl], in_=p3[:, 0:512], func=AF.Copy),
                     inc_sem="bcev")
        ctx.wait("vector", "bcev", ctx.cnt("bcev"))
        it = 0
        for c in range(NCH):
            par = c % 2
            if c >= 2:
                ctx.wait("sync", "p2dn", (c - 1) * NT)
                ctx.wait("vector", "stx2", (c - 1) * 16)
            for t in range(NT):
                sl = slice(t * 512, (t + 1) * 512)
                csl = slice(c * W + t * 512, c * W + t * 512 + 512)
                def f(E, csl=csl, par=par, c=c, tsl=sl):
                    E.tensor_tensor(out=tmp[:], in0=m0[:, csl], in1=bc_r[:, tsl], op=ALU.mult)
                    E.tensor_tensor(out=tmp[:], in0=tmp[:], in1=bc_s[:, tsl], op=ALU.add)
                    E.tensor_copy(out=xc[par][:, tsl], in_=tmp[:])
                    E.tensor_tensor(out=tmp[:], in0=tmp[:], in1=bc_m[:, tsl], op=ALU.mult)
                    return E.tensor_scalar(out=x2n[:, csl], in0=tmp[:],
                                           scalar1=n1w[:, c:c + 1], scalar2=None,
                                           op0=ALU.mult)
                ctx.step("vector", f, inc_sem="p2dn")
                it += 1
            ctx.wait("sync", "p2dn", (c + 1) * NT)
            ctx.dma("sync", x2_do.rearrange("(c p) t -> p c t", p=128)[:, c, :],
                    xc[par][:], inc_sem="stx2")

        ctx.wait("sync", "p2dn", NCH * NT)
        ctx.dma("sync", x2n_do.rearrange("(c p) t -> p c t", p=128),
                x2n[:].rearrange("p (c t) -> p c t", c=NCH), inc_sem="stx2")
        ctx.wait("sync", "str1", NCH * 16)
        ctx.wait("sync", "stx2", (NCH + 1) * 16)
        ctx.finalize(blk)
    return nc



def build_launch_D(T=T_FULL):
    nc = bass.Bass("TRN2", target_bir_lowering=False)
    x2n_di = nc.dram_tensor("x2nf", [D, T], BF16, kind="ExternalInput").ap()
    wq_di = nc.dram_tensor("wq_slice", [D, 1536], F32, kind="ExternalInput").ap()
    ao_do = nc.dram_tensor("ao", [512, T], BF16, kind="ExternalOutput").ap()
    dbg_do = nc.dram_tensor("dbgD", [2, 32], F32, kind="ExternalOutput").ap()

    NCH = D // 128
    NTT = T // 512
    NKC = T // 128
    with ExitStack() as st:
        ctx = Ctx(nc, st)
        x2n = ctx.sb("x2nD", [128, NCH * T], BF16)
        ao_sb = x2n[:].rearrange("p (c t) -> p c t", c=NCH)
        wq = ctx.sb("wqD", [128, NCH * 1536], BF16)
        qk = ctx.sb("qkD", [128, 8 * T], BF16)
        vt = ctx.sb("vtD", [128, NKC * 520], BF16)
        q2 = ctx.sb("q2D", [128, T], BF16)
        bones = ctx.sb("bonesD", [128, 2], BF16)
        stat = ctx.sb("statD", [2, 16], F32)
        mrow = ctx.sb("mrowD", [2, 8], F32)
        brow = ctx.sb("browD", [1, 8], F32)
        bias_sb = ctx.sb("biasD", [128, 1], F32)
        den = ctx.sb("denD", [1, 512], F32)
        aotmp = ctx.sb("aotmpD", [65, 512], BF16)
        denb = ctx.sb("denbD", [1, 512], BF16)
        rc64 = ctx.sb("rc64D", [64, 512], F32)
        ones1 = ctx.sb("ones1D", [1, 128], F32)
        onesb = ctx.sb("onesbD", [1, 128], BF16)
        dmask = ctx.sb("dmaskD", [128, 4 * 512], BF16)
        ps_s = [ctx.ps(f"ps_sD{i}", [128, 512]) for i in range(2)]
        ps_o = [ctx.ps(f"ps_oD{i}", [65, 512]) for i in range(2)]
        ps_n = ctx.ps("ps_nD", [64, 512])
        ps_b = ctx.ps("ps_bD", [128, 1])
        blk = st.enter_context(nc.Block())

        ao_v = x2n[:, 0:4 * T]
        pt_v = x2n[:, 4 * T:8 * T]

        ctx.dma("gpsimd", x2n[:].rearrange("p (c t) -> p c t", c=NCH),
                x2n_di.rearrange("(c p) t -> p c t", p=128), inc_sem="ldg")
        ctx.dma("gpsimd", wq[:].rearrange("p (c m) -> p c m", c=NCH),
                wq_di.rearrange("(c p) m -> p c m", p=128), inc_sem="ldg")
        ctx.step("gpsimd", lambda E: E.memset(bones[:], 0.0), inc_sem="ms")
        ctx.step("gpsimd", lambda E: E.memset(bones[0:64, 0:1], 1.0), inc_sem="ms")
        ctx.step("gpsimd", lambda E: E.memset(bones[64:128, 1:2], 1.0), inc_sem="ms")
        ctx.step("gpsimd", lambda E: E.memset(ones1[:], 1.0), inc_sem="ms")
        ctx.step("gpsimd", lambda E: E.memset(onesb[:], 1.0), inc_sem="ms")
        ctx.step("gpsimd", lambda E: E.memset(stat[:], 0.0), inc_sem="ms")
        ctx.step("gpsimd", lambda E: E.memset(
            vt[:].rearrange("p (k h e) -> p k h e", k=NKC, h=8)[:, :, :, 64:65], 1.0),
            inc_sem="ms")
        ctx.step("gpsimd", lambda E: E.memset(dmask[:], 1.0), inc_sem="ms")
        for off in range(4):
            ctx.step("gpsimd", lambda E, off=off: E.affine_select(
                out=dmask[:, off * 512:(off + 1) * 512],
                in_=dmask[:, off * 512:(off + 1) * 512],
                pattern=[[1, 512]], base=-off * 128,
                channel_multiplier=-1, compare_op=ALU.is_ge, fill=0.0),
                inc_sem="ms")

        ctx.wait("tensor", "ldg", 32)
        ctx.wait("tensor", "ms")
        mm = MM(ctx, "mmD", n_banks=2)
        for sec in range(2):
            mm.run(NCH, 4, NTT, 512,
                   lambda kc, mc, sec=sec: wq[:, kc * 1536 + sec * 512 + mc * 128:
                                               kc * 1536 + sec * 512 + (mc + 1) * 128],
                   lambda kc, nt: x2n[:, kc * T + nt * 512: kc * T + nt * 512 + 512],
                   lambda mc, nt, sec=sec: qk[:, (sec * 4 + mc) * T + nt * 512:
                                              (sec * 4 + mc) * T + nt * 512 + 512],
                   func=AF.Copy)
        for tc in range(NKC):
            b = mm.rr % 2
            mm.rr += 1
            bank = mm.banks[b]
            if mm.uses[b] > 0:
                ctx.wait("tensor", "mmD_ev", mm.uses[b])
            for kc in range(NCH):
                ctx.step("tensor",
                         lambda E, bank=bank, kc=kc, tc=tc, s=(kc == 0), p=(kc == NCH - 1):
                         E.matmul(bank[:, 0:512],
                                  x2n[:, kc * T + tc * 128: kc * T + (tc + 1) * 128],
                                  wq[:, kc * 1536 + 1024: kc * 1536 + 1536],
                                  start=s, stop=p),
                         inc_sem="mmD_pe" if kc == NCH - 1 else None)
            ctx.wait("scalar", "mmD_pe")
            ctx.step("scalar",
                     lambda E, bank=bank, tc=tc:
                     E.activation(out=vt[:, tc * 520:(tc + 1) * 520]
                                  .rearrange("p (h e) -> p h e", h=8)[:, :, 0:64],
                                  in_=bank[:, 0:512], func=AF.Copy),
                     inc_sem="mmD_ev")
            mm.uses[b] = ctx.cnt("mmD_ev")

        for c in range(8):
            if c >= 1:
                ctx.wait("scalar", "b_pe", c * NTT)
            ctx.wait("scalar", "mmD_ev", ctx.cnt("mmD_ev"))
            ctx.step("scalar", lambda E, c=c:
                     E.activation(out=q2[:], in_=qk[:, c * T:(c + 1) * T], func=AF.Square),
                     inc_sem="b_sq")
            for t in range(NTT):
                ctx.wait("tensor", "b_sq", c + 1)
                ctx.wait("tensor", "b_dv", 0 if (c == 0 and t == 0) else (c * NTT + t))
                ctx.step("tensor", lambda E, t=t:
                         E.matmul(ps_b[0:2, 0:1] if False else ps_s[0][0:2, 0:512],
                                  bones[:], q2[:, t * 512:(t + 1) * 512],
                                  start=True, stop=True),
                         inc_sem="b_pe" if t == NTT - 1 else "b_pe")
                ctx.wait("vector", "b_pe")
                def f(E, c=c, t=t):
                    E.tensor_reduce(out=mrow[0:2, 0:1], in_=ps_s[0][0:2, 0:512],
                                    axis=AX.X, op=ALU.max)
                    return E.tensor_tensor(out=stat[0:2, c:c + 1], in0=stat[0:2, c:c + 1],
                                           in1=mrow[0:2, 0:1], op=ALU.max)
                ctx.step("vector", f, inc_sem="b_dv")
        def bfin(E):
            E.tensor_tensor(out=mrow[0:2, 0:4], in0=stat[0:2, 0:4], in1=stat[0:2, 4:8],
                            op=ALU.add)
            return E.tensor_scalar(out=mrow[0:2, 0:4], in0=mrow[0:2, 0:4],
                                   scalar1=-1.0 / 16.0, scalar2=None, op0=ALU.mult)
        ctx.wait("vector", "b_dv", 64)
        ctx.step("vector", bfin, inc_sem="b_f3")
        ctx.wait("sync", "b_f3")
        for h in range(8):
            r, j = h % 2, h // 2
            ctx.dma("sync", brow[0:1, h:h + 1], mrow[r:r + 1, j:j + 1], inc_sem="b_dma")

        vall = ctx.cnt("mmD_ev")
        ctx.wait("scalar", "mmD_ev", vall)
        ctx.wait("scalar", "b_dma", 8 * 16)
        ctx.wait("vector", "ms", ctx.cnt("ms"))
        for h in range(8):
            r, j = h % 2, h // 2
            qbase = j * T
            kbase = (4 + j) * T
            if h == 0:
                ctx.wait("tensor", "b_dv", 64)
                ctx.wait("tensor", "b_dma", 8 * 16)
            else:
                ctx.wait("tensor", "bb_ev", h)
            ctx.step("tensor", lambda E, h=h:
                     E.matmul(ps_b[:, 0:1], ones1[:], brow[0:1, h:h + 1],
                              start=True, stop=True),
                     inc_sem="bb_pe")
            ctx.wait("scalar", "bb_pe")
            ctx.step("scalar", lambda E: E.activation(out=bias_sb[:], in_=ps_b[:, 0:1],
                                                      func=AF.Copy),
                     inc_sem="bb_ev")
            ctx.wait("scalar", "bb_ev")
            for tqt in range(NTT):
                kmax = 4 * (tqt + 1)
                tq_sl = slice(tqt * 512, (tqt + 1) * 512)
                sev_base = ctx.cnt("s_ev")
                mk_base = ctx.cnt("s_mk")
                for kc in range(kmax):
                    par = kc % 2
                    it = ctx.bump("s_it", 1)
                    if it > 2:
                        ctx.wait("tensor", "s_ev", it - 2)
                    ctx.step("tensor",
                             lambda E, par=par, kc=kc, r=r, kbase=kbase, qbase=qbase, tq_sl=tq_sl:
                             E.matmul(ps_s[par][:, 0:512],
                                      qk[r * 64:r * 64 + 64,
                                         kbase + kc * 128: kbase + (kc + 1) * 128],
                                      qk[r * 64:r * 64 + 64, qbase:qbase + T][:, tq_sl],
                                      start=True, stop=True),
                             inc_sem="s_pe")
                    ctx.wait("scalar", "s_pe")
                    need_mask = kc >= 4 * tqt
                    ctx.step("scalar",
                             lambda E, par=par, kc=kc:
                             E.activation(out=pt_v[:, kc * 512:(kc + 1) * 512],
                                          in_=ps_s[par][:, 0:512], func=AF.Exp,
                                          scale=0.125, bias=bias_sb[:]),
                             inc_sem="s_ev")
                    if need_mask:
                        off = kc - 4 * tqt
                        ctx.wait("vector", "s_ev")
                        ctx.step("vector",
                                 lambda E, kc=kc, off=off:
                                 E.tensor_tensor(out=pt_v[:, kc * 512:(kc + 1) * 512],
                                                 in0=pt_v[:, kc * 512:(kc + 1) * 512],
                                                 in1=dmask[:, off * 512:(off + 1) * 512],
                                                 op=ALU.mult),
                                 inc_sem="s_mk")
                parO = tqt % 2
                if ctx.cnt("o_done") >= 2:
                    ctx.wait("tensor", "o_done", ctx.cnt("o_done") - 1)
                for kc in range(kmax):
                    if kc >= 4 * tqt:
                        ctx.wait("tensor", "s_mk", mk_base + (kc - 4 * tqt) + 1)
                    else:
                        ctx.wait("tensor", "s_ev", sev_base + kc + 1)
                    ctx.step("tensor",
                             lambda E, parO=parO, kc=kc, h=h, s=(kc == 0), p=(kc == kmax - 1):
                             E.matmul(ps_o[parO][0:65, 0:512],
                                      vt[:, kc * 520 + h * 65: kc * 520 + (h + 1) * 65],
                                      pt_v[:, kc * 512:(kc + 1) * 512],
                                      start=s, stop=p),
                             inc_sem="o_pe" if kc == kmax - 1 else None)
                ctx.wait("scalar", "o_pe", ctx.cnt("o_pe"))
                ctx.wait("scalar", "o_done", ctx.cnt("o_done"))
                def evac(E, parO=parO):
                    E.activation(out=aotmp[0:64, :], in_=ps_o[parO][0:64, 0:512],
                                 func=AF.Copy)
                    return E.activation(out=denb[:], in_=ps_o[parO][64:65, 0:512],
                                        func=AF.Copy)
                ctx.step("scalar", evac, inc_sem="o_ev")
                ctx.wait("tensor", "o_ev", ctx.cnt("o_ev"))
                ctx.wait("tensor", "o_done", ctx.cnt("o_done"))
                ctx.step("tensor", lambda E:
                         E.matmul(ps_n[0:64, 0:512], onesb[0:1, 0:64],
                                  denb[:], start=True, stop=True),
                         inc_sem="n_pe")
                ctx.wait("vector", "n_pe", ctx.cnt("n_pe"))
                ctx.step("vector", lambda E:
                         E.reciprocal(out=rc64[:], in_=ps_n[0:64, 0:512]),
                         inc_sem="n_dv")
                ctx.wait("vector", "n_dv", ctx.cnt("n_dv"))
                ctx.step("vector",
                         lambda E, r=r, j=j, tq_sl=tq_sl:
                         E.tensor_tensor(out=ao_v[r * 64:r * 64 + 64, j * T:(j + 1) * T][:, tq_sl],
                                         in0=aotmp[0:64, 0:512], in1=rc64[:],
                                         op=ALU.mult),
                         inc_sem="o_done")

        ctx.wait("sync", "b_dma", 128)
        ctx.dma("sync", dbg_do[:, 0:16], stat[:], inc_sem="st")
        ctx.dma("sync", dbg_do[0:1, 16:24], brow[:], inc_sem="st")
        ctx.dma("sync", dbg_do[0:2, 24:32], mrow[:], inc_sem="st")
        ctx.wait("sync", "o_done", 64)
        ctx.dma("sync", ao_do.rearrange("(c p) t -> p c t", p=128),
                ao_v.rearrange("p (c t) -> p c t", c=4), inc_sem="st")
        ctx.wait("sync", "st", 64)
        ctx.finalize(blk)
    return nc



def build_launch_E1(W=2048):
    nc = bass.Bass("TRN2", target_bir_lowering=False)
    ao_di = nc.dram_tensor("aoi", [D, W], BF16, kind="ExternalInput").ap()
    x2_di = nc.dram_tensor("x2i", [D, W], F32, kind="ExternalInput").ap()
    ow_di = nc.dram_tensor("o_wT", [D, D], F32, kind="ExternalInput").ap()
    n2_di = nc.dram_tensor("n2w", [128, 8], F32, kind="ExternalInput").ap()
    gw_di = nc.dram_tensor("gate0T", [D, 8], F32, kind="ExternalInput").ap()
    idf_di = nc.dram_tensor("idf", [128, 128], F32, kind="ExternalInput").ap()
    t1_do = nc.dram_tensor("t1", [D, W], F32, kind="ExternalOutput").ap()
    x2m_do = nc.dram_tensor("x2m", [D, W], BF16, kind="ExternalOutput").ap()
    fb_do = nc.dram_tensor("factor", [16, 128], F32, kind="ExternalOutput").ap()

    NCH = D // 128
    NT = W // 512
    with ExitStack() as st:
        ctx = Ctx(nc, st)
        ao = ctx.sb("aoE", [128, NCH * W], BF16)
        x2m = ao
        t1 = ctx.sb("t1E", [128, NCH * W], F32)
        ow = ctx.sb("owE", [128, NCH * D], BF16)
        n2w = ctx.sb("n2wE", [128, 8], F32)
        gw = ctx.sb("gwE", [128, NCH * 8], BF16)
        idf = ctx.sb("idfE", [128, 128], F32)
        x2c = [ctx.sb(f"x2cE{i}", [128, W], F32) for i in range(2)]
        ex8 = ctx.sb("ex8E", [8, W], F32)
        ext = ctx.sb("extE", [128, 16 * 8], F32)
        ftok = ctx.sb("ftokE", [128, 16], F32)
        ftr = ctx.sb("ftrE", [16, 128], F32)
        ex2 = ctx.sb("ex2E", [128, 128], F32)
        ex2m = ctx.sb("ex2mE", [128, 16], F32)
        scr = ctx.sb("scrE", [128, 16], F32)
        mu = ctx.sb("muE", [1, W], F32)
        rstd = ctx.sb("rstdE", [1, W], F32)
        tmp = ctx.sb("tmpE", [128, 512], F32)
        p1 = ctx.ps("p1E", [128, 512])
        p2 = ctx.ps("p2E", [128, 512])
        pss = ctx.ps("pssE", [1, 512])
        psq = ctx.ps("psqE", [1, 512])
        pst = ctx.ps("pstE", [128, 128], F32)
        blk = st.enter_context(nc.Block())

        ones_c = emit_ones(ctx, "onesc", [128, 1])
        ones_r = emit_ones(ctx, "onesr", [1, 128])
        ctx.dma("gpsimd", ao[:].rearrange("p (c t) -> p c t", c=NCH),
                ao_di.rearrange("(c p) t -> p c t", p=128), inc_sem="ldg")
        ctx.dma("gpsimd", ow[:].rearrange("p (c m) -> p c m", c=NCH),
                ow_di.rearrange("(c p) m -> p c m", p=128), inc_sem="ldg")
        ctx.dma("gpsimd", gw[:].rearrange("p (c m) -> p c m", c=NCH),
                gw_di.rearrange("(c p) m -> p c m", p=128), inc_sem="ldg")
        ctx.dma("sync", n2w[:], n2_di, inc_sem="ld")
        ctx.dma("sync", idf[:], idf_di, inc_sem="ld")

        ctx.wait("tensor", "ldg", 32)
        mm = MM(ctx, "mmE1", n_banks=2)
        mm.run(NCH, NCH, NT, 512,
               lambda kc, mc: ow[:, kc * D + mc * 128: kc * D + (mc + 1) * 128],
               lambda kc, nt: ao[:, kc * W + nt * 512: kc * W + nt * 512 + 512],
               lambda mc, nt: t1[:, mc * W + nt * 512: mc * W + nt * 512 + 512],
               func=AF.Copy)
        for c in range(NCH):
            par = c % 2
            if c >= 2:
                ctx.wait("sync", "t1dn", c - 1)
            ctx.dma("sync", x2c[par][:],
                    x2_di.rearrange("(c p) t -> p c t", p=128)[:, c, :], inc_sem="ldx2")
            ctx.wait("vector", "ldx2", (c + 1) * 16)
            ctx.wait("vector", "mmE1_ev", (c + 1) * NT)
            csl = slice(c * W, (c + 1) * W)
            ctx.step("vector", lambda E, csl=csl, par=par:
                     E.tensor_tensor(out=t1[:, csl], in0=t1[:, csl], in1=x2c[par][:],
                                     op=ALU.add),
                     inc_sem="t1dn")
        ctx.wait("sync", "t1dn", NCH)
        ctx.dma("sync", t1_do.rearrange("(c p) t -> p c t", p=128),
                t1[:].rearrange("p (c t) -> p c t", c=NCH), inc_sem="stt1")

        for e in ("tensor", "scalar", "vector"):
            ctx.wait(e, "t1dn", NCH)
            ctx.wait(e, "ld", 32)
            ctx.wait(e, "ones_ms")
        emit_ln_stats(ctx, "lnE1", t1, NCH, W, mu, rstd, ones_c, eps=1e-6,
                      rms_only=True, ps_pair=(pss, psq))
        for e in ("tensor", "vector"):
            ctx.wait(e, "lnE1_fin")
        emit_normalize(ctx, "nrmE1", t1, x2m, NCH, W, mu, rstd, ones_r,
                       gain_col=lambda c: n2w[:, c:c + 1], rms_only=True,
                       ps_pair=(p1, p2))

        ctx.step("gpsimd", lambda E: E.memset(ftr[:], 1.0), inc_sem="fcp")
        ctx.wait("sync", "fcp")
        ctx.dma("sync", fb_do, ftr[:], inc_sem="stf")
        ctx.wait("sync", "stf", 16)
        ctx.wait("sync", "stt1", 16)
        ctx.wait("sync", "nrmE1_done")
        ctx.dma("sync", x2m_do.rearrange("(c p) t -> p c t", p=128),
                x2m[:].rearrange("p (c t) -> p c t", c=NCH), inc_sem="stx")
        ctx.wait("sync", "stx", 16)
        ctx.finalize(blk)
    return nc



def build_launch_E2(W=2048, NGOV=None):
    nc = bass.Bass("TRN2", target_bir_lowering=False)
    x2m_di = nc.dram_tensor("x2mi", [D, W], BF16, kind="ExternalInput").ap()
    up_di = nc.dram_tensor("upT", [D, ED], F32, kind="ExternalInput").ap()
    dn_di = nc.dram_tensor("downT", [ED, D], F32, kind="ExternalInput").ap()
    f_di = nc.dram_tensor("factori", [1, W], F32, kind="ExternalInput").ap()
    moe_do = nc.dram_tensor("moe", [D, W], BF16, kind="ExternalOutput").ap()

    NCH = D // 128
    NT = W // 512
    NG = NGOV or 8
    with ExitStack() as st:
        ctx = Ctx(nc, st)
        x2m = ctx.sb("x2mE2", [128, NCH * W], BF16)
        acc = ctx.sb("accE2", [128, NCH * W], F32)
        usil = ctx.sb("usilE2", [128, 4 * W], BF16)
        upg = [ctx.sb(f"upgE2{i}", [128, NCH * 512], BF16) for i in range(2)]
        dng = [ctx.sb(f"dngE2{i}", [128, 4 * D], BF16) for i in range(2)]
        ftb = ctx.sb("ftbE2", [128, W], BF16)
        frow = ctx.sb("frowE2", [1, W], F32)
        tmp = ctx.sb("tmpE2", [128, 512], F32)
        pfb = ctx.ps("pfbE2", [128, 512])
        blk = st.enter_context(nc.Block())

        ones_r = emit_ones(ctx, "onesr", [1, 128])
        ctx.dma("gpsimd", x2m[:].rearrange("p (c t) -> p c t", c=NCH),
                x2m_di.rearrange("(c p) t -> p c t", p=128), inc_sem="ldg")
        ctx.dma("sync", frow[:], f_di, inc_sem="ld")
        ctx.wait("tensor", "ld", 16)
        ctx.wait("tensor", "ones_ms")
        for t in range(NT):
            if t > 0:
                ctx.wait("tensor", "fb_ev", t)
            ctx.step("tensor", lambda E, t=t:
                     E.matmul(pfb[:, 0:512], ones_r[:], frow[0:1, t * 512:(t + 1) * 512],
                              start=True, stop=True), inc_sem="fb_pe")
            ctx.wait("scalar", "fb_pe")
            ctx.step("scalar", lambda E, t=t:
                     E.activation(out=ftb[:, t * 512:(t + 1) * 512], in_=pfb[:, 0:512],
                                  func=AF.Copy), inc_sem="fb_ev")
        for c in range(NCH):
            ctx.step("vector", lambda E, c=c: E.memset(acc[:, c * W:(c + 1) * W], 0.0),
                     inc_sem="accz")

        mmu = MM(ctx, "mmU", n_banks=2)
        mmd = MM(ctx, "mmDn", n_banks=2)
        ctx.wait("tensor", "ldg", 16)
        for g in range(NG):
            par = g % 2
            if g >= 2:
                ctx.wait("gpsimd", "mmU_pe", ctx.cnt("mmU_pe"))
                ctx.wait("gpsimd", "mmDn_pe", ctx.cnt("mmDn_pe"))
            ctx.dma("gpsimd", upg[par][:].rearrange("p (c m) -> p c m", c=NCH),
                    up_di.rearrange("(c p) m -> p c m", p=128)[:, :, g * 512:(g + 1) * 512],
                    inc_sem="ldu")
            ctx.dma("gpsimd", dng[par][:].rearrange("p (c m) -> p c m", c=4),
                    dn_di.rearrange("(c p) m -> p c m", p=128)[:, g * 4:(g + 1) * 4, :],
                    inc_sem="ldud")
            ctx.wait("tensor", "ldu", (g + 1) * 16)
            ctx.wait("tensor", "uscl", ctx.cnt("uscl"))
            ctx.wait("scalar", "uscl", ctx.cnt("uscl"))
            mmu.run(NCH, 4, NT, 512,
                    lambda kc, mc, par=par: upg[par][:, kc * 512 + mc * 128: kc * 512 + (mc + 1) * 128],
                    lambda kc, nt: x2m2[:, kc * W + nt * 512: kc * W + nt * 512 + 512],
                    lambda mc, nt: usil[:, mc * W + nt * 512: mc * W + nt * 512 + 512],
                    func=AF.Silu)
            ctx.wait("vector", "mmU_ev", ctx.cnt("mmU_ev"))
            ctx.wait("vector", "fb_ev", NT)
            for mc in range(4):
                ctx.step("vector", lambda E, mc=mc:
                         E.tensor_tensor(out=usil[:, mc * W:(mc + 1) * W],
                                         in0=usil[:, mc * W:(mc + 1) * W],
                                         in1=ftb[:], op=ALU.mult),
                         inc_sem="uscl")
            ctx.wait("tensor", "uscl", ctx.cnt("uscl"))
            ctx.wait("tensor", "ldud", (g + 1) * 16)
            ctx.wait("vector", "accz", NCH)
            def dn_out(mc, nt, g=g):
                return ("acc", mc, nt)
            for mc in range(NCH):
                for nt in range(NT):
                    b = mmd.rr % 2
                    mmd.rr += 1
                    bank = mmd.banks[b]
                    if mmd.uses[b] > 0:
                        ctx.wait("tensor", "mmDn_ev", mmd.uses[b])
                    for kc in range(4):
                        ctx.step("tensor",
                                 lambda E, bank=bank, kc=kc, mc=mc, nt=nt, par=par,
                                        s=(kc == 0), p=(kc == 3):
                                 E.matmul(bank[:, 0:512],
                                          dng[par][:, kc * D + mc * 128: kc * D + (mc + 1) * 128],
                                          usil[:, kc * W + nt * 512: kc * W + nt * 512 + 512],
                                          start=s, stop=p),
                                 inc_sem="mmDn_pe" if kc == 3 else None)
                    ctx.wait("vector", "mmDn_pe", ctx.cnt("mmDn_pe"))
                    ctx.step("vector",
                             lambda E, bank=bank, mc=mc, nt=nt:
                             E.tensor_tensor(out=acc[:, mc * W + nt * 512: mc * W + nt * 512 + 512],
                                             in0=acc[:, mc * W + nt * 512: mc * W + nt * 512 + 512],
                                             in1=bank[:, 0:512], op=ALU.add),
                             inc_sem="mmDn_ev")
                    mmd.uses[b] = ctx.cnt("mmDn_ev")

        ctx.wait("gpsimd", "mmDn_ev", ctx.cnt("mmDn_ev"))
        ctx.dma("gpsimd", moe_do.rearrange("(c p) t -> p c t", p=128),
                acc[:].rearrange("p (c t) -> p c t", c=NCH), inc_sem="stv")
        ctx.wait("gpsimd", "stv", 16)
        ctx.finalize(blk)
    return nc



def build_launch_E3(W=2048):
    nc = bass.Bass("TRN2", target_bir_lowering=False)
    t1_di = nc.dram_tensor("t1i", [D, W], F32, kind="ExternalInput").ap()
    moe_di = nc.dram_tensor("moei", [D, W], BF16, kind="ExternalInput").ap()
    r1_di = nc.dram_tensor("r1i", [D, W], F32, kind="ExternalInput").ap()
    r2_do = nc.dram_tensor("r2", [D, W], F32, kind="ExternalOutput").ap()
    x3_do = nc.dram_tensor("x3", [D, W], BF16, kind="ExternalOutput").ap()

    NCH = D // 128
    NT = W // 512
    with ExitStack() as st:
        ctx = Ctx(nc, st)
        t1 = ctx.sb("t1E3", [128, NCH * W], F32)
        x3 = ctx.sb("x3E3", [128, NCH * W], BF16)
        mc2 = [ctx.sb(f"mcE3{i}", [128, W], F32) for i in range(2)]
        mu = ctx.sb("muE3", [1, W], F32)
        rstd = ctx.sb("rstdE3", [1, W], F32)
        tmp = ctx.sb("tmpE3", [128, 512], F32)
        p1 = ctx.ps("p1E3", [128, 512])
        p2 = ctx.ps("p2E3", [128, 512])
        pss = ctx.ps("pssE3", [1, 512])
        psq = ctx.ps("psqE3", [1, 512])
        blk = st.enter_context(nc.Block())

        ones_c = emit_ones(ctx, "onesc", [128, 1])
        ones_r = emit_ones(ctx, "onesr", [1, 128])
        ctx.dma("gpsimd", t1[:].rearrange("p (c t) -> p c t", c=NCH),
                t1_di.rearrange("(c p) t -> p c t", p=128), inc_sem="ldg")
        ctx.wait("vector", "ldg", 16)
        for c in range(NCH):
            par = c % 2
            if c >= 2:
                ctx.wait("gpsimd", "t2dn", c - 1)
            ctx.dma("gpsimd", mc2[par][:],
                    moe_di.rearrange("(c p) t -> p c t", p=128)[:, c, :], inc_sem="ldm")
            ctx.wait("vector", "ldm", (c + 1) * 16)
            ctx.step("vector", lambda E, c=c, par=par:
                     E.tensor_tensor(out=t1[:, c * W:(c + 1) * W],
                                     in0=t1[:, c * W:(c + 1) * W], in1=mc2[par][:],
                                     op=ALU.add),
                     inc_sem="t2dn")
        for e in ("tensor", "scalar", "vector"):
            ctx.wait(e, "t2dn", NCH)
            ctx.wait(e, "ones_ms")
        lnsc = (ctx.sb("lnscE3", [128, 1024], F32), ctx.sb("lnsrE3", [1, W], F32),
                ctx.sb("lnqrE3", [1, W], F32))
        emit_ln_stats(ctx, "lnE3a", t1, NCH, W, mu, rstd, ones_c, ps_pair=(pss, psq),
                      scratch=lnsc)
        for e in ("tensor", "vector"):
            ctx.wait(e, "lnE3a_fin")
        sc2 = mu
        def prep1(E):
            E.tensor_tensor(out=sc2[:], in0=mu[:], in1=rstd[:], op=ALU.mult)
            return E.tensor_scalar(out=sc2[:], in0=sc2[:], scalar1=-1.0, scalar2=None,
                                   op0=ALU.mult)
        ctx.step("vector", prep1, inc_sem="prep1")
        ctx.wait("tensor", "prep1")
        ctx.wait("sync", "t2dn", NCH)
        it = 0
        for c in range(NCH):
            par = c % 2
            if c >= 2:
                ctx.wait("sync", "p1dn", (c - 1) * NT)
            ctx.dma("sync", mc2[par][:],
                    r1_di.rearrange("(c p) t -> p c t", p=128)[:, c, :], inc_sem="ldr")
            for t in range(NT):
                sl = slice(t * 512, (t + 1) * 512)
                if it > 0:
                    ctx.wait("tensor", "p1dn", it)
                ctx.step("tensor", lambda E, sl=sl:
                         E.matmul(p1[:, 0:512], ones_r[:], rstd[0:1, sl], start=True, stop=True),
                         inc_sem="bc1")
                ctx.step("tensor", lambda E, sl=sl:
                         E.matmul(p2[:, 0:512], ones_r[:], sc2[0:1, sl], start=True, stop=True),
                         inc_sem="bc1")
                ctx.wait("vector", "bc1")
                ctx.wait("vector", "ldr", (c + 1) * 16)
                csl = slice(c * W + t * 512, c * W + t * 512 + 512)
                def f(E, csl=csl, sl=sl, par=par):
                    E.tensor_tensor(out=tmp[:], in0=t1[:, csl], in1=p1[:, 0:512], op=ALU.mult)
                    E.tensor_tensor(out=tmp[:], in0=tmp[:], in1=p2[:, 0:512], op=ALU.add)
                    return E.tensor_tensor(out=t1[:, csl], in0=tmp[:],
                                           in1=mc2[par][:, sl], op=ALU.add)
                ctx.step("vector", f, inc_sem="p1dn")
                it += 1
            ctx.wait("sync", "p1dn", (c + 1) * NT)
            ctx.dma("sync", r2_do.rearrange("(c p) t -> p c t", p=128)[:, c, :],
                    t1[:, c * W:(c + 1) * W], inc_sem="str2")
        ctx.wait("scalar", "p1dn", NCH * NT)
        ctx.wait("tensor", "p1dn", NCH * NT)
        emit_ln_stats(ctx, "lnE3b", t1, NCH, W, mu, rstd, ones_c, ps_pair=(pss, psq),
                      scratch=lnsc)
        for e in ("tensor", "vector"):
            ctx.wait(e, "lnE3b_fin")
        emit_normalize(ctx, "nrmE3", t1, x3, NCH, W, mu, rstd, ones_r,
                       ps_pair=(p1, p2))
        ctx.wait("sync", "nrmE3_done")
        ctx.dma("sync", x3_do.rearrange("(c p) t -> p c t", p=128),
                x3[:].rearrange("p (c t) -> p c t", c=NCH), inc_sem="stx3")
        ctx.wait("sync", "stx3", 16)
        ctx.wait("sync", "str2", NCH * 16)
        ctx.finalize(blk)
    return nc



def build_launch_E4(W=2048):
    nc = bass.Bass("TRN2", target_bir_lowering=False)
    x3_di = nc.dram_tensor("x3i", [D, W], BF16, kind="ExternalInput").ap()
    wi_di = nc.dram_tensor("w_inT", [D, 2 * D], F32, kind="ExternalInput").ap()
    bi_di = nc.dram_tensor("b_in", [1, 2 * D], F32, kind="ExternalInput").ap()
    ws_di = nc.dram_tensor("w_sT", [D, D], F32, kind="ExternalInput").ap()
    bs_di = nc.dram_tensor("b_s", [1, D], F32, kind="ExternalInput").ap()
    a_do = nc.dram_tensor("a1", [D, W], BF16, kind="ExternalOutput").ap()
    bb_do = nc.dram_tensor("bb", [D, W], BF16, kind="ExternalOutput").ap()

    NCH = D // 128
    NT = W // 512
    with ExitStack() as st:
        ctx = Ctx(nc, st)
        x3 = ctx.sb("x3E4", [128, NCH * W], BF16)
        a_sb = ctx.sb("aE4", [128, NCH * W], BF16)
        val = ctx.sb("valE4", [128, NCH * W], BF16)
        sv = ctx.sb("svE4", [128, NCH * W], BF16)
        wi = ctx.sb("wiE4", [128, NCH * 2 * D], BF16)
        ws = ctx.sb("wsE4", [128, NCH * D], BF16)
        bi = ctx.sb("biE4", [1, 2 * D], BF16)
        bs = ctx.sb("bsE4", [1, D], BF16)
        tmp = ctx.sb("tmpE4", [128, 512], F32)
        blk = st.enter_context(nc.Block())

        ones_rb = ctx.sb("ones_rbE4", [1, 512], BF16)
        ctx.step("gpsimd", lambda E: E.memset(ones_rb[:], 1.0), inc_sem="ones_ms")
        ctx.dma("gpsimd", x3[:].rearrange("p (c t) -> p c t", c=NCH),
                x3_di.rearrange("(c p) t -> p c t", p=128), inc_sem="ldg")
        ctx.dma("gpsimd", wi[:].rearrange("p (c m) -> p c m", c=NCH),
                wi_di.rearrange("(c p) m -> p c m", p=128), inc_sem="ldg")
        ctx.dma("gpsimd", ws[:].rearrange("p (c m) -> p c m", c=NCH),
                ws_di.rearrange("(c p) m -> p c m", p=128), inc_sem="ldg")
        ctx.dma("gpsimd", bi[:], bi_di, inc_sem="ldg")
        ctx.dma("gpsimd", bs[:], bs_di, inc_sem="ldg")
        ctx.wait("tensor", "ldg", 80)
        ctx.wait("tensor", "ones_ms")
        mm = MM(ctx, "mmE4", n_banks=2)
        mm.run(NCH, NCH, NT, 512,
               lambda kc, mc: wi[:, kc * 2 * D + mc * 128: kc * 2 * D + (mc + 1) * 128],
               lambda kc, nt: x3[:, kc * W + nt * 512: kc * W + nt * 512 + 512],
               lambda mc, nt: a_sb[:, mc * W + nt * 512: mc * W + nt * 512 + 512],
               func=AF.Sigmoid, scale=-1.0,
               bias_row=lambda mc: bi[:, mc * 128:(mc + 1) * 128],
               ones_row=ones_rb)
        mm.run(NCH, NCH, NT, 512,
               lambda kc, mc: wi[:, kc * 2 * D + D + mc * 128: kc * 2 * D + D + (mc + 1) * 128],
               lambda kc, nt: x3[:, kc * W + nt * 512: kc * W + nt * 512 + 512],
               lambda mc, nt: val[:, mc * W + nt * 512: mc * W + nt * 512 + 512],
               func=AF.Copy,
               bias_row=lambda mc: bi[:, D + mc * 128: D + (mc + 1) * 128],
               ones_row=ones_rb)
        mm.run(NCH, NCH, NT, 512,
               lambda kc, mc: ws[:, kc * D + mc * 128: kc * D + (mc + 1) * 128],
               lambda kc, nt: val[:, kc * W + nt * 512: kc * W + nt * 512 + 512],
               lambda mc, nt: sv[:, mc * W + nt * 512: mc * W + nt * 512 + 512],
               func=AF.Tanh,
               bias_row=lambda mc: bs[:, mc * 128:(mc + 1) * 128],
               ones_row=ones_rb)
        ctx.wait("vector", "mmE4_ev", ctx.cnt("mmE4_ev"))
        for c in range(NCH):
            csl = slice(c * W, (c + 1) * W)
            def f(E, csl=csl):
                E.tensor_scalar(out=tmp[:, 0:512], in0=tmp[:, 0:512], scalar1=0.0,
                                scalar2=None, op0=ALU.mult) if False else None
                E.tensor_scalar(out=sv[:, csl], in0=sv[:, csl], scalar1=1.0,
                                scalar2=None, op0=ALU.mult) if False else None
                E.tensor_tensor(out=val[:, csl], in0=a_sb[:, csl], in1=sv[:, csl],
                                op=ALU.mult)
                return E.tensor_tensor(out=val[:, csl], in0=sv[:, csl], in1=val[:, csl],
                                       op=ALU.subtract)
            ctx.step("vector", f, inc_sem="bbdn")
        ctx.wait("sync", "bbdn", NCH)
        ctx.dma("sync", a_do.rearrange("(c p) t -> p c t", p=128),
                a_sb[:].rearrange("p (c t) -> p c t", c=NCH), inc_sem="st")
        ctx.dma("sync", bb_do.rearrange("(c p) t -> p c t", p=128),
                val[:].rearrange("p (c t) -> p c t", c=NCH), inc_sem="st")
        ctx.wait("sync", "st", 32)
        ctx.finalize(blk)
    return nc



def build_launch_E(W=2048):
    nc = bass.Bass("TRN2", target_bir_lowering=False)
    ao_di = nc.dram_tensor("aoi", [D, W], BF16, kind="ExternalInput").ap()
    x2_di = nc.dram_tensor("x2i", [D, W], F32, kind="ExternalInput").ap()
    r1_di = nc.dram_tensor("r1i", [D, W], F32, kind="ExternalInput").ap()
    ow_di = nc.dram_tensor("o_wT", [D, D], F32, kind="ExternalInput").ap()
    n2_di = nc.dram_tensor("n2w", [128, 8], F32, kind="ExternalInput").ap()
    up_di = nc.dram_tensor("upT", [D, ED], F32, kind="ExternalInput").ap()
    dn_di = nc.dram_tensor("downT", [ED, D], F32, kind="ExternalInput").ap()
    wi_di = nc.dram_tensor("w_inT", [D, 2 * D], F32, kind="ExternalInput").ap()
    bi_di = nc.dram_tensor("b_in", [1, 2 * D], F32, kind="ExternalInput").ap()
    ws_di = nc.dram_tensor("w_sT", [D, D], F32, kind="ExternalInput").ap()
    bs_di = nc.dram_tensor("b_s", [1, D], F32, kind="ExternalInput").ap()
    r2_do = nc.dram_tensor("r2", [D, W], F32, kind="ExternalOutput").ap()
    x3_do = nc.dram_tensor("x3", [D, W], BF16, kind="ExternalOutput").ap()
    a_do = nc.dram_tensor("a1", [D, W], BF16, kind="ExternalOutput").ap()
    bb_do = nc.dram_tensor("bb", [D, W], BF16, kind="ExternalOutput").ap()
    t1_dd = nc.dram_tensor("t1park", [128, (D // 128) * W], F32, kind="Internal").ap()
    x2m_dd = nc.dram_tensor("x2mpark", [128, (D // 128) * W], BF16, kind="Internal").ap()
    x3_dd = nc.dram_tensor("x3park", [128, (D // 128) * W], BF16, kind="Internal").ap()

    NCH = D // 128
    NT = W // 512
    NG = 8
    with ExitStack() as st:
        ctx = Ctx(nc, st)
        mu = ctx.sb("muE", [1, W], F32)
        rstd = ctx.sb("rstdE", [1, W], F32)
        ones_c = emit_ones(ctx, "onesc", [128, 1])
        ones_r = emit_ones(ctx, "onesr", [1, 128])
        ones_rb = ctx.sb("ones_rbE", [1, 512], BF16)
        ctx.step("gpsimd", lambda E: E.memset(ones_rb[:], 1.0), inc_sem="ones_ms")
        p1 = ctx.ps("p1E", [128, 512])
        p2 = ctx.ps("p2E", [128, 512])
        pss = ctx.ps("pssE", [1, 512])
        psq = ctx.ps("psqE", [1, 512])
        blk = st.enter_context(nc.Block())

        s1 = ExitStack()
        ctx.st = s1
        ao = ctx.sb("aoE", [128, NCH * W], BF16)
        x2m = ao
        t1 = ctx.sb("t1E", [128, NCH * W], F32)
        ow = ctx.sb("owE", [128, NCH * D], BF16)
        n2w = ctx.sb("n2wE", [128, 8], F32)
        x2c = [ctx.sb(f"x2cE{i}", [128, W], F32) for i in range(2)]
        ctx.dma("gpsimd", ao[:].rearrange("p (c t) -> p c t", c=NCH),
                ao_di.rearrange("(c p) t -> p c t", p=128), inc_sem="ldg")
        ctx.dma("gpsimd", ow[:].rearrange("p (c m) -> p c m", c=NCH),
                ow_di.rearrange("(c p) m -> p c m", p=128), inc_sem="ldg")
        ctx.dma("sync", n2w[:], n2_di, inc_sem="ld")
        ctx.wait("tensor", "ldg", 32)
        mm1 = MM(ctx, "mmE1", n_banks=2)
        mm1.run(NCH, NCH, NT, 512,
                lambda kc, mc: ow[:, kc * D + mc * 128: kc * D + (mc + 1) * 128],
                lambda kc, nt: ao[:, kc * W + nt * 512: kc * W + nt * 512 + 512],
                lambda mc, nt: t1[:, mc * W + nt * 512: mc * W + nt * 512 + 512],
                func=AF.Copy)
        for c in range(NCH):
            par = c % 2
            if c >= 2:
                ctx.wait("sync", "t1dn", c - 1)
            ctx.dma("sync", x2c[par][:],
                    x2_di.rearrange("(c p) t -> p c t", p=128)[:, c, :], inc_sem="ldx2")
            ctx.wait("vector", "ldx2", (c + 1) * 16)
            ctx.wait("vector", "mmE1_ev", (c + 1) * NT)
            csl = slice(c * W, (c + 1) * W)
            ctx.step("vector", lambda E, csl=csl, par=par:
                     E.tensor_tensor(out=t1[:, csl], in0=t1[:, csl], in1=x2c[par][:],
                                     op=ALU.add),
                     inc_sem="t1dn")
            ctx.wait("sync", "t1dn", c + 1)
            ctx.dma("sync", t1_dd[:, csl], t1[:, csl], inc_sem="t1pk")
        for e in ("tensor", "scalar", "vector"):
            ctx.wait(e, "t1dn", NCH)
            ctx.wait(e, "ld", 16)
            ctx.wait(e, "ones_ms")
        emit_ln_stats(ctx, "lnE1", t1, NCH, W, mu, rstd, ones_c, eps=1e-6,
                      rms_only=True, ps_pair=(pss, psq))
        for e in ("tensor", "vector"):
            ctx.wait(e, "lnE1_fin")
        emit_normalize(ctx, "nrmE1", t1, x2m, NCH, W, mu, rstd, ones_r,
                       gain_col=lambda c: n2w[:, c:c + 1], rms_only=True,
                       ps_pair=(p1, p2))
        ctx.wait("sync", "nrmE1_done")
        ctx.dma("sync", x2m_dd, x2m[:], inc_sem="xmpk")
        for e in Ctx.ENGINES:
            ctx.wait(e, "t1pk", NCH * 16)
            ctx.wait(e, "xmpk", 16)
        s1.close()

        s23 = ExitStack()
        ctx.st = s23
        acc = ctx.sb("accE", [128, NCH * W], F32)
        s2 = ExitStack()
        ctx.st = s2
        x2m2 = ctx.sb("x2m2E", [128, NCH * W], BF16)
        ctx.dma("gpsimd", x2m2[:], x2m_dd, inc_sem="ldxm")
        ctx.wait("tensor", "ldxm", 16)
        usil = ctx.sb("usilE2", [128, 4 * W], BF16)
        upg = [ctx.sb(f"upgE2{i}", [128, NCH * 512], BF16) for i in range(2)]
        dng = [ctx.sb(f"dngE2{i}", [128, 4 * D], BF16) for i in range(2)]
        for c in range(NCH):
            ctx.step("vector", lambda E, c=c: E.memset(acc[:, c * W:(c + 1) * W], 0.0),
                     inc_sem="accz")
        mmu = MM(ctx, "mmU", n_banks=2)
        mmd = MM(ctx, "mmDn", n_banks=2)
        for g in range(NG):
            par = g % 2
            if g >= 2:
                ctx.wait("gpsimd", "mmU_pe", ctx.cnt("mmU_pe"))
                ctx.wait("gpsimd", "mmDn_pe", ctx.cnt("mmDn_pe"))
            ctx.dma("gpsimd", upg[par][:].rearrange("p (c m) -> p c m", c=NCH),
                    up_di.rearrange("(c p) m -> p c m", p=128)[:, :, g * 512:(g + 1) * 512],
                    inc_sem="ldu")
            ctx.dma("gpsimd", dng[par][:].rearrange("p (c m) -> p c m", c=4),
                    dn_di.rearrange("(c p) m -> p c m", p=128)[:, g * 4:(g + 1) * 4, :],
                    inc_sem="ldud")
            ctx.wait("tensor", "ldu", (g + 1) * 16)
            ctx.wait("scalar", "mmDn_pe", ctx.cnt("mmDn_pe"))
            mmu.run(NCH, 4, NT, 512,
                    lambda kc, mc, par=par: upg[par][:, kc * 512 + mc * 128: kc * 512 + (mc + 1) * 128],
                    lambda kc, nt: x2m2[:, kc * W + nt * 512: kc * W + nt * 512 + 512],
                    lambda mc, nt: usil[:, mc * W + nt * 512: mc * W + nt * 512 + 512],
                    func=AF.Silu)
            ctx.wait("tensor", "mmU_ev", ctx.cnt("mmU_ev"))
            ctx.wait("tensor", "ldud", (g + 1) * 16)
            ctx.wait("vector", "accz", NCH)
            for mc in range(NCH):
                for nt in range(NT):
                    b = mmd.rr % 2
                    mmd.rr += 1
                    bank = mmd.banks[b]
                    if mmd.uses[b] > 0:
                        ctx.wait("tensor", "mmDn_ev", mmd.uses[b])
                    for kc in range(4):
                        ctx.step("tensor",
                                 lambda E, bank=bank, kc=kc, mc=mc, nt=nt, par=par,
                                        s=(kc == 0), p=(kc == 3):
                                 E.matmul(bank[:, 0:512],
                                          dng[par][:, kc * D + mc * 128: kc * D + (mc + 1) * 128],
                                          usil[:, kc * W + nt * 512: kc * W + nt * 512 + 512],
                                          start=s, stop=p),
                                 inc_sem="mmDn_pe" if kc == 3 else None)
                    ctx.wait("vector", "mmDn_pe", ctx.cnt("mmDn_pe"))
                    ctx.step("vector",
                             lambda E, bank=bank, mc=mc, nt=nt:
                             E.tensor_tensor(out=acc[:, mc * W + nt * 512: mc * W + nt * 512 + 512],
                                             in0=acc[:, mc * W + nt * 512: mc * W + nt * 512 + 512],
                                             in1=bank[:, 0:512], op=ALU.add),
                             inc_sem="mmDn_ev")
                    mmd.uses[b] = ctx.cnt("mmDn_ev")
        for e in Ctx.ENGINES:
            ctx.wait(e, "mmDn_ev", ctx.cnt("mmDn_ev"))
        s2.close()

        s3 = ExitStack()
        ctx.st = s3
        x3 = ctx.sb("x3E", [128, NCH * W], BF16)
        t1c = [ctx.sb(f"t1cE3{i}", [128, W], F32) for i in range(2)]
        r1c = [ctx.sb(f"r1cE3{i}", [128, W], F32) for i in range(2)]
        tmp3 = ctx.sb("tmpE3", [128, 512], F32)
        for c in range(NCH):
            par = c % 2
            if c >= 2:
                ctx.wait("gpsimd", "t2dn", c - 1)
            ctx.dma("gpsimd", t1c[par][:], t1_dd[:, c * W:(c + 1) * W], inc_sem="ldm")
            ctx.wait("vector", "ldm", (c + 1) * 16)
            ctx.step("vector", lambda E, c=c, par=par:
                     E.tensor_tensor(out=acc[:, c * W:(c + 1) * W],
                                     in0=acc[:, c * W:(c + 1) * W], in1=t1c[par][:],
                                     op=ALU.add),
                     inc_sem="t2dn")
        for e in ("tensor", "scalar", "vector"):
            ctx.wait(e, "t2dn", NCH)
        emit_ln_stats(ctx, "lnE3a", acc, NCH, W, mu, rstd, ones_c,
                      ps_pair=(pss, psq))
        for e in ("tensor", "vector"):
            ctx.wait(e, "lnE3a_fin")
        sc2 = mu
        def prep1(E):
            E.tensor_tensor(out=sc2[:], in0=mu[:], in1=rstd[:], op=ALU.mult)
            return E.tensor_scalar(out=sc2[:], in0=sc2[:], scalar1=-1.0, scalar2=None,
                                   op0=ALU.mult)
        ctx.step("vector", prep1, inc_sem="prep1")
        ctx.wait("tensor", "prep1")
        ctx.wait("sync", "t2dn", NCH)
        it = 0
        for c in range(NCH):
            par = c % 2
            if c >= 2:
                ctx.wait("sync", "p1dn", (c - 1) * NT)
            ctx.dma("sync", r1c[par][:],
                    r1_di.rearrange("(c p) t -> p c t", p=128)[:, c, :], inc_sem="ldr")
            for t in range(NT):
                sl = slice(t * 512, (t + 1) * 512)
                if it > 0:
                    ctx.wait("tensor", "p1dn", it)
                ctx.step("tensor", lambda E, sl=sl:
                         E.matmul(p1[:, 0:512], ones_r[:], rstd[0:1, sl], start=True, stop=True),
                         inc_sem="bc1")
                ctx.step("tensor", lambda E, sl=sl:
                         E.matmul(p2[:, 0:512], ones_r[:], sc2[0:1, sl], start=True, stop=True),
                         inc_sem="bc1")
                ctx.wait("vector", "bc1")
                ctx.wait("vector", "ldr", (c + 1) * 16)
                csl = slice(c * W + t * 512, c * W + t * 512 + 512)
                def f(E, csl=csl, sl=sl, par=par):
                    E.tensor_tensor(out=tmp3[:], in0=acc[:, csl], in1=p1[:, 0:512], op=ALU.mult)
                    E.tensor_tensor(out=tmp3[:], in0=tmp3[:], in1=p2[:, 0:512], op=ALU.add)
                    return E.tensor_tensor(out=acc[:, csl], in0=tmp3[:],
                                           in1=r1c[par][:, sl], op=ALU.add)
                ctx.step("vector", f, inc_sem="p1dn")
                it += 1
            ctx.wait("sync", "p1dn", (c + 1) * NT)
            ctx.dma("sync", r2_do.rearrange("(c p) t -> p c t", p=128)[:, c, :],
                    acc[:, c * W:(c + 1) * W], inc_sem="str2")
        ctx.wait("scalar", "p1dn", NCH * NT)
        ctx.wait("tensor", "p1dn", NCH * NT)
        emit_ln_stats(ctx, "lnE3b", acc, NCH, W, mu, rstd, ones_c,
                      ps_pair=(pss, psq))
        for e in ("tensor", "vector"):
            ctx.wait(e, "lnE3b_fin")
        emit_normalize(ctx, "nrmE3", acc, x3, NCH, W, mu, rstd, ones_r,
                       ps_pair=(p1, p2))
        ctx.wait("sync", "nrmE3_done")
        ctx.dma("sync", x3_do.rearrange("(c p) t -> p c t", p=128),
                x3[:].rearrange("p (c t) -> p c t", c=NCH), inc_sem="stx3")
        ctx.dma("sync", x3_dd, x3[:], inc_sem="stx3")
        for e in Ctx.ENGINES:
            ctx.wait(e, "stx3", 32)
            ctx.wait(e, "str2", NCH * 16)
        s3.close()
        s23.close()

        s4 = ExitStack()
        ctx.st = s4
        x34 = ctx.sb("x34E", [128, NCH * W], BF16)
        ctx.dma("gpsimd", x34[:], x3_dd, inc_sem="ldx3")
        ctx.wait("tensor", "ldx3", 16)
        a_sb = ctx.sb("aE4", [128, NCH * W], BF16)
        val = ctx.sb("valE4", [128, NCH * W], BF16)
        wi = ctx.sb("wiE4", [128, NCH * 2 * D], BF16)
        ws = ctx.sb("wsE4", [128, NCH * D], BF16)
        bi = ctx.sb("biE4", [1, 2 * D], BF16)
        bs = ctx.sb("bsE4", [1, D], BF16)
        ppA = [ctx.sb(f"ppAE4{i}", [128, 512], BF16) for i in range(2)]
        ppB = [ctx.sb(f"ppBE4{i}", [128, 512], BF16) for i in range(2)]
        ldg4_0 = ctx.cnt("ldg")
        ctx.dma("gpsimd", wi[:].rearrange("p (c m) -> p c m", c=NCH),
                wi_di.rearrange("(c p) m -> p c m", p=128), inc_sem="ldg")
        ctx.dma("gpsimd", ws[:].rearrange("p (c m) -> p c m", c=NCH),
                ws_di.rearrange("(c p) m -> p c m", p=128), inc_sem="ldg")
        ctx.dma("gpsimd", bi[:], bi_di, inc_sem="ldg")
        ctx.dma("gpsimd", bs[:], bs_di, inc_sem="ldg")
        ctx.wait("tensor", "ldg", ldg4_0 + 64)
        mm4 = MM(ctx, "mmE4", n_banks=2)
        mm4.run(NCH, NCH, NT, 512,
                lambda kc, mc: wi[:, kc * 2 * D + mc * 128: kc * 2 * D + (mc + 1) * 128],
                lambda kc, nt: x34[:, kc * W + nt * 512: kc * W + nt * 512 + 512],
                lambda mc, nt: a_sb[:, mc * W + nt * 512: mc * W + nt * 512 + 512],
                func=AF.Sigmoid, scale=-1.0,
                bias_row=lambda mc: bi[:, mc * 128:(mc + 1) * 128],
                ones_row=ones_rb)
        mm4.run(NCH, NCH, NT, 512,
                lambda kc, mc: wi[:, kc * 2 * D + D + mc * 128: kc * 2 * D + D + (mc + 1) * 128],
                lambda kc, nt: x34[:, kc * W + nt * 512: kc * W + nt * 512 + 512],
                lambda mc, nt: val[:, mc * W + nt * 512: mc * W + nt * 512 + 512],
                func=AF.Copy,
                bias_row=lambda mc: bi[:, D + mc * 128: D + (mc + 1) * 128],
                ones_row=ones_rb)
        ctx.wait("sync", "mmE4_ev", ctx.cnt("mmE4_ev"))
        ctx.dma("sync", a_do.rearrange("(c p) t -> p c t", p=128),
                a_sb[:].rearrange("p (c t) -> p c t", c=NCH), inc_sem="sta")
        ctx.wait("tensor", "mmE4_ev", ctx.cnt("mmE4_ev"))
        ctx.wait("vector", "mmE4_ev", ctx.cnt("mmE4_ev"))
        i = 0
        for mc in range(NCH):
            for nt in range(NT):
                pk = [p1, p2][i % 2]
                pp = ppA[i % 2]
                ppo = ppB[i % 2]
                if i >= 2:
                    ctx.wait("tensor", "svev", (i - 1) * 2)
                for kc in range(NCH):
                    ctx.step("tensor",
                             lambda E, pk=pk, kc=kc, mc=mc, nt=nt, s=(kc == 0):
                             E.matmul(pk[:, 0:512],
                                      ws[:, kc * D + mc * 128: kc * D + (mc + 1) * 128],
                                      val[:, kc * W + nt * 512: kc * W + nt * 512 + 512],
                                      start=s, stop=False),
                             inc_sem=None)
                ctx.step("tensor",
                         lambda E, pk=pk, mc=mc:
                         E.matmul(pk[0:128, 0:512], bs[:, mc * 128:(mc + 1) * 128],
                                  ones_rb[:, 0:512], start=False, stop=True),
                         inc_sem="svpe")
                ctx.wait("scalar", "svpe", ctx.cnt("svpe"))
                if i >= 2:
                    ctx.wait("scalar", "svev", (i - 1) * 2)
                    ctx.wait("scalar", "stbb", (i - 1) * 16)
                ctx.step("scalar", lambda E, pk=pk, pp=pp:
                         E.activation(out=pp[:], in_=pk[:, 0:512], func=AF.Tanh),
                         inc_sem="svsc")
                ctx.wait("vector", "svsc", ctx.cnt("svsc"))
                if i >= 2:
                    ctx.wait("vector", "stbb", (i - 1) * 16)
                asl = slice(mc * W + nt * 512, mc * W + nt * 512 + 512)
                ctx.step("vector", lambda E, pp=pp, ppo=ppo, asl=asl:
                         E.tensor_tensor(out=ppo[:], in0=a_sb[:, asl], in1=pp[:],
                                         op=ALU.mult), inc_sem="svev")
                ctx.wait("vector", "svev", ctx.cnt("svev"))
                ctx.step("vector", lambda E, pp=pp, ppo=ppo:
                         E.tensor_tensor(out=ppo[:], in0=pp[:], in1=ppo[:],
                                         op=ALU.subtract), inc_sem="svev")
                ctx.wait("sync", "svev", ctx.cnt("svev"))
                ctx.dma("sync", bb_do.rearrange("(c p) t -> p c t", p=128)
                        [:, mc, nt * 512:(nt + 1) * 512], ppo[:], inc_sem="stbb")
                i += 1
        ctx.wait("sync", "stbb", NCH * NT * 16)
        ctx.wait("sync", "sta", 16)
        s4.close()
        ctx.st = st
        ctx.finalize(blk)
    return nc



def build_launch_F(T=T_FULL, DH=512):
    nc = bass.Bass("TRN2", target_bir_lowering=False)
    a_di = nc.dram_tensor("a1i", [DH, T], BF16, kind="ExternalInput").ap()
    b_di = nc.dram_tensor("bbi", [DH, T], BF16, kind="ExternalInput").ap()
    h_do = nc.dram_tensor("h1", [DH, T], BF16, kind="ExternalOutput").ap()
    NCH = DH // 128
    with ExitStack() as st:
        ctx = Ctx(nc, st)
        a_sb = ctx.sb("aF", [128, NCH * T], BF16)
        b_sb = ctx.sb("bF", [128, NCH * T], BF16)
        h_sb = ctx.sb("hF", [128, NCH * T], BF16)
        blk = st.enter_context(nc.Block())
        ctx.dma("gpsimd", a_sb[:].rearrange("p (c t) -> p c t", c=NCH),
                a_di.rearrange("(c p) t -> p c t", p=128), inc_sem="ldg")
        ctx.dma("gpsimd", b_sb[:].rearrange("p (c t) -> p c t", c=NCH),
                b_di.rearrange("(c p) t -> p c t", p=128), inc_sem="ldg")
        ctx.wait("vector", "ldg", 32)
        for c in range(NCH):
            csl = slice(c * T, (c + 1) * T)
            ctx.step("vector", lambda E, csl=csl:
                     E.tensor_tensor_scan(out=h_sb[:, csl], data0=a_sb[:, csl],
                                          data1=b_sb[:, csl], initial=0.0,
                                          op0=ALU.mult, op1=ALU.add),
                     inc_sem="scdn")
        ctx.wait("sync", "scdn", NCH)
        ctx.dma("sync", h_do.rearrange("(c p) t -> p c t", p=128),
                h_sb[:].rearrange("p (c t) -> p c t", c=NCH), inc_sem="st")
        ctx.wait("sync", "st", 16)
        ctx.finalize(blk)
    return nc



def build_launch_G(W=2048):
    nc = bass.Bass("TRN2", target_bir_lowering=False)
    h_di = nc.dram_tensor("h1i", [D, W], BF16, kind="ExternalInput").ap()
    x3_di = nc.dram_tensor("x3i", [D, W], BF16, kind="ExternalInput").ap()
    r2_di = nc.dram_tensor("r2i", [D, W], F32, kind="ExternalInput").ap()
    wo_di = nc.dram_tensor("w_oT", [D, D], F32, kind="ExternalInput").ap()
    bo_di = nc.dram_tensor("b_o", [1, D], F32, kind="ExternalInput").ap()
    gw_di = nc.dram_tensor("gate1", [D, 8], F32, kind="ExternalInput").ap()
    r3_do = nc.dram_tensor("r3", [D, W], F32, kind="ExternalOutput").ap()
    x4_do = nc.dram_tensor("x4", [D, W], BF16, kind="ExternalOutput").ap()
    gl_do = nc.dram_tensor("glog", [8, W], F32, kind="ExternalOutput").ap()

    NCH = D // 128
    NT = W // 512
    with ExitStack() as st:
        ctx = Ctx(nc, st)
        h_sb = ctx.sb("hG", [128, NCH * W], BF16)
        slot = ctx.sb("slotG", [128, NCH * W], F32)
        wo = ctx.sb("woG", [128, NCH * D], BF16)
        gw = ctx.sb("gwG", [128, NCH * 8], BF16)
        bo = ctx.sb("boG", [1, D], BF16)
        xc = [ctx.sb(f"xcG{i}", [128, W], F32) for i in range(2)]
        mu = ctx.sb("muG", [1, W], F32)
        rstd = ctx.sb("rstdG", [1, W], F32)
        gl8 = ctx.sb("gl8G", [8, W], F32)
        tmp = ctx.sb("tmpG", [128, 512], F32)
        p1 = ctx.ps("p1G", [128, 512])
        p2 = ctx.ps("p2G", [128, 512])
        pss = ctx.ps("pssG", [1, 512])
        psq = ctx.ps("psqG", [1, 512])
        blk = st.enter_context(nc.Block())

        ones_c = emit_ones(ctx, "onesc", [128, 1])
        ones_r = emit_ones(ctx, "onesr", [1, 128])
        ones_rb = ctx.sb("ones_rbG", [1, 512], BF16)
        ctx.step("gpsimd", lambda E: E.memset(ones_rb[:], 1.0), inc_sem="ones_ms")
        ctx.dma("gpsimd", h_sb[:].rearrange("p (c t) -> p c t", c=NCH),
                h_di.rearrange("(c p) t -> p c t", p=128), inc_sem="ldg")
        ctx.dma("gpsimd", wo[:].rearrange("p (c m) -> p c m", c=NCH),
                wo_di.rearrange("(c p) m -> p c m", p=128), inc_sem="ldg")
        ctx.dma("gpsimd", bo[:], bo_di, inc_sem="ldg")
        ctx.dma("gpsimd", gw[:].rearrange("p (c m) -> p c m", c=NCH),
                gw_di.rearrange("(c p) m -> p c m", p=128), inc_sem="ldg")
        ctx.wait("tensor", "ldg", 64)
        ctx.wait("tensor", "ones_ms")
        mm = MM(ctx, "mmG", n_banks=2)
        mm.run(NCH, NCH, NT, 512,
               lambda kc, mc: wo[:, kc * D + mc * 128: kc * D + (mc + 1) * 128],
               lambda kc, nt: h_sb[:, kc * W + nt * 512: kc * W + nt * 512 + 512],
               lambda mc, nt: slot[:, mc * W + nt * 512: mc * W + nt * 512 + 512],
               func=AF.Copy,
               bias_row=lambda mc: bo[:, mc * 128:(mc + 1) * 128],
               ones_row=ones_rb)
        for c in range(NCH):
            par = c % 2
            if c >= 2:
                ctx.wait("gpsimd", "sldn", c - 1)
            ctx.dma("gpsimd", xc[par][:],
                    x3_di.rearrange("(c p) t -> p c t", p=128)[:, c, :], inc_sem="ldx3")
            ctx.wait("vector", "ldx3", (c + 1) * 16)
            ctx.wait("vector", "mmG_ev", (c + 1) * NT)
            ctx.step("vector", lambda E, c=c, par=par:
                     E.tensor_tensor(out=slot[:, c * W:(c + 1) * W],
                                     in0=slot[:, c * W:(c + 1) * W], in1=xc[par][:],
                                     op=ALU.add),
                     inc_sem="sldn")
        for e in ("tensor", "scalar", "vector"):
            ctx.wait(e, "sldn", NCH)
        lnsc = (ctx.sb("lnscG", [128, 1024], F32), ctx.sb("lnsrG", [1, W], F32),
                ctx.sb("lnqrG", [1, W], F32))
        emit_ln_stats(ctx, "lnGa", slot, NCH, W, mu, rstd, ones_c, ps_pair=(pss, psq),
                      scratch=lnsc)
        for e in ("tensor", "vector"):
            ctx.wait(e, "lnGa_fin")
        sc2 = mu
        def prep1(E):
            E.tensor_tensor(out=sc2[:], in0=mu[:], in1=rstd[:], op=ALU.mult)
            return E.tensor_scalar(out=sc2[:], in0=sc2[:], scalar1=-1.0, scalar2=None,
                                   op0=ALU.mult)
        ctx.step("vector", prep1, inc_sem="prep1")
        ctx.wait("tensor", "prep1")
        ctx.wait("sync", "sldn", NCH)
        it = 0
        for c in range(NCH):
            par = c % 2
            if c >= 2:
                ctx.wait("sync", "p1dn", (c - 1) * NT)
            ctx.dma("sync", xc[par][:],
                    r2_di.rearrange("(c p) t -> p c t", p=128)[:, c, :], inc_sem="ldr")
            for t in range(NT):
                sl = slice(t * 512, (t + 1) * 512)
                if it > 0:
                    ctx.wait("tensor", "p1dn", it)
                ctx.step("tensor", lambda E, sl=sl:
                         E.matmul(p1[:, 0:512], ones_r[:], rstd[0:1, sl], start=True, stop=True),
                         inc_sem="bc1")
                ctx.step("tensor", lambda E, sl=sl:
                         E.matmul(p2[:, 0:512], ones_r[:], sc2[0:1, sl], start=True, stop=True),
                         inc_sem="bc1")
                ctx.wait("vector", "bc1")
                ctx.wait("vector", "ldr", (c + 1) * 16)
                csl = slice(c * W + t * 512, c * W + t * 512 + 512)
                def f(E, csl=csl, sl=sl, par=par):
                    E.tensor_tensor(out=tmp[:], in0=slot[:, csl], in1=p1[:, 0:512], op=ALU.mult)
                    E.tensor_tensor(out=tmp[:], in0=tmp[:], in1=p2[:, 0:512], op=ALU.add)
                    return E.tensor_tensor(out=slot[:, csl], in0=tmp[:],
                                           in1=xc[par][:, sl], op=ALU.add)
                ctx.step("vector", f, inc_sem="p1dn")
                it += 1
            ctx.wait("sync", "p1dn", (c + 1) * NT)
            ctx.dma("sync", r3_do.rearrange("(c p) t -> p c t", p=128)[:, c, :],
                    slot[:, c * W:(c + 1) * W], inc_sem="str3")
        ctx.wait("scalar", "p1dn", NCH * NT)
        ctx.wait("tensor", "p1dn", NCH * NT)
        emit_ln_stats(ctx, "lnGb", slot, NCH, W, mu, rstd, ones_c, ps_pair=(pss, psq),
                      scratch=lnsc)
        for e in ("tensor", "vector"):
            ctx.wait(e, "lnGb_fin")
        x4 = h_sb
        emit_normalize(ctx, "nrmG", slot, x4, NCH, W, mu, rstd, ones_r,
                       ps_pair=(p1, p2))
        ctx.wait("tensor", "nrmG_done")
        mm2 = MM(ctx, "mmG2", n_banks=1, bank_shape=(8, 512))
        mm2.run(NCH, 1, NT, 512,
                lambda kc, mc: gw[:, kc * 8:(kc + 1) * 8],
                lambda kc, nt: x4[:, kc * W + nt * 512: kc * W + nt * 512 + 512],
                lambda mc, nt: gl8[0:8, nt * 512:(nt + 1) * 512],
                func=AF.Copy)
        ctx.wait("sync", "nrmG_done")
        ctx.dma("sync", x4_do.rearrange("(c p) t -> p c t", p=128),
                x4[:].rearrange("p (c t) -> p c t", c=NCH), inc_sem="stx4")
        ctx.wait("sync", "mmG2_ev")
        ctx.dma("sync", gl_do, gl8[:], inc_sem="stx4")
        ctx.wait("sync", "stx4", 32)
        ctx.wait("sync", "str3", NCH * 16)
        ctx.finalize(blk)
    return nc



def build_launch_H(W=2048, T=T_FULL):
    nc = bass.Bass("TRN2", target_bir_lowering=False)
    gl_di = nc.dram_tensor("glogt", [128, 32 * 8], F32, kind="ExternalInput").ap()
    x4_di = nc.dram_tensor("x4h", [D, W], BF16, kind="ExternalInput").ap()
    r3_di = nc.dram_tensor("r3h", [D, W], F32, kind="ExternalInput").ap()
    ex_di = nc.dram_tensor("expertsT", [NE * D, D], F32, kind="ExternalInput").ap()
    id_di = nc.dram_tensor("idnH", [128, 128], F32, kind="ExternalInput").ap()
    out_do = nc.dram_tensor("outfm", [D, W], F32, kind="ExternalOutput").ap()

    NCH = D // 128
    NT = W // 512
    NC32 = 32
    with ExitStack() as st:
        ctx = Ctx(nc, st)
        x4 = ctx.sb("x4H", [128, NCH * W], BF16)
        xm = ctx.sb("xmH", [128, NCH * W], BF16)
        acc = ctx.sb("accH", [128, NCH * W], BF16)
        wex = ctx.sb("wexH", [128, NCH * D], BF16)
        G = ctx.sb("GH", [128, NC32 * 8], F32)
        se32 = ctx.sb("se32H", [128, NC32], F32)
        sf = ctx.sb("sfH", [8, W], BF16)
        sel8 = ctx.sb("sel8H", [8, 8 * 128], BF16)
        mem = ctx.sb("memH", [128, 16 * 8], F32)
        rnk = ctx.sb("rnkH", [128, 16 * 8], F32)
        hrd = ctx.sb("hrdH", [128, 16 * 8], F32)
        mx16 = ctx.sb("mxH", [128, 16], F32)
        erow = ctx.sb("erowH", [128, 8], F32)
        idn = ctx.sb("idnHs", [128, 128], F32)
        r3c = [ctx.sb("r3cH0", [128, W], F32)]
        mu = ctx.sb("muH", [1, W], F32)
        rstd = ctx.sb("rstdH", [1, W], F32)
        lnsc = (ctx.sb("lnscH", [128, 1024], F32), ctx.sb("lnsrH", [1, W], F32),
                ctx.sb("lnqrH", [1, W], F32))
        tmpb2 = [ctx.sb(f"tmpbH{i}", [128, 512], BF16) for i in range(2)]
        tmpf = ctx.sb("tmpfH", [128, 512], F32)
        tmpf2 = ctx.sb("tmpf2H", [128, 512], F32)
        ones_c = emit_ones(ctx, "onesc", [128, 1])
        ones_r = emit_ones(ctx, "onesr", [1, 128])
        onesb_c = ctx.sb("onesbH", [128, 1], BF16)
        psB = ctx.ps("psBH", [128, 512])
        psS = ctx.ps("psSH", [128, 512])
        p1 = ctx.ps("p1H", [128, 512])
        p2 = ctx.ps("p2H", [128, 512])
        blk = st.enter_context(nc.Block())

        ctx.step("gpsimd", lambda E: E.memset(onesb_c[:], 1.0), inc_sem="ms")
        ctx.step("gpsimd", lambda E: E.iota(erow[:], [[1, 8]], base=1,
                                            channel_multiplier=0,
                                            allow_small_or_imprecise_dtypes=True),
                 inc_sem="ms")
        ctx.step("gpsimd", lambda E: E.memset(sel8[:], 1.0), inc_sem="ms")
        sel8v = sel8[:].rearrange("p (n j) -> p n j", n=8)
        ctx.step("gpsimd", lambda E: E.affine_select(
            out=sel8v, in_=sel8v, pattern=[[1, 8], [0, 128]], base=0,
            channel_multiplier=-1, compare_op=ALU.is_equal, fill=0.0), inc_sem="ms")
        ctx.dma("sync", G[:], gl_di, inc_sem="ld")
        ctx.dma("sync", idn[:], id_di, inc_sem="ld")
        ctx.dma("gpsimd", x4[:].rearrange("p (c t) -> p c t", c=NCH),
                x4_di.rearrange("(c p) t -> p c t", p=128), inc_sem="ldg")

        Gv = G[:].rearrange("p (c e) -> p c e", e=8)
        sA = ctx.sb("sAH", [128, NC32 * 8], F32)
        onesq = ctx.sb("onesqH", [128, 128], F32)
        blo = ctx.sb("bloH", [128, 8], F32)
        bhi = ctx.sb("bhiH", [128, 8], F32)
        bmid = ctx.sb("bmidH", [128, 8], F32)
        bcnt = ctx.sb("bcntH", [128, 8], F32)

        def vop(fn):
            ctx.step("vector", fn, inc_sem="vf")
            ctx.wait("vector", "vf", ctx.cnt("vf"))

        ctx.step("gpsimd", lambda E: E.memset(onesq[:], 1.0), inc_sem="ms")
        ctx.wait("vector", "ld", 32)
        ctx.wait("vector", "ms", ctx.cnt("ms"))
        vop(lambda E: E.tensor_scalar_max(G[:], G[:], 1e-6))
        ctx.wait("tensor", "ms", ctx.cnt("ms"))

        def treesum(out):
            vop(lambda E: E.tensor_tensor(out=sA[:, 0:128], in0=sA[:, 0:128],
                                          in1=sA[:, 128:256], op=ALU.add))
            vop(lambda E: E.tensor_tensor(out=sA[:, 0:64], in0=sA[:, 0:64],
                                          in1=sA[:, 64:128], op=ALU.add))
            vop(lambda E: E.tensor_tensor(out=sA[:, 0:32], in0=sA[:, 0:32],
                                          in1=sA[:, 32:64], op=ALU.add))
            vop(lambda E: E.tensor_tensor(out=sA[:, 0:16], in0=sA[:, 0:16],
                                          in1=sA[:, 16:32], op=ALU.add))
            vop(lambda E, out=out: E.tensor_tensor(out=out, in0=sA[:, 0:8],
                                                   in1=sA[:, 8:16], op=ALU.add))

        for it in range(8):
            ctx.wait("tensor", "vf", ctx.cnt("vf"))
            ctx.step("tensor", lambda E:
                     E.matmul(psB[:, 0:256], onesq[:], G[:], start=True, stop=True),
                     inc_sem="skp")
            ctx.wait("vector", "skp", ctx.cnt("skp"))
            vop(lambda E: E.tensor_copy(out=sA[:], in_=psB[:, 0:256]))
            treesum(bcnt[:])
            vop(lambda E: E.reciprocal(out=bcnt[:], in_=bcnt[:]))
            vop(lambda E: E.tensor_tensor(
                out=Gv, in0=Gv,
                in1=bcnt[:].unsqueeze(1).broadcast_to([128, NC32, 8]),
                op=ALU.mult))
            vop(lambda E: E.tensor_reduce(out=se32[:], in_=Gv, axis=AX.X,
                                          op=ALU.add))
            vop(lambda E: E.reciprocal(out=se32[:], in_=se32[:]))
            vop(lambda E: E.tensor_tensor(
                out=Gv, in0=Gv,
                in1=se32[:].unsqueeze(2).broadcast_to([128, NC32, 8]),
                op=ALU.mult))

        def binit(E):
            E.memset(blo[:], 0.0)
            return E.memset(bhi[:], 1.0)
        vop(binit)
        for it in range(40):
            vop(lambda E: E.tensor_tensor(out=bmid[:], in0=blo[:], in1=bhi[:],
                                          op=ALU.add))
            vop(lambda E: E.tensor_scalar(out=bmid[:], in0=bmid[:], scalar1=0.5,
                                          scalar2=None, op0=ALU.mult))
            vop(lambda E: E.tensor_tensor(
                out=sA[:].rearrange("p (c e) -> p c e", e=8), in0=Gv,
                in1=bmid[:].unsqueeze(1).broadcast_to([128, NC32, 8]),
                op=ALU.is_ge))
            ctx.wait("tensor", "vf", ctx.cnt("vf"))
            ctx.step("tensor", lambda E:
                     E.matmul(psB[:, 0:256], onesq[:], sA[:], start=True, stop=True),
                     inc_sem="skp")
            ctx.wait("vector", "skp", ctx.cnt("skp"))
            vop(lambda E: E.tensor_copy(out=sA[:], in_=psB[:, 0:256]))
            treesum(bcnt[:])
            vop(lambda E: E.tensor_scalar(out=bcnt[:], in0=bcnt[:], scalar1=512.0,
                                          scalar2=None, op0=ALU.is_ge))
            vop(lambda E: E.tensor_tensor(out=sA[:, 0:8], in0=bmid[:], in1=blo[:],
                                          op=ALU.subtract))
            vop(lambda E: E.tensor_tensor(out=sA[:, 0:8], in0=sA[:, 0:8],
                                          in1=bcnt[:], op=ALU.mult))
            vop(lambda E: E.tensor_tensor(out=blo[:], in0=blo[:], in1=sA[:, 0:8],
                                          op=ALU.add))
            vop(lambda E: E.tensor_tensor(out=sA[:, 0:8], in0=bhi[:], in1=bmid[:],
                                          op=ALU.subtract))
            vop(lambda E: E.tensor_tensor(out=sA[:, 0:8], in0=sA[:, 0:8],
                                          in1=bcnt[:], op=ALU.mult))
            vop(lambda E: E.tensor_tensor(out=bhi[:], in0=bmid[:], in1=sA[:, 0:8],
                                          op=ALU.add))

        gv3 = Gv[:, 0:16, :]
        mv3 = mem[:].rearrange("p (c e) -> p c e", e=8)
        rv3 = rnk[:].rearrange("p (c e) -> p c e", e=8)
        vop(lambda E: E.tensor_tensor(
            out=mv3, in0=gv3,
            in1=blo[:].unsqueeze(1).broadcast_to([128, 16, 8]), op=ALU.is_ge))
        vop(lambda E: E.tensor_scalar(out=hrd[:], in0=G[:, 0:128], scalar1=0.5,
                                      scalar2=None, op0=ALU.is_gt))
        vop(lambda E: E.tensor_tensor(
            out=rv3, in0=mv3,
            in1=erow[:].unsqueeze(1).broadcast_to([128, 16, 8]), op=ALU.mult))
        vop(lambda E: E.tensor_reduce(out=mx16[:], in_=rv3, axis=AX.X, op=ALU.max))
        vop(lambda E: E.tensor_tensor(
            out=rv3, in0=rv3,
            in1=mx16[:].unsqueeze(2).broadcast_to([128, 16, 8]), op=ALU.is_ge))
        vop(lambda E: E.tensor_tensor(out=rnk[:], in0=rnk[:], in1=mem[:],
                                      op=ALU.mult))
        vop(lambda E: E.tensor_tensor(out=rnk[:], in0=rnk[:], in1=hrd[:],
                                      op=ALU.mult))
        ctx.step("vector", lambda E: E.nop(), inc_sem="seldn")
        ctx.wait("tensor", "seldn", 1)
        ctx.wait("tensor", "ld", 32)
        for cch in range(16):
            if cch > 0:
                ctx.wait("tensor", "sfv", cch)
            ctx.step("tensor", lambda E, cch=cch:
                     E.transpose(psB[0:8, 0:128], rnk[:, cch * 8:(cch + 1) * 8],
                                 idn[:]), inc_sem="sfp")
            ctx.wait("vector", "sfp")
            ctx.step("vector", lambda E, cch=cch:
                     E.tensor_copy(out=sf[0:8, cch * 128:(cch + 1) * 128],
                                   in_=psB[0:8, 0:128]), inc_sem="sfv")

        for c in range(NCH):
            ctx.step("vector", lambda E, c=c: E.memset(acc[:, c * W:(c + 1) * W], 0.0),
                     inc_sem="accz")
        ctx.wait("tensor", "ldg", 16)
        ctx.wait("vector", "ldg", 16)
        for e in range(NE):
            ctx.wait("gpsimd", "expe", ctx.cnt("expe"))
            ctx.dma("gpsimd", wex[:].rearrange("p (c m) -> p c m", c=NCH),
                    ex_di.rearrange("(c p) m -> p c m", p=128)[:, e * NCH:(e + 1) * NCH, :],
                    inc_sem="ldw")
            ctx.wait("vector", "expe", ctx.cnt("expe"))
            for t in range(NT):
                if ctx.cnt("xms") > 0:
                    ctx.wait("tensor", "xms", ctx.cnt("xms"))
                ctx.wait("tensor", "sfv", 16)
                ctx.step("tensor", lambda E, e=e, t=t:
                         E.matmul(psS[:, 0:512], sel8[0:8, e * 128:(e + 1) * 128],
                                  sf[0:8, t * 512:(t + 1) * 512], start=True, stop=True),
                         inc_sem="xsp")
                ctx.wait("vector", "xsp", ctx.cnt("xsp"))
                for c in range(NCH):
                    ctx.step("vector", lambda E, c=c, t=t:
                             E.tensor_tensor(out=xm[:, c * W + t * 512: c * W + t * 512 + 512],
                                             in0=x4[:, c * W + t * 512: c * W + t * 512 + 512],
                                             in1=psS[:, 0:512], op=ALU.mult),
                             inc_sem="xms" if c == NCH - 1 else None)
            ctx.wait("tensor", "ldw", (e + 1) * 16)
            ctx.wait("tensor", "xms", ctx.cnt("xms"))
            ctx.wait("vector", "accz", NCH)
            for mc in range(NCH):
                for nt in range(NT):
                    par2 = (mc * NT + nt) % 2
                    bank = [p1, p2][par2]
                    if ctx.cnt("exev") >= 2:
                        ctx.wait("tensor", "exev", ctx.cnt("exev") - 1)
                    for kc in range(NCH):
                        ctx.step("tensor",
                                 lambda E, bank=bank, kc=kc, mc=mc, nt=nt,
                                        s=(kc == 0), p=(kc == NCH - 1):
                                 E.matmul(bank[:, 0:512],
                                          wex[:, kc * D + mc * 128: kc * D + (mc + 1) * 128],
                                          xm[:, kc * W + nt * 512: kc * W + nt * 512 + 512],
                                          start=s, stop=p),
                                 inc_sem="expe" if kc == NCH - 1 else None)
                    tb = tmpb2[(ctx.cnt("exsc")) % 2]
                    ctx.wait("scalar", "expe", ctx.cnt("expe"))
                    if ctx.cnt("exev") >= 1:
                        ctx.wait("scalar", "exev", ctx.cnt("exev") - 1)
                    ctx.step("scalar", lambda E, bank=bank, tb=tb:
                             E.activation(out=tb[:], in_=bank[:, 0:512], func=AF.Copy),
                             inc_sem="exsc")
                    ctx.wait("vector", "exsc", ctx.cnt("exsc"))
                    ctx.step("vector", lambda E, mc=mc, nt=nt, tb=tb:
                             E.tensor_tensor(out=acc[:, mc * W + nt * 512: mc * W + nt * 512 + 512],
                                             in0=acc[:, mc * W + nt * 512: mc * W + nt * 512 + 512],
                                             in1=tb[:], op=ALU.add),
                             inc_sem="exev")

        for e in ("tensor", "scalar", "vector"):
            ctx.wait(e, "exev", ctx.cnt("exev"))
            ctx.wait(e, "ms", ctx.cnt("ms"))
            ctx.wait(e, "ones_ms")
        emit_ln_stats(ctx, "lnH", acc, NCH, W, mu, rstd, onesb_c, scratch=lnsc,
                      ones_sq=ones_c)
        for e in ("tensor", "vector"):
            ctx.wait(e, "lnH_fin")
        sc2 = mu
        def prep1(E):
            E.tensor_tensor(out=sc2[:], in0=mu[:], in1=rstd[:], op=ALU.mult)
            return E.tensor_scalar(out=sc2[:], in0=sc2[:], scalar1=-1.0, scalar2=None,
                                   op0=ALU.mult)
        ctx.step("vector", prep1, inc_sem="prep1")
        ctx.wait("tensor", "prep1")
        it = 0
        for c in range(NCH):
            par = 0
            if c >= 1:
                ctx.wait("sync", "p1dn", c * NT)
            ctx.dma("sync", r3c[par][:],
                    r3_di.rearrange("(c p) t -> p c t", p=128)[:, c, :], inc_sem="ldr")
            for t in range(NT):
                sl = slice(t * 512, (t + 1) * 512)
                tb = [tmpf, tmpf2][it % 2]
                if it > 0:
                    ctx.wait("tensor", "p1dn", it)
                ctx.step("tensor", lambda E, sl=sl:
                         E.matmul(p1[:, 0:512], ones_r[:], rstd[0:1, sl], start=True, stop=True),
                         inc_sem="bc1")
                ctx.step("tensor", lambda E, sl=sl:
                         E.matmul(p2[:, 0:512], ones_r[:], sc2[0:1, sl], start=True, stop=True),
                         inc_sem="bc1")
                ctx.wait("vector", "bc1")
                ctx.wait("vector", "ldr", (c + 1) * 16)
                if it >= 2:
                    ctx.wait("vector", "str4", (it - 1) * 16)
                csl = slice(c * W + t * 512, c * W + t * 512 + 512)
                def f(E, csl=csl, sl=sl, par=par, tb=tb):
                    E.tensor_tensor(out=tb[:], in0=acc[:, csl], in1=p1[:, 0:512], op=ALU.mult)
                    E.tensor_tensor(out=tb[:], in0=tb[:], in1=p2[:, 0:512], op=ALU.add)
                    return E.tensor_tensor(out=tb[:], in0=tb[:],
                                           in1=r3c[par][:, sl], op=ALU.add)
                ctx.step("vector", f, inc_sem="p1dn")
                ctx.wait("sync", "p1dn", it + 1)
                ctx.dma("sync", out_do.rearrange("(c p) t -> p c t", p=128)[:, c, sl],
                        tb[:], inc_sem="str4")
                it += 1
        ctx.wait("sync", "str4", NCH * NT * 16)
        ctx.finalize(blk)
    return nc



_BUILT = {}


def _get(key, builder, *a):
    if key not in _BUILT:
        _BUILT[key] = builder(*a)
    return _BUILT[key]


def run_A(inputs):
    x = np.ascontiguousarray(inputs["L0_attn_x"])
    xt = np.ascontiguousarray(x.reshape(BATCH * T_FULL, D).T)
    W = (BATCH * T_FULL) // 8
    wx = np.ascontiguousarray(np.asarray(inputs["L0_attn_W_xproj"], np.float32))
    bx = np.zeros((1, 1152), np.float32)
    bx[0, :1056] = np.asarray(inputs["L0_attn_b_xproj"], np.float32)
    nc = _get(("A", W), build_launch_A, W)
    in_maps = [dict(x=np.ascontiguousarray(xt[:, i * W:(i + 1) * W]),
                    w_xproj=wx, b_xproj=bx) for i in range(8)]
    res = run_bass_kernel_spmd(nc, in_maps, core_ids=list(range(8))).results
    cat = lambda k: np.concatenate([res[i][k] for i in range(8)], axis=1)
    return dict(delta=cat("delta"), u=cat("u"), x1=cat("x1"), bc=cat("bc"))


def run_B(inputs, A_out):
    delta = A_out["delta"]; u = A_out["u"]; bc = A_out["bc"]
    al = np.asarray(inputs["L0_attn_A_log"], np.float32)
    nc = _get(("B",), build_launch_B)
    in_maps = []
    for i in range(8):
        b, dh = i // 2, i % 2
        rows = slice(dh * 512, (dh + 1) * 512)
        cols = slice(b * T_FULL, (b + 1) * T_FULL)
        alh = np.ascontiguousarray(
            al[rows].reshape(4, 128, NST).transpose(1, 0, 2).reshape(128, 4 * NST))
        bc64 = np.zeros((64, T_FULL), np.float32)
        bc64[0:16] = bc[0:16, cols]
        bc64[32:48] = bc[16:32, cols]
        in_maps.append(dict(
            delta=np.ascontiguousarray(delta[rows, cols]),
            u=np.ascontiguousarray(u[rows, cols]),
            bc=bc64,
            a_log=alh))
    res = run_bass_kernel_spmd(nc, in_maps, core_ids=list(range(8))).results
    y = np.zeros((D, BATCH * T_FULL), np.float32)
    for i in range(8):
        b, dh = i // 2, i % 2
        y[dh * 512:(dh + 1) * 512, b * T_FULL:(b + 1) * T_FULL] = \
            res[i]["y"].astype(np.float32)
    return dict(y=y)


def run_C(inputs, A_out, B_out):
    x = np.ascontiguousarray(inputs["L0_attn_x"])
    xt = np.ascontiguousarray(x.reshape(BATCH * T_FULL, D).T)
    W = (BATCH * T_FULL) // 8
    wo = np.ascontiguousarray(np.asarray(inputs["L0_attn_W_out"], np.float32))
    bo = np.ascontiguousarray(np.asarray(inputs["L0_attn_b_out"], np.float32).reshape(1, D))
    dp = np.ascontiguousarray(np.asarray(inputs["L0_attn_Dp"], np.float32).reshape(8, 128).T)
    n1 = np.ascontiguousarray(np.asarray(inputs["L0_ffn_n1_w"], np.float32).reshape(8, 128).T)
    y = B_out["y"]
    x1 = A_out["x1"]
    nc = _get(("C", W), build_launch_C, W)
    in_maps = []
    for i in range(8):
        cols = slice(i * W, (i + 1) * W)
        in_maps.append(dict(
            y=np.ascontiguousarray(y[:, cols].astype(np.float32)).astype(
                np.dtype("bfloat16") if False else y.dtype) if False else _bf(y[:, cols]),
            x1=np.ascontiguousarray(x1[:, cols]),
            x=np.ascontiguousarray(xt[:, cols]),
            dp=dp, n1w=n1, w_out=wo, b_out=bo))
    res = run_bass_kernel_spmd(nc, in_maps, core_ids=list(range(8))).results
    cat = lambda k: np.concatenate([res[i][k] for i in range(8)], axis=1)
    return dict(r1=cat("r1"), x2=cat("x2"), x2n=cat("x2n"))


def _bf(a):
    import ml_dtypes
    return np.ascontiguousarray(np.asarray(a).astype(ml_dtypes.bfloat16))


def run_D(inputs, C_out):
    x2n = C_out["x2n"]
    qw = np.ascontiguousarray(np.asarray(inputs["L0_ffn_qkv_w"], np.float32).T)
    nc = _get(("D",), build_launch_D)
    in_maps = []
    for i in range(8):
        b, hg = i // 2, i % 2
        cols = slice(b * T_FULL, (b + 1) * T_FULL)
        wslice = np.concatenate([qw[:, hg * 512:(hg + 1) * 512],
                                 qw[:, 1024 + hg * 512: 1024 + (hg + 1) * 512],
                                 qw[:, 2048 + hg * 512: 2048 + (hg + 1) * 512]], axis=1)
        in_maps.append(dict(x2nf=np.ascontiguousarray(x2n[:, cols]),
                            wq_slice=np.ascontiguousarray(wslice)))
    res = run_bass_kernel_spmd(nc, in_maps, core_ids=list(range(8))).results
    ao = np.zeros((D, BATCH * T_FULL), np.float32)
    for i in range(8):
        b, hg = i // 2, i % 2
        ao[hg * 512:(hg + 1) * 512, b * T_FULL:(b + 1) * T_FULL] = \
            res[i]["ao"].astype(np.float32)
    return dict(ao=ao)


def run_E1(inputs, C_out, D_out):
    W = (BATCH * T_FULL) // 8
    ow = np.ascontiguousarray(np.asarray(inputs["L0_ffn_o_w"], np.float32).T)
    n2 = np.ascontiguousarray(np.asarray(inputs["L0_ffn_n2_w"], np.float32).reshape(8, 128).T)
    gw = np.ascontiguousarray(np.asarray(inputs["L0_ffn_gate_w"], np.float32).T)
    idf = np.eye(128, dtype=np.float32)
    ao = D_out["ao"]
    x2 = C_out["x2"]
    nc = _get(("E1", W), build_launch_E1, W)
    in_maps = []
    for i in range(8):
        cols = slice(i * W, (i + 1) * W)
        in_maps.append(dict(aoi=_bf(ao[:, cols]), x2i=np.ascontiguousarray(x2[:, cols]),
                            o_wT=ow, n2w=n2, gate0T=gw, idf=idf))
    res = run_bass_kernel_spmd(nc, in_maps, core_ids=list(range(8))).results
    cat = lambda k: np.concatenate([res[i][k] for i in range(8)], axis=1)
    fac = np.concatenate([res[i]["factor"].reshape(-1) for i in range(8)])
    return dict(t1=cat("t1"), x2m=cat("x2m"), factor=fac)


def run_E2(inputs, E1_out):
    W = (BATCH * T_FULL) // 8
    up = np.ascontiguousarray(np.asarray(inputs["L0_ffn_up_w"], np.float32).T)
    dn = np.ascontiguousarray(np.asarray(inputs["L0_ffn_down_w"], np.float32).T)
    x2m = E1_out["x2m"]
    fac = E1_out["factor"].reshape(8, W)
    nc = _get(("E2", W), build_launch_E2, W)
    in_maps = []
    for i in range(8):
        cols = slice(i * W, (i + 1) * W)
        in_maps.append(dict(x2mi=_bf(x2m[:, cols]), upT=up, downT=dn,
                            factori=np.ascontiguousarray(fac[i].reshape(1, W))))
    res = run_bass_kernel_spmd(nc, in_maps, core_ids=list(range(8))).results
    return dict(moe=np.concatenate([res[i]["moe"] for i in range(8)], axis=1))


def run_E3(inputs, C_out, E1_out, E2_out):
    W = (BATCH * T_FULL) // 8
    nc = _get(("E3", W), build_launch_E3, W)
    in_maps = []
    for i in range(8):
        cols = slice(i * W, (i + 1) * W)
        in_maps.append(dict(t1i=np.ascontiguousarray(E1_out["t1"][:, cols]),
                            moei=_bf(E2_out["moe"][:, cols]),
                            r1i=np.ascontiguousarray(C_out["r1"][:, cols])))
    res = run_bass_kernel_spmd(nc, in_maps, core_ids=list(range(8))).results
    cat = lambda k: np.concatenate([res[i][k] for i in range(8)], axis=1)
    return dict(r2=cat("r2"), x3=cat("x3"))


def run_E4(inputs, E3_out):
    W = (BATCH * T_FULL) // 8
    wi = np.ascontiguousarray(np.asarray(inputs["L1_attn_W_in"], np.float32).T)
    bi = np.ascontiguousarray(np.asarray(inputs["L1_attn_b_in"], np.float32).reshape(1, 2 * D))
    ws = np.ascontiguousarray(np.asarray(inputs["L1_attn_W_s"], np.float32).T)
    bs = np.ascontiguousarray(np.asarray(inputs["L1_attn_b_s"], np.float32).reshape(1, D))
    nc = _get(("E4", W), build_launch_E4, W)
    in_maps = []
    for i in range(8):
        cols = slice(i * W, (i + 1) * W)
        in_maps.append(dict(x3i=_bf(E3_out["x3"][:, cols]), w_inT=wi, b_in=bi,
                            w_sT=ws, b_s=bs))
    res = run_bass_kernel_spmd(nc, in_maps, core_ids=list(range(8))).results
    cat = lambda k: np.concatenate([res[i][k] for i in range(8)], axis=1)
    return dict(a1=cat("a1"), bb=cat("bb"))


def run_E(inputs, C_out, D_out):
    W = (BATCH * T_FULL) // 8
    ow = np.ascontiguousarray(np.asarray(inputs["L0_ffn_o_w"], np.float32).T)
    n2 = np.ascontiguousarray(np.asarray(inputs["L0_ffn_n2_w"], np.float32).reshape(8, 128).T)
    up = np.ascontiguousarray(np.asarray(inputs["L0_ffn_up_w"], np.float32).T)
    dn = np.ascontiguousarray(np.asarray(inputs["L0_ffn_down_w"], np.float32).T)
    wi = np.ascontiguousarray(np.asarray(inputs["L1_attn_W_in"], np.float32).T)
    bi = np.ascontiguousarray(np.asarray(inputs["L1_attn_b_in"], np.float32).reshape(1, 2 * D))
    ws = np.ascontiguousarray(np.asarray(inputs["L1_attn_W_s"], np.float32).T)
    bs = np.ascontiguousarray(np.asarray(inputs["L1_attn_b_s"], np.float32).reshape(1, D))
    ao = D_out["ao"]
    nc = _get(("E", W), build_launch_E, W)
    in_maps = []
    for i in range(8):
        cols = slice(i * W, (i + 1) * W)
        in_maps.append(dict(aoi=_bf(ao[:, cols]),
                            x2i=np.ascontiguousarray(C_out["x2"][:, cols]),
                            r1i=np.ascontiguousarray(C_out["r1"][:, cols]),
                            o_wT=ow, n2w=n2, upT=up, downT=dn,
                            w_inT=wi, b_in=bi, w_sT=ws, b_s=bs))
    res = run_bass_kernel_spmd(nc, in_maps, core_ids=list(range(8))).results
    cat = lambda k: np.concatenate([res[i][k] for i in range(8)], axis=1)
    return dict(r2=cat("r2"), x3=cat("x3"), a1=cat("a1"), bb=cat("bb"))


def run_F(inputs, E4_out):
    nc = _get(("F",), build_launch_F)
    in_maps = []
    for i in range(8):
        b, dh = i // 2, i % 2
        rows = slice(dh * 512, (dh + 1) * 512)
        cols = slice(b * T_FULL, (b + 1) * T_FULL)
        in_maps.append(dict(a1i=_bf(E4_out["a1"][rows, cols]),
                            bbi=_bf(E4_out["bb"][rows, cols])))
    res = run_bass_kernel_spmd(nc, in_maps, core_ids=list(range(8))).results
    h = np.zeros((D, BATCH * T_FULL), np.float32)
    for i in range(8):
        b, dh = i // 2, i % 2
        h[dh * 512:(dh + 1) * 512, b * T_FULL:(b + 1) * T_FULL] = \
            res[i]["h1"].astype(np.float32)
    return dict(h=h)


def run_G(inputs, E3_out, F_out):
    W = (BATCH * T_FULL) // 8
    wo = np.ascontiguousarray(np.asarray(inputs["L1_attn_W_o"], np.float32).T)
    bo = np.ascontiguousarray(np.asarray(inputs["L1_attn_b_o"], np.float32).reshape(1, D))
    gw = np.ascontiguousarray(np.asarray(inputs["L1_ffn_gate_w"], np.float32)[0])
    nc = _get(("G", W), build_launch_G, W)
    in_maps = []
    for i in range(8):
        cols = slice(i * W, (i + 1) * W)
        in_maps.append(dict(h1i=_bf(F_out["h"][:, cols]), x3i=_bf(E3_out["x3"][:, cols]),
                            r2i=np.ascontiguousarray(E3_out["r2"][:, cols]),
                            w_oT=wo, b_o=bo, gate1=gw))
    res = run_bass_kernel_spmd(nc, in_maps, core_ids=list(range(8))).results
    cat = lambda k: np.concatenate([res[i][k] for i in range(8)], axis=1)
    return dict(r3=cat("r3"), x4=cat("x4"), glog=cat("glog"))


def run_H(inputs, C_out=None, G_out=None):
    W = (BATCH * T_FULL) // 8
    ex = np.ascontiguousarray(np.asarray(inputs["L1_ffn_experts"], np.float32)
                              .reshape(NE * D, D))
    idn = np.eye(128, dtype=np.float32)
    glog = G_out["glog"]
    x4 = G_out["x4"]
    r3 = G_out["r3"]
    nc = _get(("H", W), build_launch_H, W)
    in_maps = []
    for i in range(8):
        b, sh = i // 2, i % 2
        own = slice(b * T_FULL + sh * W, b * T_FULL + (sh + 1) * W)
        oth = slice(b * T_FULL + (1 - sh) * W, b * T_FULL + (2 - sh) * W)
        glo = np.concatenate([glog[:, own], glog[:, oth]], axis=1)
        glt = np.ascontiguousarray(glo.T.reshape(32, 128, 8).transpose(1, 0, 2)
                                   .reshape(128, 32 * 8))
        in_maps.append(dict(glogt=glt,
                            x4h=_bf(x4[:, own]),
                            r3h=np.ascontiguousarray(r3[:, own]),
                            expertsT=ex, idnH=idn))
    res = run_bass_kernel_spmd(nc, in_maps, core_ids=list(range(8))).results
    out = np.zeros((D, BATCH * T_FULL), np.float32)
    for i in range(8):
        b, sh = i // 2, i % 2
        out[:, b * T_FULL + sh * W: b * T_FULL + (sh + 1) * W] = res[i]["outfm"]
    return dict(r4=out)


def kernel(**inputs):
    inputs = {k: np.asarray(v) for k, v in inputs.items()}
    a = run_A(inputs)
    b = run_B(inputs, a)
    c = run_C(inputs, a, b)
    d = run_D(inputs, c)
    e = run_E(inputs, c, d)
    f = run_F(inputs, e)
    g = run_G(inputs, e, f)
    h = run_H(inputs, None, g)
    r4 = h["r4"]
    return np.ascontiguousarray(r4.T).reshape(BATCH, T_FULL, D).astype(np.float32)
